# version 12
# baseline (speedup 1.0000x reference)
import contextlib
import numpy as np
import concourse.bass as bass
import concourse.mybir as mybir
from concourse.bass_utils import run_bass_kernel_spmd
from concourse.alu_op_type import AluOpType as Op

F32 = mybir.dt.float32
BF16 = mybir.dt.bfloat16
AF = mybir.ActivationFunctionType
AX = mybir.AxisListType

D = 2048
T = 2048
NCH = 16
H = 16
HD = 128
FF = 5120
XSTR = 1026
PI = float(np.pi)
NS_COEFS = [(4.0848, -6.8946, 2.927), (3.9505, -6.3029, 2.6377),
            (3.7418, -5.5913, 2.3037), (2.8769, -3.1427, 1.2046),
            (2.8366, -3.0525, 1.2012)]
BASE_LR_INV = float(0.01 + np.log(-np.expm1(-0.01)))
CC_BYPASS = False

CONSTS = {
    "half_pi": PI / 2, "half": 0.5, "neg_half": -0.5,
    "inv2pi": 1.0 / (2 * PI), "neg2pi": -2 * PI, "magic": 12582912.0,
    "quarter": 0.25, "eps_ln": 1e-5, "eps_l2": 1e-5, "eps_ns": 1e-7,
    "base_lr": BASE_LR_INV, "p7": 0.7,
    "quarter_pi": PI / 4, "t_quarter_pi": T * PI / 4,
    "neg_3half_pi": -1.5 * PI, "pi": PI, "neg_pi": -PI, "neg_one": -1.0,
    "one": 1.0, "inv_hd": 1.0 / HD,
}
CIDX = {k: i for i, k in enumerate(CONSTS)}


class Counters:
    def __init__(self):
        self.v = {}

    def bump(self, sem, n):
        self.v[sem] = self.v.get(sem, 0) + n
        return self.v[sem]

    def val(self, sem):
        return self.v.get(sem, 0)


class KB:

    def __init__(self, nc, sems, t, a):
        self.nc = nc
        self.sems = sems
        self.t = t
        self.a = a

    def on(self, eng):
        return self.eng == eng

    def wait(self, eng, sem, val=None):
        if self.on(eng):
            self.E.wait_ge(self.sems[sem], self.c.val(sem) if val is None else val)

    def dma(self, queue, out, in_, sem):
        v = self.c.bump(sem, 16)
        if self.on(queue):
            self.E.dma_start(out=out, in_=in_).then_inc(self.sems[sem], 16)
        return v

    def op(self, eng, fn, sem=None, drain=False):
        if sem is not None:
            self.c.bump(sem, 1)
        if self.on(eng):
            ins = fn(self.E)
            if sem is not None:
                if eng == "sync":
                    ins.then_inc(self.sems[sem], 1)
                else:
                    self.E.drain(fusable=True).then_inc(self.sems[sem], 1)
            elif (drain or not self.no_drain) and eng in ("vector", "scalar"):
                self.E.drain(fusable=True)

    def K(self, name):
        return self.t["consts"][:, CIDX[name]:CIDX[name] + 1]

    def K1(self, name):
        return self.t["consts"][0:1, CIDX[name]:CIDX[name] + 1]

    def emit(self, eng, E):
        self.eng = eng
        self.E = E
        self.c = Counters()
        self.ab_gate = None
        self.op_gate = {}
        self.pp01_gate = {0: None, 1: None}
        self.dw_gate = {0: None, 1: None}
        self.dw_bank = 0
        self.no_drain = True
        self.tz_pieces = self.a["tz_src"][0] + self.a["tz_src"][1]
        self.tz_p1 = 0
        self.tz_p2 = 0
        self.ph_load()
        self.ph_attn_a()
        self.wait("sync", "tz_st")
        self.ph_ffn()
        self.ph_attn_b()
        self.ph_ffn()
        self.ph_store()

    def ph_load(self):
        t, a = self.t, self.a
        self.dma("sync", t["consts"][:, :], a["consts"], "dma_s")
        self.dma("sync", t["flags"][:, :], a["flags"], "dma_s")
        self.dma("sync", t["offs8"][:, :], a["offs8"], "dma_s")
        self.dma("sync", t["prev"][:, :], a["prev"], "dma_s")
        self.dma("sync", t["gamma"][:, :], a["gamma"], "dma_s")
        self.dma("sync", t["ones_f32"][:, :], a["ones"], "dma_s")
        self.dma("sync", t["ones_row"][:, :], a["ones_row"], "dma_s")
        self.dma("gpsimd", t["ones_bf"][:, :], a["ones"], "dma_g")
        self.dma("gpsimd", t["ones_row_bf"][:, :], a["ones_row"], "dma_g")
        self.dma("gpsimd", t["xres"][:, :], a["x"], "dma_g")
        for e in ("vector", "scalar", "tensor", "gpsimd", "sync"):
            self.wait(e, "dma_s")
            self.wait(e, "dma_g")
        self.op("vector", lambda E: E.memset(self.t["xln"][:, :], 0.0),
                sem="v_prog")

    def ln_stats(self, src3, x2buf):
        for t0 in (0, 1024):
            self.ln_stats_half(src3, x2buf, t0)

    def ln_stats_half(self, src3, x2buf, t0):
        t = self.t
        pp = t["pp"]
        HT = 1024
        for ci in range(NCH):
            xb = x2buf[ci % 2]
            if ci >= 2:
                self.wait("scalar", "lnx2", self.c.val("lnx2") - 1)
            self.wait("scalar", "v_prog", self.c.val("v_prog"))
            self.op("scalar", lambda E, ci=ci, xb=xb: E.activation(
                out=xb[:, 0:HT], in_=src3[:, ci, t0:t0 + HT], func=AF.Square),
                sem="s_prog")
            self.wait("tensor", "s_prog")
            for n in range(2):
                self.op("tensor", lambda E, ci=ci, n=n: E.matmul(
                    pp[0:1, n * 512:(n + 1) * 512], t["ones_bf"][:, 0:1],
                    src3[:, ci, t0 + n * 512:t0 + (n + 1) * 512],
                    start=(ci == 0), stop=(ci == NCH - 1)))
                self.op("tensor", lambda E, ci=ci, n=n, xb=xb: E.matmul(
                    pp[0:1, 1024 + n * 512:1024 + (n + 1) * 512],
                    t["ones_bf"][:, 0:1], xb[:, n * 512:(n + 1) * 512],
                    start=(ci == 0), stop=(ci == NCH - 1)),
                    sem="lnx2" if n == 1 else None)
        self.wait("vector", "lnx2")
        sc = t["SC"]
        self.op("vector", lambda E: E.tensor_scalar(
            out=sc[0:1, 0:HT], in0=pp[0:1, 0:HT], scalar1=1.0 / D, scalar2=None,
            op0=Op.mult))
        self.op("vector", lambda E: E.tensor_scalar(
            out=sc[0:1, 1024:1024 + HT], in0=pp[0:1, 1024:1024 + HT],
            scalar1=1.0 / D, scalar2=None, op0=Op.mult), sem="v_prog")
        self.wait("scalar", "v_prog")
        self.op("scalar", lambda E: E.activation(
            out=sc[0:1, 2048:2048 + HT], in_=sc[0:1, 0:HT], func=AF.Square),
            sem="s_prog")
        self.wait("vector", "s_prog")
        self.op("vector", lambda E: E.tensor_sub(
            sc[0:1, 1024:1024 + HT], sc[0:1, 1024:1024 + HT],
            sc[0:1, 2048:2048 + HT]), sem="v_prog")
        self.wait("scalar", "v_prog")
        self.op("scalar", lambda E: E.activation(
            out=sc[0:1, 1024:1024 + HT], in_=sc[0:1, 1024:1024 + HT], func=AF.Ln,
            bias=self.K1("eps_ln")))
        self.op("scalar", lambda E: E.activation(
            out=sc[0:1, 1024:1024 + HT], in_=sc[0:1, 1024:1024 + HT], func=AF.Exp,
            scale=self.K1("neg_half")), sem="s_prog")
        self.wait("vector", "s_prog")
        self.op("vector", lambda E: E.tensor_copy(
            t["stats"][0:1, t0:t0 + HT], sc[0:1, 0:HT]))
        self.op("vector", lambda E: E.tensor_copy(
            t["stats"][0:1, T + t0:T + t0 + HT], sc[0:1, 1024:1024 + HT]),
            sem="v_prog")
        self.wait("tensor", "v_prog")

    def ln_bcast(self, t0, ps_mean, ps_rstd):
        t = self.t
        self.wait("tensor", "v_prog")
        self.op("tensor", lambda E: E.matmul(
            ps_mean, t["ones_row_bf"][0:1, :], t["stats"][0:1, t0:t0 + 512],
            start=True, stop=True))
        self.op("tensor", lambda E: E.matmul(
            ps_rstd, t["ones_row_bf"][0:1, :],
            t["stats"][0:1, T + t0:T + t0 + 512], start=True, stop=True),
            sem="t_bc")
        self.wait("vector", "t_bc")

    def build_xln_tile(self, src3, nt, first_ln):
        t = self.t
        pp = t["pp"]
        x3 = t["xln"].rearrange("p (c s) -> p c s", c=NCH)
        for hf in range(2):
            t0 = nt * 1024 + hf * 512
            pm, pr = pp[:, 1024:1536], pp[:, 1536:2048]
            self.ln_bcast(t0, pm, pr)
            for ci in range(NCH):
                o = 1 + hf * 512
                self.op("vector", lambda E, ci=ci, o=o, t0=t0: E.tensor_tensor(
                    out=t["SC"][:, 0:512], in0=src3[:, ci, t0:t0 + 512],
                    in1=pm, op=Op.subtract))
                self.op("vector", lambda E, ci=ci, o=o: E.tensor_tensor(
                    out=x3[:, ci, o:o + 512], in0=t["SC"][:, 0:512],
                    in1=pr, op=Op.mult),
                    sem="v_prog" if ci == NCH - 1 else None)
        self.op("vector", lambda E: E.tensor_copy(
            t["xln_last"].rearrange("p (c o) -> p c o", o=2)[:, :, 0:1],
            x3[:, :, 1024:1025]), sem="v_prog")

    def prev_ln(self):
        t = self.t
        pp = self.t["pp"]
        sc = t["SC"]
        saved_nd = self.no_drain
        self.no_drain = False
        self.wait("vector", "s_prog", self.c.val("s_prog"))
        self.op("vector", lambda E: E.tensor_copy(t["prev_bf"][:, :],
                                                  t["prev"][:, :]), sem="v_prog")
        self.wait("scalar", "v_prog")
        self.op("scalar", lambda E: E.activation(
            out=t["prevsq_bf"][:, :], in_=t["prev"][:, :], func=AF.Square),
            sem="s_prog")
        self.wait("tensor", "s_prog")
        self.wait("tensor", "v_prog")
        self.op("tensor", lambda E: E.matmul(
            pp[0:1, 0:16], t["ones_bf"][:, 0:1], t["prev_bf"][:, :],
            start=True, stop=True))
        self.op("tensor", lambda E: E.matmul(
            pp[0:1, 512:528], t["ones_bf"][:, 0:1], t["prevsq_bf"][:, :],
            start=True, stop=True), sem="t_prog")
        self.wait("vector", "t_prog")
        self.op("vector", lambda E: E.tensor_scalar(
            out=t["redtmp"][0:1, 8:24], in0=pp[0:1, 0:16], scalar1=1.0,
            scalar2=0.0, op0=Op.mult, op1=Op.add,
            accum_out=t["redtmp"][0:1, 0:1]))
        self.op("vector", lambda E: E.tensor_copy(
            t["prev_bf"][:, :], t["prev"][:, :]))
        self.op("vector", lambda E: E.tensor_scalar(
            out=t["redtmp"][0:1, 24:40], in0=pp[0:1, 512:528], scalar1=1.0,
            scalar2=0.0, op0=Op.mult, op1=Op.add,
            accum_out=t["redtmp"][0:1, 1:2]))
        self.op("vector", lambda E: E.tensor_scalar(
            out=t["redtmp"][0:1, 2:4], in0=t["redtmp"][0:1, 0:2],
            scalar1=1.0 / D, scalar2=None, op0=Op.mult), sem="v_prog")
        self.wait("scalar", "v_prog")
        self.op("scalar", lambda E: E.activation(
            out=t["redtmp"][0:1, 4:5], in_=t["redtmp"][0:1, 2:3],
            func=AF.Square), sem="s_prog")
        self.wait("vector", "s_prog")
        self.op("vector", lambda E: E.tensor_sub(
            t["redtmp"][0:1, 3:4], t["redtmp"][0:1, 3:4],
            t["redtmp"][0:1, 4:5]), sem="v_prog")
        self.wait("scalar", "v_prog")
        self.op("scalar", lambda E: E.activation(
            out=t["redtmp"][0:1, 3:4], in_=t["redtmp"][0:1, 3:4], func=AF.Ln,
            bias=self.K1("eps_ln")), drain=True)
        self.op("scalar", lambda E: E.activation(
            out=t["redtmp"][0:1, 3:4], in_=t["redtmp"][0:1, 3:4], func=AF.Exp,
            scale=self.K1("neg_half")), sem="s_prog")
        self.wait("vector", "s_prog")
        self.op("vector", lambda E: E.tensor_copy(
            t["prevstat_bf"][0:1, :], t["redtmp"][0:1, 2:4]), sem="v_prog")
        self.wait("tensor", "v_prog")
        self.op("tensor", lambda E: E.matmul(
            pp[:, 1024:1026], t["ones_row_bf"][0:1, :], t["prevstat_bf"][0:1, :],
            start=True, stop=True), sem="t_prog")
        self.wait("vector", "t_prog")
        self.op("vector", lambda E: E.tensor_scalar(
            out=t["prev_ln"].rearrange("p (c o) -> p c o", o=2)[:, :, 0:1]
                .rearrange("p c o -> p (c o)"),
            in0=t["prev"][:, :],
            scalar1=pp[:, 1024:1025], scalar2=pp[:, 1025:1026],
            op0=Op.subtract, op1=Op.mult), sem="v_prog")
        self.no_drain = saved_nd

    def ph_attn_a(self):
        t, a = self.t, self.a
        xres3 = t["xres"].rearrange("p (c t) -> p c t", c=NCH)
        xv3 = t["pool"].rearrange("p (c t) -> p c t", c=NCH)
        x2pair = (t["pool"].rearrange("p (c t) -> p c t", c=NCH)[:, 0, :],
                  t["pool"].rearrange("p (c t) -> p c t", c=NCH)[:, 1, :])
        self.ln_stats(xres3, x2pair)
        self.prev_ln()
        self.tz_p1_gate_v = self.c.val("v_prog")
        for nt in range(2):
            self.build_xln_tile(xres3, nt, first_ln=True)
            self.wact_sweep(nt, xv3)
        self.tz_pump_p1(len(self.tz_pieces))
        self.wait("scalar", "tz_fr")
        self.tz_am()
        self.scan_phase(xv3)
        self.tz_p2_gate_v = self.c.val("v_prog")
        self.tz_p2_gate_s = self.c.val("s_prog")
        for nt in range(2):
            self.build_xln_tile(xres3, nt, first_ln=False)
            self.attn_a_epilogue(nt, xres3, xv3)

    def wact_sweep(self, nt, xv3):
        t, a = self.t, self.a
        pp = t["pp"]
        x3 = t["xln"].rearrange("p (c s) -> p c s", c=NCH)
        self.wait("tensor", "v_prog")
        if nt == 0:
            self.pp01_gate[0] = ("v_prog", self.c.val("v_prog"))
            self.pp01_gate[1] = ("v_prog", self.c.val("v_prog"))
        for m in range(16):
            for h2 in range(2):
                g = (nt * 16 + m) * 2 + h2
                buf = t["wsl"][g % 3]
                if self.on("gpsimd") and g >= 3:
                    self.E.wait_ge(self.sems["w_fr"], g - 2)
                ldv = self.dma("gpsimd", buf[:, :],
                               a["wact"][m][:, h2 * 2048:(h2 + 1) * 2048], "w_ld")
                if h2 == 0:
                    ld0 = ldv
                else:
                    ld1 = ldv
            for n2 in range(2):
                gate = self.pp01_gate[n2]
                if gate is not None:
                    self.wait("tensor", gate[0], gate[1])
                ps = pp[:, n2 * 512:(n2 + 1) * 512]
                bnd = t["prev_ln"] if nt == 0 else t["xln_last"]
                for h2 in range(2):
                    g = (nt * 16 + m) * 2 + h2
                    buf = t["wsl"][g % 3]
                    self.wait("tensor", "w_ld", ld0 if h2 == 0 else ld1)
                    for kk in range(16):
                        k = h2 * 16 + kk
                        ci = k % 16
                        off = 0 if k < 16 else 1
                        rhs = x3[:, ci, off + n2 * 512:off + n2 * 512 + 512]
                        last = (k == 31 and n2 == 1)
                        self.op("tensor",
                                lambda E, ps=ps, kk=kk, rhs=rhs, buf=buf,
                                k=k, n2=n2: E.matmul(
                                    ps, buf[:, kk * 128:(kk + 1) * 128], rhs,
                                    start=(k == 0), stop=(k == 31 and n2 == 1)),
                                sem="t_prog" if last else None)
                if n2 == 0:
                    g0 = (nt * 16 + m) * 2
                    buf0 = t["wsl"][g0 % 3]
                    for kk in range(16):
                        self.op("tensor",
                                lambda E, ps=ps, kk=kk, buf0=buf0, bnd=bnd:
                                E.matmul(
                                    ps[:, 0:1],
                                    buf0[:, kk * 128:(kk + 1) * 128],
                                    bnd[:, 2 * kk:2 * kk + 1],
                                    start=False, stop=(kk == 15)),
                                sem="t_prog" if kk == 15 else None)
                self.wait("scalar", "t_prog")
                self.wait("scalar", "v_prog", self.c.val("v_prog"))
                self.op("scalar", lambda E, ps=ps: E.activation(
                    out=t["SC"][:, 2048:2560], in_=ps, func=AF.Sigmoid),
                    sem="s_prog")
                self.pp01_gate[n2] = ("s_prog", self.c.val("s_prog"))
                self.wait("vector", "s_prog")
                self.op("vector", lambda E, m=m, nt=nt, n2=n2: E.tensor_scalar(
                    out=xv3[:, m, nt * 1024 + n2 * 512:nt * 1024 + n2 * 512 + 512],
                    in0=t["SC"][:, 2048:2560], scalar1=0.5, scalar2=None,
                    op0=Op.max), sem="v_prog")
            self.op("tensor", lambda E: E.nop(), sem="w_fr")
            self.op("tensor", lambda E: E.nop(), sem="w_fr")
            self.tz_pump_p1(5)

    def chunk_s_theta(self, xv3, c8):
        t = self.t
        sc = t["SC"]
        sc1, sc2, sc3 = sc[:, 0:T], sc[:, 2048:2048 + T], sc[:, 4096:4096 + T]
        self.wait("scalar", "v_prog", self.c.val("v_prog"))
        self.op("scalar", lambda E, c8=c8: E.activation(
            out=sc1, in_=xv3[:, c8, :], func=AF.Square))
        self.op("scalar", lambda E, c8=c8: E.activation(
            out=sc2, in_=xv3[:, 8 + c8, :], func=AF.Square), sem="s_prog")
        self.wait("vector", "s_prog")
        self.op("vector", lambda E: E.tensor_add(sc1, sc1, sc2))
        self.op("vector", lambda E, c8=c8: E.tensor_sub(
            sc2, xv3[:, 8 + c8, :], xv3[:, c8, :]))
        self.op("vector", lambda E, c8=c8: E.tensor_add(
            sc3, xv3[:, 8 + c8, :], xv3[:, c8, :]), sem="v_prog")
        self.wait("scalar", "v_prog")
        self.op("scalar", lambda E: E.activation(out=sc1, in_=sc1, func=AF.Ln))
        self.op("scalar", lambda E: E.activation(out=sc3, in_=sc3, func=AF.Ln))
        self.op("scalar", lambda E: E.activation(
            out=sc3, in_=sc3, func=AF.Exp, scale=self.K("neg_one")), sem="s_prog")
        self.wait("vector", "s_prog")
        self.op("vector", lambda E: E.tensor_mul(sc2, sc2, sc3), sem="v_prog")
        self.wait("scalar", "v_prog")
        self.op("scalar", lambda E: E.activation(
            out=sc2, in_=sc2, func=AF.Arctan), sem="s_prog")
        self.wait("vector", "s_prog")

    def scan_phase(self, xv3):
        t, a = self.t, self.a
        sc = t["SC"]
        sc1, sc2, sc3 = sc[:, 0:T], sc[:, 2048:2048 + T], sc[:, 4096:4096 + T]
        for c8 in range(8):
            self.chunk_s_theta(xv3, c8)
            self.op("vector", lambda E, c8=c8: E.tensor_reduce(
                out=t["summ"][:, 16 + c8:17 + c8], in_=sc2, axis=AX.X, op=Op.add))
            self.op("vector", lambda E, c8=c8: E.tensor_scalar(
                out=t["summ"][:, 16 + c8:17 + c8],
                in0=t["summ"][:, 16 + c8:17 + c8],
                scalar1=self.K("t_quarter_pi"), scalar2=None, op0=Op.add))
            self.op("vector", lambda E: E.tensor_tensor_scan(
                out=sc3, data0=sc1, data1=sc1, initial=-1e30, op0=Op.max,
                op1=Op.bypass))
            self.op("vector", lambda E, c8=c8: E.tensor_copy(
                t["summ"][:, c8:c8 + 1], sc3[:, T - 1:T]))
            self.op("vector", lambda E, c8=c8: E.scalar_tensor_tensor(
                out=sc2, in0=sc1, scalar=1.0, in1=sc3, op0=Op.mult,
                op1=Op.subtract, accum_out=t["summ"][:, 8 + c8:9 + c8]),
                sem="v_prog")
        self.wait("gpsimd", "v_prog")
        self.dma("gpsimd", a["cc_ag_in"], t["summ"][:, 0:24], "dma_g")
        self.wait("gpsimd", "dma_g")
        if CC_BYPASS:
            self.dma("gpsimd", t["peers"][:, 0:24], a["cc_ag_in"], "dma_g")
        else:
            ccv = self.c.bump("cc", 1)
            if self.on("gpsimd"):
                self.E.collective_compute(
                    "AllGather", Op.bypass,
                    replica_groups=[[0, 1], [2, 3], [4, 5], [6, 7]],
                    ins=[a["cc_ag_in"]], outs=[a["cc_ag_out"]],
                ).then_inc(self.sems["cc"], 1)
            self.wait("gpsimd", "cc", ccv)
            self.dma("gpsimd", t["peers"][:, 0:24], a["cc_ag_out_r0"], "dma_g")
        self.wait("vector", "dma_g")
        self.op("vector", lambda E: E.scalar_tensor_tensor(
            out=t["carry"][:, 0:8], in0=t["peers"][:, 0:8],
            scalar=t["flags"][:, 0:1], in1=t["offs8"][:, 0:8],
            op0=Op.mult, op1=Op.add))
        self.op("vector", lambda E: E.tensor_scalar(
            out=t["carry"][:, 8:24], in0=t["peers"][:, 8:24],
            scalar1=t["flags"][:, 0:1], scalar2=None, op0=Op.mult), sem="v_prog")
        for c8 in range(8):
            self.chunk_s_theta(xv3, c8)
            self.op("vector", lambda E: E.tensor_tensor_scan(
                out=sc3, data0=sc1, data1=sc1, initial=-1e30, op0=Op.max,
                op1=Op.bypass))
            self.op("vector", lambda E, c8=c8: E.tensor_scalar(
                out=sc3, in0=sc3, scalar1=t["carry"][:, c8:c8 + 1],
                scalar2=None, op0=Op.max))
            self.op("vector", lambda E: E.tensor_sub(sc1, sc1, sc3))
            self.op("vector", lambda E, c8=c8: E.tensor_tensor_scan(
                out=sc1, data0=sc1, data1=sc1,
                initial=t["carry"][:, 8 + c8:9 + c8], op0=Op.add, op1=Op.bypass))
            self.op("vector", lambda E, c8=c8: E.tensor_tensor_scan(
                out=sc2, data0=sc2,
                data1=self.K("quarter_pi").broadcast_to((128, T)),
                initial=t["carry"][:, 16 + c8:17 + c8], op0=Op.add,
                op1=Op.add), sem="v_prog")
            self.wait("scalar", "v_prog")
            self.op("scalar", lambda E: E.activation(
                out=sc1, in_=sc1, func=AF.Exp, scale=self.K("half")),
                sem="s_prog")
            self.op("vector", lambda E: E.tensor_scalar(
                out=sc3, in0=sc2, scalar1=self.K("inv2pi"),
                scalar2=self.K("magic"), op0=Op.mult, op1=Op.add))
            self.op("vector", lambda E: E.tensor_scalar(
                out=sc3, in0=sc3, scalar1=self.K("magic"), scalar2=None,
                op0=Op.subtract))
            self.op("vector", lambda E: E.scalar_tensor_tensor(
                out=sc3, in0=sc3, scalar=self.K("neg2pi"), in1=sc2,
                op0=Op.mult, op1=Op.add))
            self.op("vector", lambda E: E.tensor_scalar(
                out=sc3, in0=sc3, scalar1=self.K("pi"), scalar2=self.K("neg_pi"),
                op0=Op.min, op1=Op.max), sem="v_prog")
            self.wait("scalar", "v_prog")
            self.op("scalar", lambda E: E.activation(
                out=sc3, in_=sc3, func=AF.Sin), sem="s_prog")
            self.wait("vector", "s_prog")
            self.op("vector", lambda E, c8=c8: E.tensor_tensor(
                out=xv3[:, 8 + c8, :], in0=sc1, in1=sc3, op=Op.mult),
                sem="v_prog")
            self.op("vector", lambda E: E.tensor_scalar(
                out=sc3, in0=sc2, scalar1=self.K("inv2pi"),
                scalar2=self.K("quarter"), op0=Op.mult, op1=Op.add))
            self.op("vector", lambda E: E.tensor_scalar(
                out=sc3, in0=sc3, scalar1=self.K("magic"),
                scalar2=self.K("magic"), op0=Op.add, op1=Op.subtract))
            self.op("vector", lambda E: E.scalar_tensor_tensor(
                out=sc3, in0=sc3, scalar=self.K("neg2pi"), in1=sc2,
                op0=Op.mult, op1=Op.add))
            self.op("vector", lambda E: E.tensor_scalar(
                out=sc3, in0=sc3, scalar1=self.K("half_pi"),
                scalar2=self.K("neg_3half_pi"), op0=Op.min, op1=Op.max),
                sem="v_prog")
            self.wait("scalar", "v_prog")
            self.op("scalar", lambda E: E.activation(
                out=sc3, in_=sc3, func=AF.Sin, bias=self.K("half_pi")),
                sem="s_prog")
            self.wait("vector", "s_prog")
            self.op("vector", lambda E, c8=c8: E.tensor_tensor(
                out=xv3[:, c8, :], in0=sc1, in1=sc3, op=Op.mult), sem="v_prog")

    def attn_a_epilogue(self, nt, xres3, xv3):
        t, a = self.t, self.a
        pp = t["pp"]
        x3 = t["xln"].rearrange("p (c s) -> p c s", c=NCH)
        wv = t["SC"][:, :].bitcast(BF16)[:, 8192:8192 + 2048]
        t0 = nt * 1024
        self.wait("gpsimd", "v_prog")
        self.wait("tensor", "v_prog")
        for m in range(16):
            for h2 in range(2):
                g = (nt * 16 + m) * 2 + h2
                buf = t["wsl"][g % 3]
                if self.on("gpsimd") and g >= 3:
                    self.E.wait_ge(self.sems["wr_fr"], g - 2)
                ldv = self.dma("gpsimd", buf[:, :],
                               a["wrec"][m][:, h2 * 2048:(h2 + 1) * 2048], "wr_ld")
                if h2 == 0:
                    ld0 = ldv
                else:
                    ld1 = ldv
            gv = nt * 16 + m
            if self.on("gpsimd") and gv >= 1:
                self.E.wait_ge(self.sems["wv_fr"], gv)
            ldv2 = self.dma("gpsimd", wv, a["wval"][m], "wv_ld")
            for n2 in range(2):
                gate = self.pp01_gate[0]
                if gate is not None:
                    self.wait("tensor", gate[0], gate[1])
                ps_r = pp[:, 0:512]
                ps_o = pp[:, 512:1024]
                bnd = t["prev_ln"] if nt == 0 else t["xln_last"]
                for h2 in range(2):
                    g = (nt * 16 + m) * 2 + h2
                    buf = t["wsl"][g % 3]
                    self.wait("tensor", "wr_ld", ld0 if h2 == 0 else ld1)
                    for kk in range(16):
                        k = h2 * 16 + kk
                        ci = k % 16
                        off = 0 if k < 16 else 1
                        rhs = x3[:, ci, off + n2 * 512:off + n2 * 512 + 512]
                        last = (k == 31 and n2 == 1)
                        self.op("tensor",
                                lambda E, ps_r=ps_r, kk=kk, rhs=rhs, buf=buf,
                                k=k, n2=n2: E.matmul(
                                    ps_r, buf[:, kk * 128:(kk + 1) * 128], rhs,
                                    start=(k == 0), stop=(k == 31 and n2 == 1)),
                                sem="t_prog" if last else None)
                if n2 == 0:
                    g0 = (nt * 16 + m) * 2
                    buf0 = t["wsl"][g0 % 3]
                    for kk in range(16):
                        self.op("tensor",
                                lambda E, ps_r=ps_r, kk=kk, buf0=buf0, bnd=bnd:
                                E.matmul(
                                    ps_r[:, 0:1],
                                    buf0[:, kk * 128:(kk + 1) * 128],
                                    bnd[:, 2 * kk:2 * kk + 1],
                                    start=False, stop=(kk == 15)),
                                sem="t_prog" if kk == 15 else None)
                self.wait("tensor", "wv_ld", ldv2)
                gate = self.pp01_gate[1]
                if gate is not None:
                    self.wait("tensor", gate[0], gate[1])
                for k in range(16):
                    rhs = xv3[:, k, t0 + n2 * 512:t0 + n2 * 512 + 512]
                    self.op("tensor",
                            lambda E, ps_o=ps_o, k=k, rhs=rhs: E.matmul(
                                ps_o, wv[:, k * 128:(k + 1) * 128], rhs,
                                start=(k == 0), stop=(k == 15)),
                            sem="t_prog" if k == 15 else None)
                self.wait("scalar", "t_prog")
                self.wait("scalar", "v_prog", self.c.val("v_prog"))
                self.op("scalar", lambda E, ps_r=ps_r: E.activation(
                    out=t["SC"][:, 0:512], in_=ps_r, func=AF.Sigmoid),
                    sem="s_prog")
                self.pp01_gate[0] = ("s_prog", self.c.val("s_prog"))
                self.wait("vector", "s_prog")
                self.op("vector", lambda E: E.tensor_scalar(
                    out=t["SC"][:, 0:512], in0=t["SC"][:, 0:512], scalar1=0.5,
                    scalar2=None, op0=Op.max))
                self.op("vector", lambda E: E.tensor_mul(
                    t["SC"][:, 0:512], t["SC"][:, 0:512], t["SC"][:, 0:512]))
                self.op("vector", lambda E: E.tensor_mul(
                    t["SC"][:, 0:512], t["SC"][:, 0:512], t["SC"][:, 0:512]))
                self.op("vector", lambda E, ps_o=ps_o: E.tensor_tensor(
                    out=t["SC"][:, 2048:2560], in0=ps_o, in1=t["SC"][:, 0:512],
                    op=Op.mult))
                self.op("vector", lambda E, m=m, t0=t0, n2=n2: E.tensor_tensor(
                    out=xres3[:, m, t0 + n2 * 512:t0 + n2 * 512 + 512],
                    in0=xres3[:, m, t0 + n2 * 512:t0 + n2 * 512 + 512],
                    in1=t["SC"][:, 2048:2560], op=Op.add), sem="v_prog")
                self.pp01_gate[1] = ("v_prog", self.c.val("v_prog"))
            self.op("tensor", lambda E: E.nop(), sem="wr_fr")
            self.op("tensor", lambda E: E.nop(), sem="wr_fr")
            self.op("tensor", lambda E: E.nop(), sem="wv_fr")
            self.tz_pump_p2(5)

    def tz_pump_p1(self, n):
        t, a = self.t, self.a
        sc = t["SC"]
        SG = (sc[:, 512:1536], sc[:, 2560:3584])
        for _ in range(n):
            i = self.tz_p1
            if i >= len(self.tz_pieces):
                return
            self.tz_p1 += 1
            src_ap, _dst, w = self.tz_pieces[i]
            if i < 2:
                self.wait("sync", "v_prog", self.tz_p1_gate_v)
            else:
                self.wait("sync", "tz_fr", i - 1)
            ldv = self.dma("sync", SG[i % 2], src_ap, "tz_ld")
            self.wait("vector", "tz_ld", ldv)
            self.op("vector", lambda E, i=i, b=SG[i % 2]: E.tensor_reduce(
                out=t["tzsl"][:, i:i + 1], in_=b, axis=AX.X, op=Op.add,
                apply_absolute_value=True), sem="tz_fr")

    def tz_am(self):
        t, a = self.t, self.a
        pp = t["pp"]
        n0 = len(a["tz_src"][0])
        spans = [(0, n0), (n0, len(self.tz_pieces))]
        stg = t["SC"][:, 2560:2656].bitcast(BF16)
        self.wait("vector", "tz_fr")
        npc = len(self.tz_pieces)
        self.op("vector", lambda E, npc=npc: E.tensor_copy(
            stg[:, 0:npc], t["tzsl"][:, 0:npc]), sem="v_prog")
        self.wait("tensor", "v_prog")
        for wi, (lo, hi) in enumerate(spans):
            self.op("tensor", lambda E, wi=wi, lo=lo, hi=hi: E.matmul(
                pp[0:1, 2048 + wi * 512:2048 + wi * 512 + (hi - lo)],
                t["ones_bf"][:, 0:1], stg[:, lo:hi], start=True, stop=True),
                sem="t_prog" if wi == 1 else None)
        self.wait("vector", "t_prog")
        for wi, (lo, hi) in enumerate(spans):
            nel = float(a["tz_nel"][wi])
            self.op("vector", lambda E, wi=wi, lo=lo, hi=hi: E.tensor_scalar(
                out=t["SC"][0:1, 512:512 + (hi - lo)],
                in0=pp[0:1, 2048 + wi * 512:2048 + wi * 512 + (hi - lo)],
                scalar1=1.0, scalar2=0.0, op0=Op.mult, op1=Op.add,
                accum_out=t["redtmp"][0:1, 5 + wi:6 + wi]), drain=True)
            self.op("vector", lambda E: E.memset(
                t["SC"][0:1, 1024:1536], 0.0))
            self.op("vector", lambda E, wi=wi, nel=nel: E.tensor_scalar(
                out=t["redtmp"][0:1, 7 + wi:8 + wi],
                in0=t["redtmp"][0:1, 5 + wi:6 + wi],
                scalar1=1.0 / nel, scalar2=1e-5, op0=Op.mult, op1=Op.max),
                drain=True)
            self.op("vector", lambda E: E.memset(
                t["SC"][0:1, 1024:1536], 0.0))
            self.op("vector", lambda E, wi=wi: E.tensor_scalar(
                out=t["redtmp"][0:1, 44 + 3 * wi:45 + 3 * wi],
                in0=t["redtmp"][0:1, 7 + wi:8 + wi],
                scalar1=-0.7, scalar2=None, op0=Op.mult), drain=True)
            self.op("vector", lambda E, wi=wi: E.tensor_scalar(
                out=t["redtmp"][0:1, 45 + 3 * wi:46 + 3 * wi],
                in0=t["redtmp"][0:1, 7 + wi:8 + wi],
                scalar1=0.7, scalar2=None, op0=Op.mult), drain=True)
            self.op("vector", lambda E, wi=wi: E.tensor_scalar(
                out=t["redtmp"][0:1, 46 + 3 * wi:47 + 3 * wi],
                in0=t["redtmp"][0:1, 7 + wi:8 + wi],
                scalar1=1.0, scalar2=None, op0=Op.mult),
                sem="v_prog" if wi == 1 else None, drain=True)
        self.wait("tensor", "v_prog")
        self.op("tensor", lambda E: E.matmul(
            pp[:, 1024:1030], t["ones_row"][0:1, :],
            t["redtmp"][0:1, 44:50], start=True, stop=True), sem="t_prog")
        self.wait("vector", "t_prog")
        self.op("vector", lambda E: E.tensor_copy(
            t["bc4"][:, 0:3], pp[:, 1024:1027]))
        self.op("vector", lambda E: E.tensor_copy(
            t["bc6"][:, 0:3], pp[:, 1027:1030]), sem="v_prog")
        self.tz_am_t = self.c.val("t_prog")

    def tz_pump_p2(self, n):
        t, a = self.t, self.a
        sc = t["SC"]
        SG = (sc[:, 512:1536], sc[:, 2560:3584])
        n0 = len(a["tz_src"][0])
        for _ in range(n):
            i = self.tz_p2
            if i >= len(self.tz_pieces):
                return
            self.tz_p2 += 1
            src_ap, dst_ap, w = self.tz_pieces[i]
            bcs = t["bc4"] if i < n0 else t["bc6"]
            if i < 2:
                self.wait("sync", "v_prog", self.tz_p2_gate_v)
                self.wait("sync", "s_prog", self.tz_p2_gate_s)
                self.wait("sync", "t_prog", self.tz_am_t)
            else:
                self.wait("sync", "tz_st", (i - 1) * 16)
            ldv = self.dma("sync", SG[i % 2], src_ap, "tz_ld")
            self.wait("vector", "tz_ld", ldv)
            sb0 = sc[:, 3584:4096].bitcast(BF16)
            sb1 = sc[:, 1536:2048].bitcast(BF16)
            self.op("vector", lambda E, b=SG[i % 2], sb0=sb0, bcs=bcs:
                    E.tensor_scalar(out=sb0, in0=b, scalar1=bcs[:, 1:2],
                                    scalar2=bcs[:, 2:3], op0=Op.is_gt,
                                    op1=Op.mult))
            self.op("vector", lambda E, b=SG[i % 2], sb1=sb1, bcs=bcs:
                    E.tensor_scalar(out=sb1, in0=b, scalar1=bcs[:, 0:1],
                                    scalar2=bcs[:, 2:3], op0=Op.is_lt,
                                    op1=Op.mult))
            res = SG[i % 2].bitcast(BF16)[:, 0:1024]
            self.op("vector", lambda E, res=res, sb0=sb0, sb1=sb1:
                    E.tensor_sub(res, sb0, sb1), sem="tz_v")
            self.wait("gpsimd", "tz_v")
            self.dma("gpsimd", dst_ap, res, "tz_st")

    def phase_barrier(self):
        for e in ("vector", "scalar", "tensor", "gpsimd", "sync"):
            for s in ("v_prog", "s_prog", "t_prog"):
                self.wait(e, s)

    def ph_ternarize(self):
        t, a = self.t, self.a
        pp = t["pp"]
        sc = t["SC"]
        self.phase_barrier()
        for wi in range(2):
            pieces = a["tz_src"][wi]
            nel = float(a["tz_nel"][wi])
            for s_i, (src_ap, _dst, w) in enumerate(pieces):
                third = s_i % 3
                scv = sc[:, third * 2048:third * 2048 + w]
                if s_i >= 3:
                    self.wait("sync", "tz_fr", self.c.val("tz_fr") - 2)
                ldv = self.dma("sync", scv, src_ap, "tz_ld")
                self.wait("scalar", "tz_ld", ldv)
                self.wait("scalar", "t_prog", self.c.val("t_prog"))
                self.op("scalar", lambda E, scv=scv, w=w: E.activation(
                    out=t["wsl"][0][:, 0:w], in_=scv, func=AF.Abs), sem="s_prog")
                self.op("scalar", lambda E: E.nop(), sem="tz_fr")
                self.wait("tensor", "s_prog")
                for n in range(w // 512):
                    self.op("tensor", lambda E, n=n: E.matmul(
                        pp[0:1, (n % 4) * 512:(n % 4) * 512 + 512],
                        t["ones_bf"][:, 0:1],
                        t["wsl"][0][:, n * 512:(n + 1) * 512],
                        start=True, stop=True),
                        sem="t_prog" if n == w // 512 - 1 else None)
                self.wait("vector", "t_prog")
                self.op("vector", lambda E, w=w, s_i=s_i: E.tensor_scalar(
                    out=t["pool"][0:1, 0:2 * w].bitcast(F32),
                    in0=pp[0:1, 0:w], scalar1=1.0, scalar2=0.0,
                    op0=Op.mult, op1=Op.add,
                    accum_out=t["redtmp"][0:1, 50 + s_i:51 + s_i]), sem="v_prog")
                self.wait("tensor", "v_prog", self.c.val("v_prog"))
            npc = len(pieces)
            self.op("vector", lambda E, npc=npc: E.tensor_reduce(
                out=t["redtmp"][0:1, 5:6], in_=t["redtmp"][0:1, 50:50 + npc],
                axis=AX.X, op=Op.add), drain=True)
            self.op("vector", lambda E: E.memset(
                t["pool"][0:1, 0:512], 0.0))
            self.op("vector", lambda E, nel=nel: E.tensor_scalar(
                out=t["redtmp"][0:1, 7:8], in0=t["redtmp"][0:1, 5:6],
                scalar1=1.0 / nel, scalar2=1e-5, op0=Op.mult, op1=Op.max),
                drain=True)
            self.op("vector", lambda E: E.memset(
                t["pool"][0:1, 0:512], 0.0))
            self.op("vector", lambda E: E.tensor_scalar(
                out=t["redtmp"][0:1, 44:45], in0=t["redtmp"][0:1, 7:8],
                scalar1=-0.7, scalar2=None, op0=Op.mult), drain=True)
            self.op("vector", lambda E: E.tensor_scalar(
                out=t["redtmp"][0:1, 45:46], in0=t["redtmp"][0:1, 7:8],
                scalar1=0.7, scalar2=None, op0=Op.mult), drain=True)
            self.op("vector", lambda E: E.tensor_scalar(
                out=t["redtmp"][0:1, 46:47], in0=t["redtmp"][0:1, 7:8],
                scalar1=0.5, scalar2=None, op0=Op.mult), sem="v_prog")
            self.wait("tensor", "v_prog")
            self.op("tensor", lambda E: E.matmul(
                pp[:, 1024:1027], t["ones_row"][0:1, :],
                t["redtmp"][0:1, 44:47], start=True, stop=True), sem="t_prog")
            self.wait("vector", "t_prog")
            self.op("vector", lambda E: E.tensor_copy(
                t["bc4"][:, 0:3], pp[:, 1024:1027]), sem="v_prog")
            self.wait("scalar", "v_prog")
            self.wait("sync", "t_prog", self.c.val("t_prog"))
            for s_i, (src_ap, dst_ap, w) in enumerate(pieces):
                third = s_i % 3
                scv = sc[:, third * 2048:third * 2048 + w]
                if s_i >= 3:
                    self.wait("sync", "tz_fr", self.c.val("tz_fr") - 2)
                ldv = self.dma("sync", scv, src_ap, "tz_ld")
                self.wait("scalar", "tz_ld", ldv)
                self.wait("scalar", "v_prog", self.c.val("v_prog"))
                self.op("scalar", lambda E, scv=scv, w=w: E.sign(
                    out=t["wsl"][0][:, 0:w], in_=scv, bias=t["bc4"][:, 0:1]))
                self.op("scalar", lambda E, scv=scv, w=w: E.sign(
                    out=t["wsl"][1][:, 0:w], in_=scv, bias=t["bc4"][:, 1:2]),
                    sem="s_prog")
                self.wait("vector", "s_prog")
                self.op("vector", lambda E, w=w: E.tensor_add(
                    t["wsl"][2][:, 0:w], t["wsl"][0][:, 0:w], t["wsl"][1][:, 0:w]))
                self.op("vector", lambda E, w=w: E.tensor_scalar(
                    out=t["wsl"][2][:, 0:w], in0=t["wsl"][2][:, 0:w],
                    scalar1=t["bc4"][:, 2:3], scalar2=None, op0=Op.mult),
                    sem="v_prog")
                self.wait("gpsimd", "v_prog")
                stv = self.dma("gpsimd", dst_ap, t["wsl"][2][:, 0:w], "tz_st")
                self.wait("scalar", "tz_st", stv)
                self.op("scalar", lambda E: E.nop(), sem="tz_fr")
            self.wait("sync", "tz_st")

    def ph_ffn(self):
        t, a = self.t, self.a
        pp = t["pp"]
        xres3 = t["xres"].rearrange("p (c t) -> p c t", c=NCH)
        x3 = t["xln"].rearrange("p (c s) -> p c s", c=NCH)
        pool = t["pool"]
        wp = [pool[:, 20480 + i * 5120:20480 + (i + 1) * 5120] for i in range(2)]
        self.phase_barrier()
        self.ln_stats(xres3, (pool.rearrange("p (c t) -> p c t", c=NCH)[:, 0, :],
                              pool.rearrange("p (c t) -> p c t", c=NCH)[:, 1, :]))
        for nt in range(4):
            t0 = nt * 512
            self.ln_bcast(t0, pp[:, 1024:1536], pp[:, 1536:2048])
            for ci in range(NCH):
                self.op("vector", lambda E, ci=ci, t0=t0: E.tensor_tensor(
                    out=t["SC"][:, 0:512], in0=xres3[:, ci, t0:t0 + 512],
                    in1=pp[:, 1024:1536], op=Op.subtract))
                self.op("vector", lambda E, ci=ci: E.tensor_tensor(
                    out=x3[:, ci, 0:512], in0=t["SC"][:, 0:512],
                    in1=pp[:, 1536:2048], op=Op.mult),
                    sem="v_prog" if ci == NCH - 1 else None)
            self.wait("tensor", "v_prog")
            for m in range(40):
                g = self.c.bump("f_seq", 1) - 1
                buf = t["wsl"][g % 3]
                if self.on("sync") and g >= 3:
                    self.E.wait_ge(self.sems["f_fr"], g - 2)
                ldv = self.dma("sync", buf[:, :], a["wfc_t_sl"][m], "f_ld")
                self.wait("tensor", "f_ld", ldv)
                bk = m % 2
                gate = self.pp01_gate[bk]
                if gate is not None:
                    self.wait("tensor", gate[0], gate[1])
                ps = pp[:, bk * 512:bk * 512 + 512]
                for k in range(16):
                    self.op("tensor", lambda E, ps=ps, k=k, buf=buf: E.matmul(
                        ps, buf[:, k * 128:(k + 1) * 128],
                        x3[:, k, 0:512], start=(k == 0), stop=(k == 15)),
                        sem="t_prog" if k == 15 else None)
                self.op("tensor", lambda E: E.nop(), sem="f_fr")
                self.wait("scalar", "t_prog")
                self.wait("scalar", "v_prog", self.c.val("v_prog"))
                self.op("scalar", lambda E, ps=ps: E.activation(
                    out=t["SC"][:, 2048:2560], in_=ps, func=AF.Relu),
                    sem="s_prog")
                self.pp01_gate[bk] = ("s_prog", self.c.val("s_prog"))
                self.wait("vector", "s_prog")
                self.op("vector", lambda E, m=m: E.tensor_mul(
                    pool[:, m * 512:(m + 1) * 512], t["SC"][:, 2048:2560],
                    t["SC"][:, 2048:2560]), sem="v_prog")
            self.wait("tensor", "v_prog")
            for mo in range(16):
                g = self.c.bump("fp_seq", 1) - 1
                buf = wp[g % 2]
                if self.on("sync") and g >= 2:
                    self.E.wait_ge(self.sems["fp_fr"], g - 1)
                ldv = self.dma("sync", buf, a["wproj_t_sl"][mo], "fp_ld")
                self.wait("tensor", "fp_ld", ldv)
                bk = mo % 2
                gate = self.pp01_gate[bk]
                if gate is not None:
                    self.wait("tensor", gate[0], gate[1])
                ps = pp[:, bk * 512:bk * 512 + 512]
                for k in range(40):
                    self.op("tensor", lambda E, ps=ps, k=k, buf=buf: E.matmul(
                        ps, buf[:, k * 128:(k + 1) * 128],
                        pool[:, k * 512:(k + 1) * 512],
                        start=(k == 0), stop=(k == 39)),
                        sem="t_prog" if k == 39 else None)
                self.op("tensor", lambda E: E.nop(), sem="fp_fr")
                self.wait("vector", "t_prog")
                self.op("vector", lambda E, mo=mo, t0=t0, ps=ps: E.tensor_tensor(
                    out=xres3[:, mo, t0:t0 + 512],
                    in0=xres3[:, mo, t0:t0 + 512], in1=ps, op=Op.add),
                    sem="v_prog")
                self.pp01_gate[bk] = ("v_prog", self.c.val("v_prog"))

    def ph_attn_b(self):
        t, a = self.t, self.a
        pp = t["pp"]
        sc = t["SC"]
        xres3 = t["xres"].rearrange("p (c t) -> p c t", c=NCH)
        x3 = t["xln"].rearrange("p (c s) -> p c s", c=NCH)
        pool = t["pool"]
        SLAB = pool[:, 0:6144]
        WSM = pool[:, 6144:12288]
        KTM = pool[:, 12288:13312]
        VTM = pool[:, 13312:14336]
        KFM = pool[:, 14336:15360]
        VFM = pool[:, 15360:16384]
        SCR = [pool[:, 16384 + i * 1024:16384 + (i + 1) * 1024] for i in range(4)]
        DWF = pool[:, 20480:32768].bitcast(F32)
        self.phase_barrier()
        self.dma("gpsimd", WSM, a["wsmall"], "dma_g")
        self.dma("gpsimd", t["ident"][:, :], a["ident"], "dma_g")
        self.dma("gpsimd", t["pool"][:, 19456:20224], a["lrproj"], "dma_g")
        self.wait("tensor", "dma_g")
        self.wait("vector", "dma_g")
        self.op("vector", lambda E: E.memset(DWF, 0.0), sem="v_prog")
        self.ln_stats(xres3, (SCR[0], SCR[1]))
        for tile in range(2):
            t0 = tile * 1024
            self.build_xln_ab(xres3, x3, t0)
            self.lr_tile(x3)
            for h in range(H):
                self.ab_head_ph1(tile, h, x3)
        self.wait("gpsimd", "v_prog")
        self.dma("gpsimd", a["cc_ar_in"], DWF, "dma_g")
        self.wait("gpsimd", "dma_g")
        if CC_BYPASS:
            self.dma("gpsimd", DWF, a["cc_ar_in"], "dma_g")
        else:
            ccv = self.c.bump("cc", 1)
            if self.on("gpsimd"):
                self.E.collective_compute(
                    "AllReduce", Op.add,
                    replica_groups=[[0, 1], [2, 3], [4, 5], [6, 7]],
                    ins=[a["cc_ar_in"]], outs=[a["cc_ar_out"]],
                ).then_inc(self.sems["cc"], 1)
            self.wait("gpsimd", "cc", ccv)
            self.dma("gpsimd", DWF, a["cc_ar_out"], "dma_g")
        self.wait("vector", "dma_g")
        self.wait("scalar", "dma_g")
        self.wait("tensor", "dma_g")
        self.ns_renorm()
        for tile in range(2):
            t0 = tile * 1024
            for h in range(H):
                self.ab_head_ph2(tile, h)
            self.ab_oproj(tile, xres3)

    def build_xln_ab(self, xres3, x3, t0):
        t = self.t
        pp = t["pp"]
        for hf in range(2):
            tt = t0 + hf * 512
            self.ln_bcast(tt, pp[:, 1024:1536], pp[:, 1536:2048])
            for ci in range(NCH):
                self.op("vector", lambda E, ci=ci, tt=tt: E.tensor_tensor(
                    out=t["SC"][:, 0:512], in0=xres3[:, ci, tt:tt + 512],
                    in1=pp[:, 1024:1536], op=Op.subtract))
                self.op("vector", lambda E, ci=ci, hf=hf: E.tensor_tensor(
                    out=x3[:, ci, hf * 512:hf * 512 + 512], in0=t["SC"][:, 0:512],
                    in1=pp[:, 1536:2048], op=Op.mult),
                    sem="v_prog" if ci == NCH - 1 else None)
        self.wait("tensor", "v_prog")

    def lr_tile(self, x3):
        t = self.t
        pp = t["pp"]
        for tc in range(8):
            ps = pp[:, (tc % 4) * 512:(tc % 4) * 512 + 48]
            for k in range(NCH):
                self.op("tensor", lambda E, ps=ps, k=k, tc=tc: E.matmul(
                    ps, x3[:, k, tc * 128:(tc + 1) * 128],
                    t["pool"][:, 19456 + k * 48:19456 + (k + 1) * 48],
                    start=(k == 0), stop=(k == NCH - 1)),
                    sem="t_prog" if k == NCH - 1 else None)
            self.wait("scalar", "t_prog")
            self.wait("scalar", "v_prog", self.c.val("v_prog"))
            lt = t["SC"][:, 5248 + tc * 48:5248 + (tc + 1) * 48]
            self.op("scalar", lambda E, ps=ps: E.activation(
                out=t["SC"][:, 0:48], in_=ps, func=AF.Abs,
                bias=self.K("base_lr")), drain=True)
            self.op("scalar", lambda E: E.activation(
                out=t["SC"][:, 0:48], in_=t["SC"][:, 0:48], func=AF.Exp,
                scale=self.K("neg_one")), drain=True)
            self.op("scalar", lambda E: E.activation(
                out=t["SC"][:, 0:48], in_=t["SC"][:, 0:48], func=AF.Ln,
                bias=self.K("one")), sem="s_prog")
            self.wait("vector", "s_prog")
            self.op("vector", lambda E, ps=ps: E.tensor_scalar(
                out=t["SC"][:, 64:112], in0=ps, scalar1=self.K("base_lr"),
                scalar2=0.0, op0=Op.add, op1=Op.max), drain=True)
            self.op("vector", lambda E, lt=lt: E.tensor_add(
                lt, t["SC"][:, 0:48], t["SC"][:, 64:112]), sem="v_prog")
            self.wait("tensor", "v_prog", self.c.val("v_prog"))

    def ab_head_ph1(self, tile, h, x3):
        t, a = self.t, self.a
        pp = t["pp"]
        sc = t["SC"]
        pool = t["pool"]
        SLAB = pool[:, 0:6144]
        WSM3 = pool[:, 6144:12288].rearrange("p (b j) -> p b j", b=48)
        KTM = pool[:, 12288:13312]
        VTM = pool[:, 13312:14336]
        KFM = pool[:, 14336:15360]
        VFM = pool[:, 15360:16384]
        SCR = [pool[:, 16384 + i * 1024:16384 + (i + 1) * 1024] for i in range(4)]
        DW3 = pool[:, 20480:32768].bitcast(F32).rearrange(
            "p (b j) -> p b j", b=48)
        kvqs = sc[:, 0:3072].rearrange("p (c e) -> p c e", c=8)
        l2v = sc[:, 5120:5152]
        if self.ab_gate is not None:
            self.wait("gpsimd", "t_prog", self.ab_gate)
        ldv = self.dma("gpsimd", SLAB, a["wkvq"][h], "ab_ld")
        self.wait("tensor", "ab_ld", ldv)
        for grp in range(2):
            if grp == 1:
                self.wait("tensor", "s_prog")
            for tc4 in range(4):
                tc = grp * 4 + tc4
                ps = pp[:, tc4 * 512:tc4 * 512 + 384]
                for k in range(NCH):
                    self.op("tensor", lambda E, ps=ps, k=k, tc=tc: E.matmul(
                        ps, x3[:, k, tc * 128:(tc + 1) * 128],
                        SLAB[:, k * 384:(k + 1) * 384],
                        start=(k == 0), stop=(k == NCH - 1)),
                        sem="t_prog" if (k == NCH - 1 and tc4 == 3) else None)
            self.wait("scalar", "t_prog")
            self.wait("scalar", "v_prog", self.c.val("v_prog"))
            self.op("scalar", lambda E, grp=grp: E.activation(
                out=kvqs[:, grp * 4:(grp + 1) * 4, :],
                in_=pp[:, 0:2048].rearrange("p (c e) -> p c e", c=4)
                    [:, :, 0:384], func=AF.Silu), sem="s_prog")
        self.ab_gate = self.c.val("t_prog")
        self.wait("vector", "s_prog")
        self.wait("scalar", "v_prog", self.c.val("v_prog"))
        self.op("scalar", lambda E: E.activation(
            out=sc[:, 3072:4096], in_=kvqs[:, :, 0:128], func=AF.Square))
        self.op("scalar", lambda E: E.activation(
            out=sc[:, 4096:5120], in_=kvqs[:, :, 256:384], func=AF.Square),
            sem="s_prog")
        self.wait("vector", "s_prog")
        self.op("vector", lambda E: E.tensor_reduce(
            out=l2v[:, 0:8], in_=sc[:, 3072:4096].rearrange(
                "p (c e) -> p c e", c=8), axis=AX.X, op=Op.add))
        self.op("vector", lambda E: E.tensor_reduce(
            out=l2v[:, 8:16], in_=sc[:, 4096:5120].rearrange(
                "p (c e) -> p c e", c=8), axis=AX.X, op=Op.add), sem="v_prog")
        self.wait("scalar", "v_prog")
        self.op("scalar", lambda E: E.activation(
            out=l2v[:, 16:32], in_=l2v[:, 0:16], func=AF.Sqrt),
            sem="s_prog")
        self.wait("vector", "s_prog")
        self.op("vector", lambda E: E.tensor_scalar(
            out=l2v[:, 16:32], in0=l2v[:, 16:32],
            scalar1=self.K("eps_l2"), scalar2=None, op0=Op.add), drain=True)
        self.op("vector", lambda E: E.reciprocal(
            out=l2v[:, 16:32], in_=l2v[:, 16:32]), drain=True)
        self.op("vector", lambda E: E.tensor_tensor(
            out=KTM.rearrange("p (c e) -> p c e", c=8), in0=kvqs[:, :, 0:128],
            in1=l2v[:, 16:24].rearrange("p (c o) -> p c o", o=1)
                .broadcast_to((128, 8, 128)), op=Op.mult))
        self.op("vector", lambda E: E.tensor_copy(
            VTM.rearrange("p (c e) -> p c e", c=8), kvqs[:, :, 128:256]))
        self.op("vector", lambda E: E.tensor_tensor(
            out=sc[:, 3072:4096].rearrange("p (c e) -> p c e", c=8),
            in0=kvqs[:, :, 256:384],
            in1=l2v[:, 24:32].rearrange("p (c o) -> p c o", o=1)
                .broadcast_to((128, 8, 128)), op=Op.mult), sem="v_prog")
        self.wait("gpsimd", "v_prog")
        qsv = self.dma("gpsimd", a["qstore"][tile * H + h], sc[:, 3072:4096],
                       "q_st")
        self.wait("tensor", "v_prog")
        for tc in range(8):
            self.op("tensor", lambda E, tc=tc: E.transpose(
                t["ppb"][:, tc * 128:(tc + 1) * 128],
                KTM[:, tc * 128:(tc + 1) * 128], t["ident"][:, :]))
            self.op("tensor", lambda E, tc=tc: E.transpose(
                t["ppb"][:, 1024 + tc * 128:1024 + (tc + 1) * 128],
                VTM[:, tc * 128:(tc + 1) * 128], t["ident"][:, :]),
                sem="t_prog" if tc == 7 else None)
        self.wait("vector", "t_prog")
        self.op("vector", lambda E: E.tensor_copy(
            pool[:, 14336:16384], t["ppb"][:, 0:2048]), sem="v_prog")
        self.wait("tensor", "v_prog", self.c.val("v_prog"))
        for grp in range(2):
            for tc4 in range(4):
                tc = grp * 4 + tc4
                ps = pp[:, tc4 * 512:tc4 * 512 + 512]
                self.op("tensor", lambda E, ps=ps, tc=tc, h=h: E.matmul(
                    ps[:, 0:128], KFM[:, tc * 128:(tc + 1) * 128],
                    WSM3[:, h, :], start=True, stop=True))
                self.op("tensor", lambda E, ps=ps, tc=tc, h=h: E.matmul(
                    ps[:, 128:256], KFM[:, tc * 128:(tc + 1) * 128],
                    WSM3[:, 16 + h, :], start=True, stop=True))
                self.op("tensor", lambda E, ps=ps, tc=tc, h=h: E.matmul(
                    ps[:, 256:384], VFM[:, tc * 128:(tc + 1) * 128],
                    WSM3[:, 32 + h, :], start=True, stop=True),
                    sem="t_prog" if tc4 == 3 else None)
            self.wait("vector", "t_prog")
            pv = pp[:, 0:2048].rearrange("p (c e) -> p c e", c=4)
            for j, base in ((0, 0), (1, 1024), (2, 2048)):
                self.op("vector", lambda E, j=j, base=base, grp=grp: E.tensor_copy(
                    sc[:, base:base + 1024].rearrange("p (c e) -> p c e", c=8)
                        [:, grp * 4:(grp + 1) * 4, :],
                    pv[:, :, j * 128:(j + 1) * 128]),
                    sem="v_prog" if j == 2 else None)
            self.wait("tensor", "v_prog", self.c.val("v_prog"))
        A, Bq, C = sc[:, 0:1024], sc[:, 1024:2048], sc[:, 2048:3072]
        Dq, Eq = sc[:, 3072:4096], sc[:, 4096:5120]
        self.wait("scalar", "v_prog", self.c.val("v_prog"))
        self.wait("scalar", "q_st", qsv)
        self.op("scalar", lambda E: E.activation(out=Dq, in_=A, func=AF.Silu),
                sem="s_prog")
        self.wait("vector", "s_prog")
        self.op("vector", lambda E: E.tensor_mul(Eq, Dq, Bq))
        self.op("vector", lambda E: E.tensor_tensor(
            out=SCR[1].rearrange("p e -> p e"), in0=C, in1=Dq, op=Op.mult))
        self.op("vector", lambda E: E.tensor_mul(C, C, Bq), sem="v_prog")
        self.wait("scalar", "v_prog")
        self.op("scalar", lambda E: E.activation(out=Bq, in_=A, func=AF.Sigmoid),
                sem="s_prog")
        self.wait("vector", "s_prog")
        self.op("vector", lambda E: E.tensor_mul(A, Dq, Bq))
        self.op("vector", lambda E: E.tensor_sub(A, Dq, A))
        self.op("vector", lambda E: E.tensor_add(A, A, Bq))
        self.op("vector", lambda E: E.tensor_tensor(
            out=SCR[0].rearrange("p e -> p e"), in0=C, in1=A, op=Op.mult),
            sem="v_prog")
        lr3 = sc[:, 5248:5632].rearrange("p (c e) -> p c e", c=8)
        self.op("vector", lambda E, h=h: E.tensor_tensor(
            out=SCR[2].rearrange("p (c e) -> p c e", c=8),
            in0=Eq.rearrange("p (c e) -> p c e", c=8),
            in1=lr3[:, :, 3 * h + 1:3 * h + 2].broadcast_to((128, 8, 128)),
            op=Op.mult), sem="v_prog")
        self.wait("tensor", "v_prog")
        self.dw_mm(16 + h, VTM, SCR[2])
        self.op("vector", lambda E, h=h: E.tensor_tensor(
            out=SCR[2].rearrange("p (c e) -> p c e", c=8),
            in0=KTM.rearrange("p (c e) -> p c e", c=8),
            in1=lr3[:, :, 3 * h:3 * h + 1].broadcast_to((128, 8, 128)),
            op=Op.mult), sem="v_prog")
        self.wait("tensor", "v_prog")
        self.dw_mm(h, SCR[0], SCR[2])
        self.op("vector", lambda E, h=h: E.tensor_tensor(
            out=SCR[2].rearrange("p (c e) -> p c e", c=8),
            in0=KTM.rearrange("p (c e) -> p c e", c=8),
            in1=lr3[:, :, 3 * h + 2:3 * h + 3].broadcast_to((128, 8, 128)),
            op=Op.mult), sem="v_prog")
        self.wait("tensor", "v_prog")
        self.dw_mm(32 + h, SCR[1], SCR[2])

    def dw_mm(self, block, lhs_tm, rhs_tm):
        t = self.t
        pp = t["pp"]
        DW3 = t["pool"][:, 20480:32768].bitcast(F32).rearrange(
            "p (b j) -> p b j", b=48)
        bk = self.dw_bank
        self.dw_bank ^= 1
        gate = self.dw_gate[bk]
        if gate is not None:
            self.wait("tensor", "v_prog", gate)
        ps = pp[:, 2048 + bk * 512:2048 + bk * 512 + 128]
        for tc in range(8):
            self.op("tensor", lambda E, ps=ps, tc=tc: E.matmul(
                ps, lhs_tm[:, tc * 128:(tc + 1) * 128],
                rhs_tm[:, tc * 128:(tc + 1) * 128],
                start=(tc == 0), stop=(tc == 7)),
                sem="t_prog" if tc == 7 else None)
        self.wait("vector", "t_prog")
        self.op("vector", lambda E, block=block, ps=ps: E.tensor_tensor(
            out=DW3[:, block, :], in0=DW3[:, block, :], in1=ps, op=Op.add),
            sem="v_prog")
        self.dw_gate[bk] = self.c.val("v_prog")

    def transpose48(self, src3, dst3):
        t = self.t
        for grp in range(3):
            self.wait("tensor", "v_prog", self.c.val("v_prog"))
            for b16 in range(16):
                b = grp * 16 + b16
                self.op("tensor", lambda E, b=b, b16=b16: E.transpose(
                    t["ppb"][:, b16 * 128:(b16 + 1) * 128], src3[:, b, :],
                    t["ident"][:, :]), sem="t_prog" if b16 == 15 else None)
            self.wait("vector", "t_prog")
            self.op("vector", lambda E, grp=grp: E.tensor_copy(
                dst3[:, grp * 16:(grp + 1) * 16, :],
                t["ppb"][:, 0:2048].rearrange("p (c e) -> p c e", c=16)),
                sem="v_prog")

    def mm48(self, lhs3, rhs3, out3, post=None):
        t = self.t
        pp = t["pp"]
        for grp in range(12):
            self.wait("tensor", "v_prog", self.c.val("v_prog"))
            for b4 in range(4):
                b = grp * 4 + b4
                self.op("tensor", lambda E, b=b, b4=b4: E.matmul(
                    pp[:, b4 * 512:b4 * 512 + 128], lhs3[:, b, :],
                    rhs3[:, b, :], start=True, stop=True),
                    sem="t_prog" if b4 == 3 else None)
            self.wait("vector", "t_prog")
            pv = pp[:, 0:2048].rearrange("p (c e) -> p c e", c=4)[:, :, 0:128]
            if post is None:
                self.op("vector", lambda E, grp=grp: E.tensor_copy(
                    out3[:, grp * 4:(grp + 1) * 4, :], pv), sem="v_prog")
            else:
                post(grp, pv)

    def ns_renorm(self):
        t, a = self.t, self.a
        pp = t["pp"]
        sc = t["SC"]
        pool = t["pool"]
        XB3 = pool[:, 0:6144].rearrange("p (b j) -> p b j", b=48)
        XTB3 = pool[:, 6144:12288].rearrange("p (b j) -> p b j", b=48)
        AB3 = pool[:, 12288:18432].rearrange("p (b j) -> p b j", b=48)
        DWF = pool[:, 20480:32768].bitcast(F32)
        DW3 = DWF.rearrange("p (b j) -> p b j", b=48)
        ns48 = t["xln"][:, 0:320].bitcast(F32)
        self.phase_barrier()
        ldv = self.dma("gpsimd", sc[:, :], a["w012"], "dma_g")
        self.op("scalar", lambda E: E.activation(
            out=pool[:, 0:12288].bitcast(F32), in_=DWF, func=AF.Square),
            sem="s_prog")
        self.wait("vector", "s_prog")
        self.op("vector", lambda E: E.tensor_reduce(
            out=ns48[:, 0:48],
            in_=pool[:, 0:12288].bitcast(F32).rearrange("p (b j) -> p b j", b=48),
            axis=AX.X, op=Op.add), sem="v_prog")
        self.row48_recip_bcast(ns48[:, 0:48], "eps_ns", pp, 512)
        self.op("vector", lambda E: E.tensor_tensor(
            out=XB3, in0=DW3,
            in1=pp[:, 512:560].rearrange("p (b o) -> p b o", o=1)
                .broadcast_to((128, 48, 128)), op=Op.mult), sem="v_prog")
        self.transpose48(XB3, XTB3)
        for it, (ca, cb, cc_) in enumerate(NS_COEFS):
            self.mm48(XTB3, XTB3, AB3)
            for grp in range(12):
                self.wait("tensor", "v_prog", self.c.val("v_prog"))
                for b4 in range(4):
                    b = grp * 4 + b4
                    self.op("tensor", lambda E, b=b, b4=b4: E.matmul(
                        pp[:, b4 * 512:b4 * 512 + 128], AB3[:, b, :],
                        AB3[:, b, :], start=True, stop=True),
                        sem="t_prog" if b4 == 3 else None)
                self.wait("vector", "t_prog")
                pv = pp[:, 0:2048].rearrange("p (c e) -> p c e", c=4)[:, :, 0:128]
                tmp3 = pool[:, 18944:19456].rearrange("p (c e) -> p c e", c=4)
                self.op("vector", lambda E, pv=pv, cc_=cc_, tmp3=tmp3: E.tensor_scalar(
                    out=tmp3, in0=pv, scalar1=cc_, scalar2=None, op0=Op.mult))
                self.op("vector", lambda E, tmp3=tmp3, cb=cb, grp=grp: E.scalar_tensor_tensor(
                    out=AB3[:, grp * 4:(grp + 1) * 4, :],
                    in0=AB3[:, grp * 4:(grp + 1) * 4, :], scalar=cb, in1=tmp3,
                    op0=Op.mult, op1=Op.add), sem="v_prog")
            for grp in range(12):
                self.wait("tensor", "v_prog", self.c.val("v_prog"))
                for b4 in range(4):
                    b = grp * 4 + b4
                    self.op("tensor", lambda E, b=b, b4=b4: E.matmul(
                        pp[:, b4 * 512:b4 * 512 + 128], AB3[:, b, :],
                        XB3[:, b, :], start=True, stop=True))
                    self.op("tensor", lambda E, b=b, b4=b4: E.matmul(
                        pp[:, b4 * 512 + 128:b4 * 512 + 256], XB3[:, b, :],
                        AB3[:, b, :], start=True, stop=True),
                        sem="t_prog" if b4 == 3 else None)
                self.wait("vector", "t_prog")
                pv = pp[:, 0:2048].rearrange("p (c e) -> p c e", c=4)
                self.op("vector", lambda E, pv=pv, ca=ca, grp=grp: E.scalar_tensor_tensor(
                    out=XB3[:, grp * 4:(grp + 1) * 4, :],
                    in0=XB3[:, grp * 4:(grp + 1) * 4, :], scalar=ca,
                    in1=pv[:, :, 0:128], op0=Op.mult, op1=Op.add))
                self.op("vector", lambda E, pv=pv, ca=ca, grp=grp: E.scalar_tensor_tensor(
                    out=XTB3[:, grp * 4:(grp + 1) * 4, :],
                    in0=XTB3[:, grp * 4:(grp + 1) * 4, :], scalar=ca,
                    in1=pv[:, :, 128:256], op0=Op.mult, op1=Op.add),
                    sem="v_prog")
        self.wait("vector", "dma_g", ldv)
        self.op("vector", lambda E: E.tensor_tensor(
            out=DWF, in0=sc[:, :], in1=pool[:, 0:6144], op=Op.add),
            sem="v_prog")
        FS = pool[:, 12288:18432].bitcast(F32)
        FS3 = FS.rearrange("p (b j) -> p b j", b=24)
        self.wait("scalar", "v_prog")
        for hf in range(2):
            self.op("scalar", lambda E, hf=hf: E.activation(
                out=FS, in_=sc[:, hf * 3072:(hf + 1) * 3072], func=AF.Square),
                sem="s_prog")
            self.wait("vector", "s_prog")
            self.op("vector", lambda E, hf=hf: E.tensor_reduce(
                out=ns48[:, 48 + hf * 24:48 + (hf + 1) * 24], in_=FS3,
                axis=AX.X, op=Op.add), sem="v_prog")
            self.wait("scalar", "v_prog")
        for hf in range(2):
            self.op("scalar", lambda E, hf=hf: E.activation(
                out=FS, in_=DWF[:, hf * 3072:(hf + 1) * 3072], func=AF.Square),
                sem="s_prog")
            self.wait("vector", "s_prog")
            self.op("vector", lambda E, hf=hf: E.tensor_reduce(
                out=ns48[:, hf * 24:(hf + 1) * 24], in_=FS3,
                axis=AX.X, op=Op.add), sem="v_prog")
            self.wait("scalar", "v_prog")
        self.op("scalar", lambda E: E.activation(
            out=ns48[:, 96:144], in_=ns48[:, 48:96], func=AF.Sqrt), drain=True)
        self.op("scalar", lambda E: E.activation(
            out=ns48[:, 0:48], in_=ns48[:, 0:48], func=AF.Sqrt),
            sem="s_prog")
        self.wait("vector", "s_prog")
        self.op("vector", lambda E: E.tensor_scalar(
            out=ns48[:, 0:48], in0=ns48[:, 0:48],
            scalar1=self.K("eps_l2"), scalar2=None, op0=Op.add), drain=True)
        self.op("vector", lambda E: E.reciprocal(
            out=ns48[:, 0:48], in_=ns48[:, 0:48]), drain=True)
        self.op("vector", lambda E: E.tensor_mul(
            ns48[:, 96:144], ns48[:, 96:144], ns48[:, 0:48]), drain=True)
        self.op("vector", lambda E: E.tensor_tensor(
            out=XB3, in0=DW3,
            in1=ns48[:, 96:144].rearrange("p (b o) -> p b o", o=1)
                .broadcast_to((128, 48, 128)), op=Op.mult), sem="v_prog")
        self.transpose48(XB3, XTB3)

    def row48_recip_bcast(self, src48, epsname, pp, col0):
        t = self.t
        pool = self.t["pool"]
        stg = pool[:, 18432:18480]
        self.wait("vector", "v_prog", self.c.val("v_prog"))
        self.op("vector", lambda E, stg=stg, src48=src48: E.tensor_copy(
            stg, src48), sem="v_prog")
        self.wait("tensor", "v_prog")
        self.op("tensor", lambda E, stg=stg: E.matmul(
            pp[0:1, 0:48], t["ones_bf"][:, 0:1], stg,
            start=True, stop=True), sem="t_prog")
        self.wait("scalar", "t_prog")
        row = t["redtmp"][0:1, 96:144]
        self.op("scalar", lambda E, row=row: E.activation(
            out=row, in_=pp[0:1, 0:48], func=AF.Sqrt), drain=True)
        self.op("scalar", lambda E, row=row, epsname=epsname: E.activation(
            out=row, in_=row, func=AF.Ln, bias=self.K1(epsname)), drain=True)
        self.op("scalar", lambda E, row=row: E.activation(
            out=row, in_=row, func=AF.Exp, scale=self.K1("neg_one")),
            sem="s_prog")
        self.wait("vector", "s_prog")
        self.op("vector", lambda E, row=row: E.tensor_copy(
            t["stats"][0:1, 0:48], row), sem="v_prog")
        self.wait("tensor", "v_prog")
        self.op("tensor", lambda E, col0=col0: E.matmul(
            pp[:, col0:col0 + 48], t["ones_row_bf"][0:1, :],
            t["stats"][0:1, 0:48], start=True, stop=True), sem="t_prog")
        self.wait("vector", "t_prog")

    def ab_head_ph2(self, tile, h):
        t, a = self.t, self.a
        pp = t["pp"]
        sc = t["SC"]
        pool = t["pool"]
        WFIN3 = pool[:, 0:6144].rearrange("p (b j) -> p b j", b=48)
        WFINT3 = pool[:, 6144:12288].rearrange("p (b j) -> p b j", b=48)
        Q = pool[:, 12288:13312]
        QFM = pool[:, 13312:14336]
        OG = pool[:, 14336:15360]
        OGFM = pool[:, 15360:16384]
        OALL = pool[:, 16384:32768]
        self.wait("gpsimd", "t_prog")
        self.wait("gpsimd", "v_prog")
        ldv = self.dma("gpsimd", Q, a["qstore"][tile * H + h], "ab_ld")
        self.wait("tensor", "ab_ld", ldv)
        self.wait("tensor", "v_prog")
        for tc in range(8):
            self.op("tensor", lambda E, tc=tc: E.transpose(
                t["ppb"][:, tc * 128:(tc + 1) * 128],
                Q[:, tc * 128:(tc + 1) * 128], t["ident"][:, :]),
                sem="t_prog" if tc == 7 else None)
        self.wait("vector", "t_prog")
        self.op("vector", lambda E: E.tensor_copy(QFM, t["ppb"][:, 0:1024]),
                sem="v_prog")
        self.wait("tensor", "v_prog", self.c.val("v_prog"))
        for grp in range(2):
            for tc4 in range(4):
                tc = grp * 4 + tc4
                ps = pp[:, tc4 * 512:tc4 * 512 + 512]
                self.op("tensor", lambda E, ps=ps, tc=tc, h=h: E.matmul(
                    ps[:, 0:128], QFM[:, tc * 128:(tc + 1) * 128],
                    WFINT3[:, h, :], start=True, stop=True))
                self.op("tensor", lambda E, ps=ps, tc=tc, h=h: E.matmul(
                    ps[:, 128:256], QFM[:, tc * 128:(tc + 1) * 128],
                    WFINT3[:, 32 + h, :], start=True, stop=True),
                    sem="t_prog" if tc4 == 3 else None)
            self.wait("scalar", "t_prog")
            self.wait("scalar", "v_prog", self.c.val("v_prog"))
            pv = pp[:, 0:2048].rearrange("p (c e) -> p c e", c=4)
            self.op("scalar", lambda E, pv=pv, grp=grp: E.activation(
                out=sc[:, 0:1024].rearrange("p (c e) -> p c e", c=8)
                    [:, grp * 4:(grp + 1) * 4, :],
                in_=pv[:, :, 0:128], func=AF.Silu), sem="s_prog")
            self.wait("vector", "s_prog")
            self.op("vector", lambda E, pv=pv, grp=grp: E.tensor_copy(
                sc[:, 1024:2048].rearrange("p (c e) -> p c e", c=8)
                    [:, grp * 4:(grp + 1) * 4, :], pv[:, :, 128:256]),
                sem="v_prog")
            self.wait("tensor", "v_prog", self.c.val("v_prog"))
        self.op("vector", lambda E: E.tensor_tensor(
            out=OG, in0=sc[:, 0:1024], in1=sc[:, 1024:2048], op=Op.mult),
            sem="v_prog")
        self.wait("tensor", "v_prog")
        for tc in range(8):
            self.op("tensor", lambda E, tc=tc: E.transpose(
                t["ppb"][:, tc * 128:(tc + 1) * 128],
                OG[:, tc * 128:(tc + 1) * 128], t["ident"][:, :]),
                sem="t_prog" if tc == 7 else None)
        self.wait("vector", "t_prog")
        self.op("vector", lambda E: E.tensor_copy(OGFM, t["ppb"][:, 0:1024]),
                sem="v_prog")
        self.wait("tensor", "v_prog", self.c.val("v_prog"))
        for n2 in range(2):
            self.op("tensor", lambda E, n2=n2, h=h: E.matmul(
                pp[:, n2 * 512:(n2 + 1) * 512], WFINT3[:, 16 + h, :],
                OGFM[:, n2 * 512:(n2 + 1) * 512], start=True, stop=True),
                sem="t_prog" if n2 == 1 else None)
        self.wait("scalar", "t_prog")
        self.wait("scalar", "v_prog", self.c.val("v_prog"))
        self.op("scalar", lambda E: E.activation(
            out=pool[:, 12288:13312], in_=pp[:, 0:1024], func=AF.Square),
            sem="s_prog")
        self.wait("tensor", "s_prog")
        self.op("tensor", lambda E: E.matmul(
            pp[0:1, 1024:1536], t["ones_bf"][:, 0:1], pool[:, 12288:12800],
            start=True, stop=True))
        self.op("tensor", lambda E: E.matmul(
            pp[0:1, 1536:2048], t["ones_bf"][:, 0:1], pool[:, 12800:13312],
            start=True, stop=True), sem="t_prog")
        self.wait("scalar", "t_prog")
        row = sc[0:1, 4096:5120]
        self.op("scalar", lambda E, row=row: E.activation(
            out=row[0:1, 0:512], in_=pp[0:1, 1024:1536], func=AF.Ln,
            scale=self.K1("inv_hd"), bias=self.K1("eps_ln")), drain=True)
        self.op("scalar", lambda E, row=row: E.activation(
            out=row[0:1, 512:1024], in_=pp[0:1, 1536:2048], func=AF.Ln,
            scale=self.K1("inv_hd"), bias=self.K1("eps_ln")), drain=True)
        self.op("scalar", lambda E, row=row: E.activation(
            out=row, in_=row, func=AF.Exp, scale=self.K1("neg_half")),
            sem="s_prog")
        self.wait("vector", "s_prog")
        self.op("vector", lambda E, row=row: E.tensor_copy(
            t["stats"][0:1, 0:1024], row), sem="v_prog")
        self.wait("tensor", "v_prog")
        self.op("tensor", lambda E: E.matmul(
            pp[:, 1024:1536], t["ones_row_bf"][0:1, :],
            t["stats"][0:1, 0:512], start=True, stop=True))
        self.op("tensor", lambda E: E.matmul(
            pp[:, 1536:2048], t["ones_row_bf"][0:1, :],
            t["stats"][0:1, 512:1024], start=True, stop=True),
            sem="t_prog")
        self.wait("vector", "t_prog")
        self.op("vector", lambda E: E.tensor_copy(sc[:, 0:1024], pp[:, 0:1024]),
                sem="v_prog")
        self.op("vector", lambda E, h=h: E.tensor_tensor(
            out=OALL[:, h * 1024:(h + 1) * 1024], in0=sc[:, 0:1024],
            in1=pp[:, 1024:2048], op=Op.mult), sem="v_prog")
        self.wait("tensor", "v_prog", self.c.val("v_prog"))

    def ab_oproj(self, tile, xres3):
        t, a = self.t, self.a
        pp = t["pp"]
        pool = t["pool"]
        OALL = pool[:, 16384:32768]
        wps = [pool[:, 12288 + i * 2048:12288 + (i + 1) * 2048] for i in range(2)]
        t0 = tile * 1024
        self.wait("gpsimd", "t_prog")
        self.wait("gpsimd", "v_prog")
        self.pp01_gate[0] = ("v_prog", self.c.val("v_prog"))
        self.pp01_gate[1] = ("v_prog", self.c.val("v_prog"))
        for mo in range(16):
            g = self.c.bump("op_seq", 1) - 1
            buf = wps[g % 2]
            if g - 2 in self.op_gate:
                self.wait("gpsimd", "t_prog", self.op_gate[g - 2])
            ldv = self.dma("gpsimd", buf, a["woproj"][mo], "op_ld")
            self.wait("vector", "op_ld", ldv)
            self.op("vector", lambda E, buf=buf: E.tensor_scalar(
                out=buf, in0=buf, scalar1=t["gamma"][:, 0:1], scalar2=None,
                op0=Op.mult), sem="v_prog")
            self.wait("tensor", "v_prog", self.c.val("v_prog"))
            for n2 in range(2):
                bk = n2
                gate = self.pp01_gate[bk]
                if gate is not None:
                    self.wait("tensor", gate[0], gate[1])
                ps = pp[:, bk * 512:bk * 512 + 512]
                for k in range(NCH):
                    self.op("tensor", lambda E, ps=ps, k=k, n2=n2, buf=buf: E.matmul(
                        ps, buf[:, k * 128:(k + 1) * 128],
                        OALL[:, k * 1024 + n2 * 512:k * 1024 + n2 * 512 + 512],
                        start=(k == 0), stop=(k == NCH - 1)),
                        sem="t_prog" if k == NCH - 1 else None)
                self.wait("vector", "t_prog")
                self.op("vector", lambda E, mo=mo, t0=t0, n2=n2, ps=ps: E.tensor_tensor(
                    out=xres3[:, mo, t0 + n2 * 512:t0 + n2 * 512 + 512],
                    in0=xres3[:, mo, t0 + n2 * 512:t0 + n2 * 512 + 512],
                    in1=ps, op=Op.add), sem="v_prog")
                self.pp01_gate[bk] = ("v_prog", self.c.val("v_prog"))
            self.op_gate[g] = self.c.val("t_prog")

    def ph_store(self):
        t, a = self.t, self.a
        self.wait("gpsimd", "v_prog")
        self.dma("gpsimd", a["out"], t["xres"][:, :], "dma_st")
        self.wait("sync", "dma_st")


def build_nc():
    nc = bass.Bass("TRN2", target_bir_lowering=False, debug=False,
                   detect_race_conditions=False)
    a = {}
    a["x"] = nc.dram_tensor("x", [128, NCH, T], F32, kind="ExternalInput").ap()
    a["prev"] = nc.dram_tensor("prev", [128, 16], F32, kind="ExternalInput").ap()
    a["consts"] = nc.dram_tensor("consts", [128, len(CONSTS)], F32,
                                 kind="ExternalInput").ap()
    a["flags"] = nc.dram_tensor("flags", [128, 2], F32, kind="ExternalInput").ap()
    a["offs8"] = nc.dram_tensor("offs8", [128, 8], F32, kind="ExternalInput").ap()
    a["gamma"] = nc.dram_tensor("gamma", [128, 1], F32, kind="ExternalInput").ap()
    a["ones"] = nc.dram_tensor("ones", [128, 2], F32, kind="ExternalInput").ap()
    a["ones_row"] = nc.dram_tensor("ones_row", [1, 128], F32,
                                   kind="ExternalInput").ap()
    wact = nc.dram_tensor("wact", [16, 128, 32 * 128], F32, kind="ExternalInput")
    wrec = nc.dram_tensor("wrec", [16, 128, 32 * 128], F32, kind="ExternalInput")
    wval = nc.dram_tensor("wval", [16, 128, 16 * 128], F32, kind="ExternalInput")
    a["wact"] = [wact.ap()[m] for m in range(16)]
    a["wrec"] = [wrec.ap()[m] for m in range(16)]
    a["wval"] = [wval.ap()[m] for m in range(16)]
    wfc = nc.dram_tensor("wfc", [40, 128, 2048], F32, kind="ExternalInput")
    wproj = nc.dram_tensor("wproj", [16, 128, 5120], F32, kind="ExternalInput")
    wfc_t = nc.dram_tensor("wfc_t", [40, 128, 2048], BF16)
    wproj_t = nc.dram_tensor("wproj_t", [16, 128, 5120], BF16)
    a["wfc_t_sl"] = [wfc_t.ap()[m] for m in range(40)]
    a["wproj_t_sl"] = [wproj_t.ap()[m] for m in range(16)]
    tz0 = []
    for m in range(40):
        for o in range(0, 2048, 1024):
            tz0.append((wfc.ap()[m][:, o:o + 1024], wfc_t.ap()[m][:, o:o + 1024], 1024))
    tz1 = []
    for m in range(16):
        for o in range(0, 5120, 1024):
            tz1.append((wproj.ap()[m][:, o:o + 1024], wproj_t.ap()[m][:, o:o + 1024], 1024))
    a["tz_src"] = [tz0, tz1]
    a["tz_nel"] = [40 * 128 * 2048, 16 * 128 * 5120]
    wkvq = nc.dram_tensor("wkvq", [16, 128, 6144], F32, kind="ExternalInput")
    a["wkvq"] = [wkvq.ap()[hh] for hh in range(16)]
    a["wsmall"] = nc.dram_tensor("wsmall", [128, 6144], F32,
                                 kind="ExternalInput").ap()
    a["w012"] = nc.dram_tensor("w012", [128, 6144], F32,
                               kind="ExternalInput").ap()
    woproj = nc.dram_tensor("woproj", [16, 128, 2048], F32,
                            kind="ExternalInput")
    a["woproj"] = [woproj.ap()[mm] for mm in range(16)]
    a["lrproj"] = nc.dram_tensor("lrproj", [128, 768], F32,
                                 kind="ExternalInput").ap()
    a["ident"] = nc.dram_tensor("ident", [128, 128], F32,
                                kind="ExternalInput").ap()
    qstore = nc.dram_tensor("qstore", [32, 128, 1024], BF16)
    a["qstore"] = [qstore.ap()[i] for i in range(32)]
    a["cc_ar_in"] = nc.dram_tensor("cc_ar_in", [128, 6144], F32).ap()
    a["cc_ar_out"] = nc.dram_tensor("cc_ar_out", [128, 6144], F32).ap()
    a["out"] = nc.dram_tensor("out", [128, NCH * T], F32,
                              kind="ExternalOutput").ap()
    a["cc_ag_in"] = nc.dram_tensor("cc_ag_in", [128, 24], F32).ap()
    ago = nc.dram_tensor("cc_ag_out", [256, 24], F32)
    a["cc_ag_out"] = ago.ap()
    a["cc_ag_out_r0"] = ago.ap()[0:128, :]

    stack = contextlib.ExitStack()
    t = {}

    def sb(name, shape, dt=F32):
        t[name] = stack.enter_context(nc.sbuf_tensor("sb_" + name, shape, dt))
        return t[name]

    sb("consts", [128, len(CONSTS)])
    sb("flags", [128, 2])
    sb("offs8", [128, 8])
    sb("gamma", [128, 1])
    sb("prev", [128, 16])
    sb("prev_ln", [128, 32], BF16)
    sb("prev_bf", [128, 16], BF16)
    sb("prevsq_bf", [128, 16], BF16)
    sb("prevstat_bf", [1, 2], BF16)
    sb("redtmp", [1, 160])
    sb("bc4", [128, 4])
    sb("ones_bf", [128, 2], BF16)
    sb("ones_f32", [128, 2])
    sb("ones_row", [1, 128])
    sb("ones_row_bf", [1, 128], BF16)
    sb("xres", [128, NCH * T], BF16)
    sb("pool", [128, NCH * T], BF16)
    sb("xln", [128, NCH * XSTR], BF16)
    sb("xln_last", [128, 32], BF16)
    sb("SC", [128, 6144])
    sb("stats", [1, 2 * T], BF16)
    sb("summ", [128, 24])
    sb("peers", [128, 24])
    sb("carry", [128, 24])
    t["wsl"] = [sb("wsl0", [128, 2048], BF16), sb("wsl1", [128, 2048], BF16),
                sb("wsl2", [128, 2048], BF16)]
    sb("ident", [128, 128], BF16)
    sb("tzsl", [128, 192])
    sb("bc6", [128, 4])

    pstack = contextlib.ExitStack()
    t["pp"] = pstack.enter_context(nc.psum_tensor("pp", [128, 3072], F32))
    t["ppb"] = pstack.enter_context(nc.psum_tensor("ppb", [128, 2048], BF16))

    sems = {}
    semstack = contextlib.ExitStack()
    for s in ["boot", "dma_s", "dma_g", "dma_st", "v_prog", "s_prog", "t_prog",
              "t_bc", "g_prog", "cc", "lnx2", "w_ld", "w_fr", "wr_ld", "wr_fr",
              "wv_ld", "wv_fr", "tz_ld", "tz_st", "tz_fr", "f_ld", "f_fr",
              "fp_ld", "fp_fr", "ab_ld", "q_st", "op_ld", "tz_s", "tz_v"]:
        sems[s] = semstack.enter_context(nc.semaphore(s))

    kb = KB(nc, sems, t, a)
    with nc.Block() as block:
        @block.sync
        def _(E):
            kb.emit("sync", E)

        @block.vector
        def _(E):
            kb.emit("vector", E)

        @block.scalar
        def _(E):
            kb.emit("scalar", E)

        @block.tensor
        def _(E):
            kb.emit("tensor", E)

        @block.gpsimd
        def _(E):
            kb.emit("gpsimd", E)

    stack.close()
    pstack.close()
    semstack.close()
    return nc


def make_in_maps(inputs):
    x = np.asarray(inputs["L0_attn_a_x"], dtype=np.float32)
    B_, S, C = x.shape
    Th = S // 2

    consts = np.zeros((128, len(CONSTS)), dtype=np.float32)
    for k, v in CONSTS.items():
        consts[:, CIDX[k]] = v
    ones = np.zeros((128, 2), dtype=np.float32)
    ones[:, 0] = 1.0
    ones_row = np.ones((1, 128), dtype=np.float32)

    W_act = np.asarray(inputs["L0_attn_a_W_act"], dtype=np.float32)
    perm = np.concatenate([np.arange(0, C, 2), np.arange(1, C, 2)])
    wact_sl = np.ascontiguousarray(
        np.ascontiguousarray(W_act[perm].T).reshape(32, 128, 16, 128)
        .transpose(2, 1, 0, 3)).reshape(16, 128, 32 * 128)
    W_rec = np.asarray(inputs["L0_attn_a_W_rec"], dtype=np.float32)
    wrec_sl = np.ascontiguousarray(
        np.ascontiguousarray(W_rec.T).reshape(32, 128, 16, 128)
        .transpose(2, 1, 0, 3)).reshape(16, 128, 32 * 128)
    W_val = np.asarray(inputs["L0_attn_a_W_val"], dtype=np.float32)
    wval_sl = np.ascontiguousarray(
        np.ascontiguousarray(W_val[:, perm].T).reshape(16, 128, 16, 128)
        .transpose(2, 1, 0, 3)).reshape(16, 128, 16 * 128)

    W_fc = np.asarray(inputs["L0_ffn_w_fc"], dtype=np.float32)
    wfc_sl = np.ascontiguousarray(
        np.ascontiguousarray(W_fc.T).reshape(16, 128, 40, 128)
        .transpose(2, 1, 0, 3)).reshape(40, 128, 2048)
    W_proj = np.asarray(inputs["L0_ffn_w_proj"], dtype=np.float32)
    wproj_sl = np.ascontiguousarray(
        np.ascontiguousarray(W_proj.T).reshape(40, 128, 16, 128)
        .transpose(2, 1, 0, 3)).reshape(16, 128, 5120)
    gamma = np.zeros((128, 1), dtype=np.float32)
    gamma[:, 0] = np.asarray(inputs["L0_attn_b_o_norm_w"], dtype=np.float32)

    tq = np.asarray(inputs["L0_attn_b_to_qkv_w"], dtype=np.float32)
    kvq_sl = np.empty((16, 128, 6144), np.float32)
    for hh in range(16):
        cat = np.concatenate([tq[2048 + hh * 128:2048 + (hh + 1) * 128],
                              tq[4096 + hh * 128:4096 + (hh + 1) * 128],
                              tq[hh * 128:(hh + 1) * 128]], axis=0)
        kvq_sl[hh] = cat.reshape(384, 16, 128).transpose(2, 1, 0).reshape(128, 6144)
    w0 = np.asarray(inputs["L0_attn_b_w0"], np.float32)
    w1 = np.asarray(inputs["L0_attn_b_w1"], np.float32)
    w2 = np.asarray(inputs["L0_attn_b_w2"], np.float32)
    w012 = np.ascontiguousarray(
        np.concatenate([w0, w1, w2], axis=0).transpose(1, 0, 2)).reshape(128, 6144)
    wsmall = np.ascontiguousarray(
        np.concatenate([w0.transpose(0, 2, 1), w2.transpose(0, 2, 1), w1],
                       axis=0).transpose(1, 0, 2)).reshape(128, 6144)
    op_ = np.asarray(inputs["L0_attn_b_o_proj_w"], np.float32)
    wop = np.ascontiguousarray(
        op_.reshape(16, 128, 16, 128).transpose(0, 3, 2, 1)).reshape(16, 128, 2048)
    lrp = np.asarray(inputs["L0_attn_b_lr_proj_w"], np.float32)
    lrh = np.ascontiguousarray(
        lrp.T.reshape(16, 128, 48).transpose(1, 0, 2)).reshape(128, 768)
    identm = np.eye(128, dtype=np.float32)

    in_maps = []
    for core in range(8):
        b_, half = core // 2, core % 2
        xs = np.ascontiguousarray(x[b_, half * Th:(half + 1) * Th].T)
        x_l = np.ascontiguousarray(xs.reshape(16, 128, Th).transpose(1, 0, 2))
        prev = (np.zeros(C, dtype=np.float32) if half == 0
                else np.ascontiguousarray(x[b_, Th - 1]))
        prev_l = np.ascontiguousarray(prev.reshape(16, 128).T)
        flags = np.zeros((128, 2), dtype=np.float32)
        flags[:, 0] = float(half)
        offs8 = np.full((128, 8), (float(half) - 1.0) * 1e30, dtype=np.float32)
        in_maps.append({
            "x": x_l, "prev": prev_l, "consts": consts, "flags": flags,
            "offs8": offs8, "gamma": gamma, "ones": ones, "ones_row": ones_row,
            "wact": wact_sl, "wrec": wrec_sl, "wval": wval_sl,
            "wfc": wfc_sl, "wproj": wproj_sl,
            "wkvq": kvq_sl, "wsmall": wsmall, "w012": w012,
            "woproj": wop, "lrproj": lrh, "ident": identm,
        })
    return in_maps


def kernel(**inputs):
    x = np.asarray(inputs["L0_attn_a_x"], dtype=np.float32)
    B_, S, C = x.shape
    Th = S // 2
    in_maps = make_in_maps(inputs)
    nc = build_nc()
    res = run_bass_kernel_spmd(nc, in_maps, core_ids=list(range(8)))
    out = np.empty((B_, S, C), dtype=np.float32)
    for core in range(8):
        b_, half = core // 2, core % 2
        o = res.results[core]["out"].reshape(128, 16, Th)
        out[b_, half * Th:(half + 1) * Th] = (
            o.transpose(1, 0, 2).reshape(C, Th).T)
    return out


# version 13
# speedup vs baseline: 1.0330x; 1.0330x over previous
import contextlib
import numpy as np
import concourse.bass as bass
import concourse.mybir as mybir
from concourse.bass_utils import run_bass_kernel_spmd
from concourse.alu_op_type import AluOpType as Op

F32 = mybir.dt.float32
BF16 = mybir.dt.bfloat16
AF = mybir.ActivationFunctionType
AX = mybir.AxisListType

D = 2048
T = 2048
NCH = 16
H = 16
HD = 128
FF = 5120
XSTR = 1026
PI = float(np.pi)
NS_COEFS = [(4.0848, -6.8946, 2.927), (3.9505, -6.3029, 2.6377),
            (3.7418, -5.5913, 2.3037), (2.8769, -3.1427, 1.2046),
            (2.8366, -3.0525, 1.2012)]
BASE_LR_INV = float(0.01 + np.log(-np.expm1(-0.01)))
CC_BYPASS = False

CONSTS = {
    "half_pi": PI / 2, "half": 0.5, "neg_half": -0.5,
    "inv2pi": 1.0 / (2 * PI), "neg2pi": -2 * PI, "magic": 12582912.0,
    "quarter": 0.25, "eps_ln": 1e-5, "eps_l2": 1e-5, "eps_ns": 1e-7,
    "base_lr": BASE_LR_INV, "p7": 0.7,
    "quarter_pi": PI / 4, "t_quarter_pi": T * PI / 4,
    "neg_3half_pi": -1.5 * PI, "pi": PI, "neg_pi": -PI, "neg_one": -1.0,
    "one": 1.0, "inv_hd": 1.0 / HD,
}
CIDX = {k: i for i, k in enumerate(CONSTS)}


class Counters:
    def __init__(self):
        self.v = {}

    def bump(self, sem, n):
        self.v[sem] = self.v.get(sem, 0) + n
        return self.v[sem]

    def val(self, sem):
        return self.v.get(sem, 0)


class KB:

    def __init__(self, nc, sems, t, a):
        self.nc = nc
        self.sems = sems
        self.t = t
        self.a = a

    def on(self, eng):
        return self.eng == eng

    def wait(self, eng, sem, val=None):
        if self.on(eng):
            self.E.wait_ge(self.sems[sem], self.c.val(sem) if val is None else val)

    def dma(self, queue, out, in_, sem):
        v = self.c.bump(sem, 16)
        if self.on(queue):
            self.E.dma_start(out=out, in_=in_).then_inc(self.sems[sem], 16)
        return v

    def op(self, eng, fn, sem=None, drain=False):
        if sem is not None:
            self.c.bump(sem, 1)
        if self.on(eng):
            ins = fn(self.E)
            if sem is not None:
                if eng == "sync":
                    ins.then_inc(self.sems[sem], 1)
                else:
                    self.E.drain(fusable=True).then_inc(self.sems[sem], 1)
            elif (drain or not self.no_drain) and eng in ("vector", "scalar"):
                self.E.drain(fusable=True)

    def K(self, name):
        return self.t["consts"][:, CIDX[name]:CIDX[name] + 1]

    def K1(self, name):
        return self.t["consts"][0:1, CIDX[name]:CIDX[name] + 1]

    def emit(self, eng, E):
        self.eng = eng
        self.E = E
        self.c = Counters()
        self.ab_gate = None
        self.op_gate = {}
        self.pp01_gate = {0: None, 1: None}
        self.dw_gate = {0: None, 1: None}
        self.dw_bank = 0
        self.no_drain = True
        self.tz_pieces = self.a["tz_src"][0] + self.a["tz_src"][1]
        self.tz_p1 = 0
        self.tz_p2 = 0
        self.tz_pending = None
        self.ph_load()
        self.ph_attn_a()
        self.wait("sync", "tz_st")
        self.ph_ffn()
        self.ph_attn_b()
        self.ph_ffn()
        self.ph_store()

    def ph_load(self):
        t, a = self.t, self.a
        self.dma("sync", t["consts"][:, :], a["consts"], "dma_s")
        self.dma("sync", t["flags"][:, :], a["flags"], "dma_s")
        self.dma("sync", t["offs8"][:, :], a["offs8"], "dma_s")
        self.dma("sync", t["prev"][:, :], a["prev"], "dma_s")
        self.dma("sync", t["gamma"][:, :], a["gamma"], "dma_s")
        self.dma("sync", t["ones_f32"][:, :], a["ones"], "dma_s")
        self.dma("sync", t["ones_row"][:, :], a["ones_row"], "dma_s")
        self.dma("gpsimd", t["ones_bf"][:, :], a["ones"], "dma_g")
        self.dma("gpsimd", t["ones_row_bf"][:, :], a["ones_row"], "dma_g")
        self.dma("gpsimd", t["xres"][:, :], a["x"], "dma_g")
        for e in ("vector", "scalar", "tensor", "gpsimd", "sync"):
            self.wait(e, "dma_s")
            self.wait(e, "dma_g")
        self.op("vector", lambda E: E.memset(self.t["xln"][:, :], 0.0),
                sem="v_prog")

    def ln_stats(self, src3, x2buf):
        for t0 in (0, 1024):
            self.ln_stats_half(src3, x2buf, t0)

    def ln_stats_half(self, src3, x2buf, t0):
        t = self.t
        pp = t["pp"]
        HT = 1024
        for ci in range(NCH):
            xb = x2buf[ci % 2]
            if ci >= 2:
                self.wait("scalar", "lnx2", self.c.val("lnx2") - 1)
            self.wait("scalar", "v_prog", self.c.val("v_prog"))
            self.op("scalar", lambda E, ci=ci, xb=xb: E.activation(
                out=xb[:, 0:HT], in_=src3[:, ci, t0:t0 + HT], func=AF.Square),
                sem="s_prog")
            self.wait("tensor", "s_prog")
            for n in range(2):
                self.op("tensor", lambda E, ci=ci, n=n: E.matmul(
                    pp[0:1, n * 512:(n + 1) * 512], t["ones_bf"][:, 0:1],
                    src3[:, ci, t0 + n * 512:t0 + (n + 1) * 512],
                    start=(ci == 0), stop=(ci == NCH - 1)))
                self.op("tensor", lambda E, ci=ci, n=n, xb=xb: E.matmul(
                    pp[0:1, 1024 + n * 512:1024 + (n + 1) * 512],
                    t["ones_bf"][:, 0:1], xb[:, n * 512:(n + 1) * 512],
                    start=(ci == 0), stop=(ci == NCH - 1)),
                    sem="lnx2" if n == 1 else None)
        self.wait("vector", "lnx2")
        sc = t["SC"]
        self.op("vector", lambda E: E.tensor_scalar(
            out=sc[0:1, 0:HT], in0=pp[0:1, 0:HT], scalar1=1.0 / D, scalar2=None,
            op0=Op.mult))
        self.op("vector", lambda E: E.tensor_scalar(
            out=sc[0:1, 1024:1024 + HT], in0=pp[0:1, 1024:1024 + HT],
            scalar1=1.0 / D, scalar2=None, op0=Op.mult), sem="v_prog")
        self.wait("scalar", "v_prog")
        self.op("scalar", lambda E: E.activation(
            out=sc[0:1, 2048:2048 + HT], in_=sc[0:1, 0:HT], func=AF.Square),
            sem="s_prog")
        self.wait("vector", "s_prog")
        self.op("vector", lambda E: E.tensor_sub(
            sc[0:1, 1024:1024 + HT], sc[0:1, 1024:1024 + HT],
            sc[0:1, 2048:2048 + HT]), sem="v_prog")
        self.wait("scalar", "v_prog")
        self.op("scalar", lambda E: E.activation(
            out=sc[0:1, 1024:1024 + HT], in_=sc[0:1, 1024:1024 + HT], func=AF.Ln,
            bias=self.K1("eps_ln")))
        self.op("scalar", lambda E: E.activation(
            out=sc[0:1, 1024:1024 + HT], in_=sc[0:1, 1024:1024 + HT], func=AF.Exp,
            scale=self.K1("neg_half")), sem="s_prog")
        self.wait("vector", "s_prog")
        self.op("vector", lambda E: E.tensor_copy(
            t["stats"][0:1, t0:t0 + HT], sc[0:1, 0:HT]))
        self.op("vector", lambda E: E.tensor_copy(
            t["stats"][0:1, T + t0:T + t0 + HT], sc[0:1, 1024:1024 + HT]),
            sem="v_prog")
        self.wait("tensor", "v_prog")

    def ln_bcast(self, t0, ps_mean, ps_rstd):
        t = self.t
        self.wait("tensor", "v_prog")
        self.op("tensor", lambda E: E.matmul(
            ps_mean, t["ones_row_bf"][0:1, :], t["stats"][0:1, t0:t0 + 512],
            start=True, stop=True))
        self.op("tensor", lambda E: E.matmul(
            ps_rstd, t["ones_row_bf"][0:1, :],
            t["stats"][0:1, T + t0:T + t0 + 512], start=True, stop=True),
            sem="t_bc")
        self.wait("vector", "t_bc")

    def build_xln_tile(self, src3, nt, first_ln):
        t = self.t
        pp = t["pp"]
        x3 = t["xln"].rearrange("p (c s) -> p c s", c=NCH)
        for hf in range(2):
            t0 = nt * 1024 + hf * 512
            pm, pr = pp[:, 1024:1536], pp[:, 1536:2048]
            self.ln_bcast(t0, pm, pr)
            for ci in range(NCH):
                o = 1 + hf * 512
                self.op("vector", lambda E, ci=ci, o=o, t0=t0: E.tensor_tensor(
                    out=t["SC"][:, 0:512], in0=src3[:, ci, t0:t0 + 512],
                    in1=pm, op=Op.subtract))
                self.op("vector", lambda E, ci=ci, o=o: E.tensor_tensor(
                    out=x3[:, ci, o:o + 512], in0=t["SC"][:, 0:512],
                    in1=pr, op=Op.mult),
                    sem="v_prog" if ci == NCH - 1 else None)
        self.op("vector", lambda E: E.tensor_copy(
            t["xln_last"].rearrange("p (c o) -> p c o", o=2)[:, :, 0:1],
            x3[:, :, 1024:1025]), sem="v_prog")

    def prev_ln(self):
        t = self.t
        pp = self.t["pp"]
        sc = t["SC"]
        saved_nd = self.no_drain
        self.no_drain = False
        self.wait("vector", "s_prog", self.c.val("s_prog"))
        self.op("vector", lambda E: E.tensor_copy(t["prev_bf"][:, :],
                                                  t["prev"][:, :]), sem="v_prog")
        self.wait("scalar", "v_prog")
        self.op("scalar", lambda E: E.activation(
            out=t["prevsq_bf"][:, :], in_=t["prev"][:, :], func=AF.Square),
            sem="s_prog")
        self.wait("tensor", "s_prog")
        self.wait("tensor", "v_prog")
        self.op("tensor", lambda E: E.matmul(
            pp[0:1, 0:16], t["ones_bf"][:, 0:1], t["prev_bf"][:, :],
            start=True, stop=True))
        self.op("tensor", lambda E: E.matmul(
            pp[0:1, 512:528], t["ones_bf"][:, 0:1], t["prevsq_bf"][:, :],
            start=True, stop=True), sem="t_prog")
        self.wait("vector", "t_prog")
        self.op("vector", lambda E: E.tensor_scalar(
            out=t["redtmp"][0:1, 8:24], in0=pp[0:1, 0:16], scalar1=1.0,
            scalar2=0.0, op0=Op.mult, op1=Op.add,
            accum_out=t["redtmp"][0:1, 0:1]))
        self.op("vector", lambda E: E.tensor_copy(
            t["prev_bf"][:, :], t["prev"][:, :]))
        self.op("vector", lambda E: E.tensor_scalar(
            out=t["redtmp"][0:1, 24:40], in0=pp[0:1, 512:528], scalar1=1.0,
            scalar2=0.0, op0=Op.mult, op1=Op.add,
            accum_out=t["redtmp"][0:1, 1:2]))
        self.op("vector", lambda E: E.tensor_scalar(
            out=t["redtmp"][0:1, 2:4], in0=t["redtmp"][0:1, 0:2],
            scalar1=1.0 / D, scalar2=None, op0=Op.mult), sem="v_prog")
        self.wait("scalar", "v_prog")
        self.op("scalar", lambda E: E.activation(
            out=t["redtmp"][0:1, 4:5], in_=t["redtmp"][0:1, 2:3],
            func=AF.Square), sem="s_prog")
        self.wait("vector", "s_prog")
        self.op("vector", lambda E: E.tensor_sub(
            t["redtmp"][0:1, 3:4], t["redtmp"][0:1, 3:4],
            t["redtmp"][0:1, 4:5]), sem="v_prog")
        self.wait("scalar", "v_prog")
        self.op("scalar", lambda E: E.activation(
            out=t["redtmp"][0:1, 3:4], in_=t["redtmp"][0:1, 3:4], func=AF.Ln,
            bias=self.K1("eps_ln")), drain=True)
        self.op("scalar", lambda E: E.activation(
            out=t["redtmp"][0:1, 3:4], in_=t["redtmp"][0:1, 3:4], func=AF.Exp,
            scale=self.K1("neg_half")), sem="s_prog")
        self.wait("vector", "s_prog")
        self.op("vector", lambda E: E.tensor_copy(
            t["prevstat_bf"][0:1, :], t["redtmp"][0:1, 2:4]), sem="v_prog")
        self.wait("tensor", "v_prog")
        self.op("tensor", lambda E: E.matmul(
            pp[:, 1024:1026], t["ones_row_bf"][0:1, :], t["prevstat_bf"][0:1, :],
            start=True, stop=True), sem="t_prog")
        self.wait("vector", "t_prog")
        self.op("vector", lambda E: E.tensor_scalar(
            out=t["prev_ln"].rearrange("p (c o) -> p c o", o=2)[:, :, 0:1]
                .rearrange("p c o -> p (c o)"),
            in0=t["prev"][:, :],
            scalar1=pp[:, 1024:1025], scalar2=pp[:, 1025:1026],
            op0=Op.subtract, op1=Op.mult), sem="v_prog")
        self.no_drain = saved_nd

    def ph_attn_a(self):
        t, a = self.t, self.a
        xres3 = t["xres"].rearrange("p (c t) -> p c t", c=NCH)
        xv3 = t["pool"].rearrange("p (c t) -> p c t", c=NCH)
        x2pair = (t["pool"].rearrange("p (c t) -> p c t", c=NCH)[:, 0, :],
                  t["pool"].rearrange("p (c t) -> p c t", c=NCH)[:, 1, :])
        self.ln_stats(xres3, x2pair)
        self.prev_ln()
        self.tz_p1_gate_v = self.c.val("v_prog")
        for nt in range(2):
            self.build_xln_tile(xres3, nt, first_ln=True)
            self.wact_sweep(nt, xv3)
        self.tz_pump_p1(len(self.tz_pieces))
        self.wait("scalar", "tz_fr")
        self.tz_am()
        self.scan_phase(xv3)
        self.tz_p2_gate_v = self.c.val("v_prog")
        self.tz_p2_gate_s = self.c.val("s_prog")
        for nt in range(2):
            self.build_xln_tile(xres3, nt, first_ln=False)
            self.attn_a_epilogue(nt, xres3, xv3)
        self.tz_pump_p2(len(self.tz_pieces))
        self.tz_flush_p2()

    def wact_sweep(self, nt, xv3):
        t, a = self.t, self.a
        pp = t["pp"]
        x3 = t["xln"].rearrange("p (c s) -> p c s", c=NCH)
        self.wait("tensor", "v_prog")
        if nt == 0:
            self.pp01_gate[0] = ("v_prog", self.c.val("v_prog"))
            self.pp01_gate[1] = ("v_prog", self.c.val("v_prog"))
        for m in range(16):
            for h2 in range(2):
                g = (nt * 16 + m) * 2 + h2
                buf = t["wsl"][g % 3]
                if self.on("gpsimd") and g >= 3:
                    self.E.wait_ge(self.sems["w_fr"], g - 2)
                ldv = self.dma("gpsimd", buf[:, :],
                               a["wact"][m][:, h2 * 2048:(h2 + 1) * 2048], "w_ld")
                if h2 == 0:
                    ld0 = ldv
                else:
                    ld1 = ldv
            for n2 in range(2):
                gate = self.pp01_gate[n2]
                if gate is not None:
                    self.wait("tensor", gate[0], gate[1])
                ps = pp[:, n2 * 512:(n2 + 1) * 512]
                bnd = t["prev_ln"] if nt == 0 else t["xln_last"]
                for h2 in range(2):
                    g = (nt * 16 + m) * 2 + h2
                    buf = t["wsl"][g % 3]
                    self.wait("tensor", "w_ld", ld0 if h2 == 0 else ld1)
                    for kk in range(16):
                        k = h2 * 16 + kk
                        ci = k % 16
                        off = 0 if k < 16 else 1
                        rhs = x3[:, ci, off + n2 * 512:off + n2 * 512 + 512]
                        last = (k == 31 and n2 == 1)
                        self.op("tensor",
                                lambda E, ps=ps, kk=kk, rhs=rhs, buf=buf,
                                k=k, n2=n2: E.matmul(
                                    ps, buf[:, kk * 128:(kk + 1) * 128], rhs,
                                    start=(k == 0), stop=(k == 31 and n2 == 1)),
                                sem="t_prog" if last else None)
                if n2 == 0:
                    g0 = (nt * 16 + m) * 2
                    buf0 = t["wsl"][g0 % 3]
                    for kk in range(16):
                        self.op("tensor",
                                lambda E, ps=ps, kk=kk, buf0=buf0, bnd=bnd:
                                E.matmul(
                                    ps[:, 0:1],
                                    buf0[:, kk * 128:(kk + 1) * 128],
                                    bnd[:, 2 * kk:2 * kk + 1],
                                    start=False, stop=(kk == 15)),
                                sem="t_prog" if kk == 15 else None)
                self.wait("scalar", "t_prog")
                self.wait("scalar", "v_prog", self.c.val("v_prog"))
                self.op("scalar", lambda E, ps=ps: E.activation(
                    out=t["SC"][:, 2048:2560], in_=ps, func=AF.Sigmoid),
                    sem="s_prog")
                self.pp01_gate[n2] = ("s_prog", self.c.val("s_prog"))
                self.wait("vector", "s_prog")
                self.op("vector", lambda E, m=m, nt=nt, n2=n2: E.tensor_scalar(
                    out=xv3[:, m, nt * 1024 + n2 * 512:nt * 1024 + n2 * 512 + 512],
                    in0=t["SC"][:, 2048:2560], scalar1=0.5, scalar2=None,
                    op0=Op.max), sem="v_prog")
            self.op("tensor", lambda E: E.nop(), sem="w_fr")
            self.op("tensor", lambda E: E.nop(), sem="w_fr")
            self.tz_pump_p1(5)

    def chunk_s_theta(self, xv3, c8):
        t = self.t
        sc = t["SC"]
        sc1, sc2, sc3 = sc[:, 0:T], sc[:, 2048:2048 + T], sc[:, 4096:4096 + T]
        self.wait("scalar", "v_prog", self.c.val("v_prog"))
        self.op("scalar", lambda E, c8=c8: E.activation(
            out=sc1, in_=xv3[:, c8, :], func=AF.Square))
        self.op("scalar", lambda E, c8=c8: E.activation(
            out=sc2, in_=xv3[:, 8 + c8, :], func=AF.Square), sem="s_prog")
        self.wait("vector", "s_prog")
        self.op("vector", lambda E: E.tensor_add(sc1, sc1, sc2))
        self.op("vector", lambda E, c8=c8: E.tensor_sub(
            sc2, xv3[:, 8 + c8, :], xv3[:, c8, :]))
        self.op("vector", lambda E, c8=c8: E.tensor_add(
            sc3, xv3[:, 8 + c8, :], xv3[:, c8, :]), sem="v_prog")
        self.wait("scalar", "v_prog")
        self.op("scalar", lambda E: E.activation(out=sc1, in_=sc1, func=AF.Ln))
        self.op("scalar", lambda E: E.activation(out=sc3, in_=sc3, func=AF.Ln))
        self.op("scalar", lambda E: E.activation(
            out=sc3, in_=sc3, func=AF.Exp, scale=self.K("neg_one")), sem="s_prog")
        self.wait("vector", "s_prog")
        self.op("vector", lambda E: E.tensor_mul(sc2, sc2, sc3), sem="v_prog")
        self.wait("scalar", "v_prog")
        self.op("scalar", lambda E: E.activation(
            out=sc2, in_=sc2, func=AF.Arctan), sem="s_prog")
        self.wait("vector", "s_prog")

    def scan_phase(self, xv3):
        t, a = self.t, self.a
        sc = t["SC"]
        sc1, sc2, sc3 = sc[:, 0:T], sc[:, 2048:2048 + T], sc[:, 4096:4096 + T]
        for c8 in range(8):
            self.chunk_s_theta(xv3, c8)
            self.op("vector", lambda E, c8=c8: E.tensor_reduce(
                out=t["summ"][:, 16 + c8:17 + c8], in_=sc2, axis=AX.X, op=Op.add))
            self.op("vector", lambda E, c8=c8: E.tensor_scalar(
                out=t["summ"][:, 16 + c8:17 + c8],
                in0=t["summ"][:, 16 + c8:17 + c8],
                scalar1=self.K("t_quarter_pi"), scalar2=None, op0=Op.add))
            self.op("vector", lambda E: E.tensor_tensor_scan(
                out=sc3, data0=sc1, data1=sc1, initial=-1e30, op0=Op.max,
                op1=Op.bypass))
            self.op("vector", lambda E, c8=c8: E.tensor_copy(
                t["summ"][:, c8:c8 + 1], sc3[:, T - 1:T]))
            self.op("vector", lambda E, c8=c8: E.scalar_tensor_tensor(
                out=sc2, in0=sc1, scalar=1.0, in1=sc3, op0=Op.mult,
                op1=Op.subtract, accum_out=t["summ"][:, 8 + c8:9 + c8]),
                sem="v_prog")
        self.wait("gpsimd", "v_prog")
        self.dma("gpsimd", a["cc_ag_in"], t["summ"][:, 0:24], "dma_g")
        self.wait("gpsimd", "dma_g")
        if CC_BYPASS:
            self.dma("gpsimd", t["peers"][:, 0:24], a["cc_ag_in"], "dma_g")
        else:
            ccv = self.c.bump("cc", 1)
            if self.on("gpsimd"):
                self.E.collective_compute(
                    "AllGather", Op.bypass,
                    replica_groups=[[0, 1], [2, 3], [4, 5], [6, 7]],
                    ins=[a["cc_ag_in"]], outs=[a["cc_ag_out"]],
                ).then_inc(self.sems["cc"], 1)
            self.wait("gpsimd", "cc", ccv)
            self.dma("gpsimd", t["peers"][:, 0:24], a["cc_ag_out_r0"], "dma_g")
        self.wait("vector", "dma_g")
        self.op("vector", lambda E: E.scalar_tensor_tensor(
            out=t["carry"][:, 0:8], in0=t["peers"][:, 0:8],
            scalar=t["flags"][:, 0:1], in1=t["offs8"][:, 0:8],
            op0=Op.mult, op1=Op.add))
        self.op("vector", lambda E: E.tensor_scalar(
            out=t["carry"][:, 8:24], in0=t["peers"][:, 8:24],
            scalar1=t["flags"][:, 0:1], scalar2=None, op0=Op.mult), sem="v_prog")
        for c8 in range(8):
            self.chunk_s_theta(xv3, c8)
            self.op("vector", lambda E: E.tensor_tensor_scan(
                out=sc3, data0=sc1, data1=sc1, initial=-1e30, op0=Op.max,
                op1=Op.bypass))
            self.op("vector", lambda E, c8=c8: E.tensor_scalar(
                out=sc3, in0=sc3, scalar1=t["carry"][:, c8:c8 + 1],
                scalar2=None, op0=Op.max))
            self.op("vector", lambda E: E.tensor_sub(sc1, sc1, sc3))
            self.op("vector", lambda E, c8=c8: E.tensor_tensor_scan(
                out=sc1, data0=sc1, data1=sc1,
                initial=t["carry"][:, 8 + c8:9 + c8], op0=Op.add, op1=Op.bypass))
            self.op("vector", lambda E, c8=c8: E.tensor_tensor_scan(
                out=sc2, data0=sc2,
                data1=self.K("quarter_pi").broadcast_to((128, T)),
                initial=t["carry"][:, 16 + c8:17 + c8], op0=Op.add,
                op1=Op.add), sem="v_prog")
            self.wait("scalar", "v_prog")
            self.op("scalar", lambda E: E.activation(
                out=sc1, in_=sc1, func=AF.Exp, scale=self.K("half")),
                sem="s_prog")
            self.op("vector", lambda E: E.tensor_scalar(
                out=sc3, in0=sc2, scalar1=self.K("inv2pi"),
                scalar2=self.K("magic"), op0=Op.mult, op1=Op.add))
            self.op("vector", lambda E: E.tensor_scalar(
                out=sc3, in0=sc3, scalar1=self.K("magic"), scalar2=None,
                op0=Op.subtract))
            self.op("vector", lambda E: E.scalar_tensor_tensor(
                out=sc3, in0=sc3, scalar=self.K("neg2pi"), in1=sc2,
                op0=Op.mult, op1=Op.add))
            self.op("vector", lambda E: E.tensor_scalar(
                out=sc3, in0=sc3, scalar1=self.K("pi"), scalar2=self.K("neg_pi"),
                op0=Op.min, op1=Op.max), sem="v_prog")
            self.wait("scalar", "v_prog")
            self.op("scalar", lambda E: E.activation(
                out=sc3, in_=sc3, func=AF.Sin), sem="s_prog")
            self.wait("vector", "s_prog")
            self.op("vector", lambda E, c8=c8: E.tensor_tensor(
                out=xv3[:, 8 + c8, :], in0=sc1, in1=sc3, op=Op.mult),
                sem="v_prog")
            self.op("vector", lambda E: E.tensor_scalar(
                out=sc3, in0=sc2, scalar1=self.K("inv2pi"),
                scalar2=self.K("quarter"), op0=Op.mult, op1=Op.add))
            self.op("vector", lambda E: E.tensor_scalar(
                out=sc3, in0=sc3, scalar1=self.K("magic"),
                scalar2=self.K("magic"), op0=Op.add, op1=Op.subtract))
            self.op("vector", lambda E: E.scalar_tensor_tensor(
                out=sc3, in0=sc3, scalar=self.K("neg2pi"), in1=sc2,
                op0=Op.mult, op1=Op.add))
            self.op("vector", lambda E: E.tensor_scalar(
                out=sc3, in0=sc3, scalar1=self.K("half_pi"),
                scalar2=self.K("neg_3half_pi"), op0=Op.min, op1=Op.max),
                sem="v_prog")
            self.wait("scalar", "v_prog")
            self.op("scalar", lambda E: E.activation(
                out=sc3, in_=sc3, func=AF.Sin, bias=self.K("half_pi")),
                sem="s_prog")
            self.wait("vector", "s_prog")
            self.op("vector", lambda E, c8=c8: E.tensor_tensor(
                out=xv3[:, c8, :], in0=sc1, in1=sc3, op=Op.mult), sem="v_prog")

    def attn_a_epilogue(self, nt, xres3, xv3):
        t, a = self.t, self.a
        pp = t["pp"]
        x3 = t["xln"].rearrange("p (c s) -> p c s", c=NCH)
        wv = t["SC"][:, :].bitcast(BF16)[:, 8192:8192 + 2048]
        t0 = nt * 1024
        self.wait("gpsimd", "v_prog")
        self.wait("tensor", "v_prog")
        for m in range(16):
            for h2 in range(2):
                g = (nt * 16 + m) * 2 + h2
                buf = t["wsl"][g % 3]
                if self.on("gpsimd") and g >= 3:
                    self.E.wait_ge(self.sems["wr_fr"], g - 2)
                ldv = self.dma("gpsimd", buf[:, :],
                               a["wrec"][m][:, h2 * 2048:(h2 + 1) * 2048], "wr_ld")
                if h2 == 0:
                    ld0 = ldv
                else:
                    ld1 = ldv
            gv = nt * 16 + m
            if self.on("gpsimd") and gv >= 1:
                self.E.wait_ge(self.sems["wv_fr"], gv)
            ldv2 = self.dma("gpsimd", wv, a["wval"][m], "wv_ld")
            for n2 in range(2):
                gate = self.pp01_gate[0]
                if gate is not None:
                    self.wait("tensor", gate[0], gate[1])
                ps_r = pp[:, 0:512]
                ps_o = pp[:, 512:1024]
                bnd = t["prev_ln"] if nt == 0 else t["xln_last"]
                for h2 in range(2):
                    g = (nt * 16 + m) * 2 + h2
                    buf = t["wsl"][g % 3]
                    self.wait("tensor", "wr_ld", ld0 if h2 == 0 else ld1)
                    for kk in range(16):
                        k = h2 * 16 + kk
                        ci = k % 16
                        off = 0 if k < 16 else 1
                        rhs = x3[:, ci, off + n2 * 512:off + n2 * 512 + 512]
                        last = (k == 31 and n2 == 1)
                        self.op("tensor",
                                lambda E, ps_r=ps_r, kk=kk, rhs=rhs, buf=buf,
                                k=k, n2=n2: E.matmul(
                                    ps_r, buf[:, kk * 128:(kk + 1) * 128], rhs,
                                    start=(k == 0), stop=(k == 31 and n2 == 1)),
                                sem="t_prog" if last else None)
                if n2 == 0:
                    g0 = (nt * 16 + m) * 2
                    buf0 = t["wsl"][g0 % 3]
                    for kk in range(16):
                        self.op("tensor",
                                lambda E, ps_r=ps_r, kk=kk, buf0=buf0, bnd=bnd:
                                E.matmul(
                                    ps_r[:, 0:1],
                                    buf0[:, kk * 128:(kk + 1) * 128],
                                    bnd[:, 2 * kk:2 * kk + 1],
                                    start=False, stop=(kk == 15)),
                                sem="t_prog" if kk == 15 else None)
                self.wait("tensor", "wv_ld", ldv2)
                gate = self.pp01_gate[1]
                if gate is not None:
                    self.wait("tensor", gate[0], gate[1])
                for k in range(16):
                    rhs = xv3[:, k, t0 + n2 * 512:t0 + n2 * 512 + 512]
                    self.op("tensor",
                            lambda E, ps_o=ps_o, k=k, rhs=rhs: E.matmul(
                                ps_o, wv[:, k * 128:(k + 1) * 128], rhs,
                                start=(k == 0), stop=(k == 15)),
                            sem="t_prog" if k == 15 else None)
                self.wait("scalar", "t_prog")
                self.wait("scalar", "v_prog", self.c.val("v_prog"))
                self.op("scalar", lambda E, ps_r=ps_r: E.activation(
                    out=t["SC"][:, 0:512], in_=ps_r, func=AF.Sigmoid),
                    sem="s_prog")
                self.pp01_gate[0] = ("s_prog", self.c.val("s_prog"))
                self.wait("vector", "s_prog")
                self.op("vector", lambda E: E.tensor_scalar(
                    out=t["SC"][:, 0:512], in0=t["SC"][:, 0:512], scalar1=0.5,
                    scalar2=None, op0=Op.max))
                self.op("vector", lambda E: E.tensor_mul(
                    t["SC"][:, 0:512], t["SC"][:, 0:512], t["SC"][:, 0:512]))
                self.op("vector", lambda E: E.tensor_mul(
                    t["SC"][:, 0:512], t["SC"][:, 0:512], t["SC"][:, 0:512]))
                self.op("vector", lambda E, ps_o=ps_o: E.tensor_tensor(
                    out=t["SC"][:, 2048:2560], in0=ps_o, in1=t["SC"][:, 0:512],
                    op=Op.mult))
                self.op("vector", lambda E, m=m, t0=t0, n2=n2: E.tensor_tensor(
                    out=xres3[:, m, t0 + n2 * 512:t0 + n2 * 512 + 512],
                    in0=xres3[:, m, t0 + n2 * 512:t0 + n2 * 512 + 512],
                    in1=t["SC"][:, 2048:2560], op=Op.add), sem="v_prog")
                self.pp01_gate[1] = ("v_prog", self.c.val("v_prog"))
            self.op("tensor", lambda E: E.nop(), sem="wr_fr")
            self.op("tensor", lambda E: E.nop(), sem="wr_fr")
            self.op("tensor", lambda E: E.nop(), sem="wv_fr")
            self.tz_pump_p2(5)

    def tz_pump_p1(self, n):
        t, a = self.t, self.a
        sc = t["SC"]
        SG = (sc[:, 512:1536], sc[:, 2560:3584])
        for _ in range(n):
            i = self.tz_p1
            if i >= len(self.tz_pieces):
                return
            self.tz_p1 += 1
            src_ap, _dst, w = self.tz_pieces[i]
            if i < 2:
                self.wait("sync", "v_prog", self.tz_p1_gate_v)
            else:
                self.wait("sync", "tz_fr", i - 1)
            ldv = self.dma("sync", SG[i % 2], src_ap, "tz_ld")
            self.wait("vector", "tz_ld", ldv)
            self.op("vector", lambda E, i=i, b=SG[i % 2]: E.tensor_reduce(
                out=t["tzsl"][:, i:i + 1], in_=b, axis=AX.X, op=Op.add,
                apply_absolute_value=True), sem="tz_fr")

    def tz_am(self):
        t, a = self.t, self.a
        pp = t["pp"]
        n0 = len(a["tz_src"][0])
        spans = [(0, n0), (n0, len(self.tz_pieces))]
        stg = t["SC"][:, 2560:2656].bitcast(BF16)
        self.wait("vector", "tz_fr")
        npc = len(self.tz_pieces)
        self.op("vector", lambda E, npc=npc: E.tensor_copy(
            stg[:, 0:npc], t["tzsl"][:, 0:npc]), sem="v_prog")
        self.wait("tensor", "v_prog")
        for wi, (lo, hi) in enumerate(spans):
            self.op("tensor", lambda E, wi=wi, lo=lo, hi=hi: E.matmul(
                pp[0:1, 2048 + wi * 512:2048 + wi * 512 + (hi - lo)],
                t["ones_bf"][:, 0:1], stg[:, lo:hi], start=True, stop=True),
                sem="t_prog" if wi == 1 else None)
        self.wait("vector", "t_prog")
        for wi, (lo, hi) in enumerate(spans):
            nel = float(a["tz_nel"][wi])
            self.op("vector", lambda E, wi=wi, lo=lo, hi=hi: E.tensor_scalar(
                out=t["SC"][0:1, 512:512 + (hi - lo)],
                in0=pp[0:1, 2048 + wi * 512:2048 + wi * 512 + (hi - lo)],
                scalar1=1.0, scalar2=0.0, op0=Op.mult, op1=Op.add,
                accum_out=t["redtmp"][0:1, 5 + wi:6 + wi]), drain=True)
            self.op("vector", lambda E: E.memset(
                t["SC"][0:1, 1024:1536], 0.0))
            self.op("vector", lambda E, wi=wi, nel=nel: E.tensor_scalar(
                out=t["redtmp"][0:1, 7 + wi:8 + wi],
                in0=t["redtmp"][0:1, 5 + wi:6 + wi],
                scalar1=1.0 / nel, scalar2=1e-5, op0=Op.mult, op1=Op.max),
                drain=True)
            self.op("vector", lambda E: E.memset(
                t["SC"][0:1, 1024:1536], 0.0))
            self.op("vector", lambda E, wi=wi: E.tensor_scalar(
                out=t["redtmp"][0:1, 44 + 3 * wi:45 + 3 * wi],
                in0=t["redtmp"][0:1, 7 + wi:8 + wi],
                scalar1=-0.7, scalar2=None, op0=Op.mult), drain=True)
            self.op("vector", lambda E, wi=wi: E.tensor_scalar(
                out=t["redtmp"][0:1, 45 + 3 * wi:46 + 3 * wi],
                in0=t["redtmp"][0:1, 7 + wi:8 + wi],
                scalar1=0.7, scalar2=None, op0=Op.mult), drain=True)
            self.op("vector", lambda E, wi=wi: E.tensor_scalar(
                out=t["redtmp"][0:1, 46 + 3 * wi:47 + 3 * wi],
                in0=t["redtmp"][0:1, 7 + wi:8 + wi],
                scalar1=1.0, scalar2=None, op0=Op.mult),
                sem="v_prog" if wi == 1 else None, drain=True)
        self.wait("tensor", "v_prog")
        self.op("tensor", lambda E: E.matmul(
            pp[:, 1024:1030], t["ones_row"][0:1, :],
            t["redtmp"][0:1, 44:50], start=True, stop=True), sem="t_prog")
        self.wait("vector", "t_prog")
        self.op("vector", lambda E: E.tensor_copy(
            t["bc4"][:, 0:3], pp[:, 1024:1027]))
        self.op("vector", lambda E: E.tensor_copy(
            t["bc6"][:, 0:3], pp[:, 1027:1030]), sem="v_prog")
        self.tz_am_t = self.c.val("t_prog")

    def tz_pump_p2(self, n):
        t, a = self.t, self.a
        sc = t["SC"]
        SG = (sc[:, 512:1536], sc[:, 2560:3584])
        n0 = len(a["tz_src"][0])
        for _ in range(n):
            i = self.tz_p2
            if i >= len(self.tz_pieces):
                return
            self.tz_p2 += 1
            src_ap, dst_ap, w = self.tz_pieces[i]
            bcs = t["bc4"] if i < n0 else t["bc6"]
            if i < 2:
                self.wait("sync", "v_prog", self.tz_p2_gate_v)
                self.wait("sync", "s_prog", self.tz_p2_gate_s)
                self.wait("sync", "t_prog", self.tz_am_t)
            else:
                self.wait("sync", "tz_st", (i - 1) * 16)
            ldv = self.dma("sync", SG[i % 2], src_ap, "tz_ld")
            if self.tz_pending is not None:
                pi, pdst, pres = self.tz_pending
                self.wait("sync", "tz_v", pi + 1)
                self.dma("sync", pdst, pres, "tz_st")
                self.tz_pending = None
            self.wait("vector", "tz_ld", ldv)
            sb0 = sc[:, 3584:4096].bitcast(BF16)
            sb1 = sc[:, 1536:2048].bitcast(BF16)
            self.op("vector", lambda E, b=SG[i % 2], sb0=sb0, bcs=bcs:
                    E.tensor_scalar(out=sb0, in0=b, scalar1=bcs[:, 1:2],
                                    scalar2=bcs[:, 2:3], op0=Op.is_gt,
                                    op1=Op.mult))
            self.op("vector", lambda E, b=SG[i % 2], sb1=sb1, bcs=bcs:
                    E.tensor_scalar(out=sb1, in0=b, scalar1=bcs[:, 0:1],
                                    scalar2=bcs[:, 2:3], op0=Op.is_lt,
                                    op1=Op.mult))
            res = SG[i % 2].bitcast(BF16)[:, 0:1024]
            self.op("vector", lambda E, res=res, sb0=sb0, sb1=sb1:
                    E.tensor_sub(res, sb0, sb1), sem="tz_v")
            self.tz_pending = (i, dst_ap, res)

    def tz_flush_p2(self):
        if self.tz_pending is not None:
            pi, pdst, pres = self.tz_pending
            self.wait("sync", "tz_v", pi + 1)
            self.dma("sync", pdst, pres, "tz_st")
            self.tz_pending = None

    def phase_barrier(self):
        for e in ("vector", "scalar", "tensor", "gpsimd", "sync"):
            for s in ("v_prog", "s_prog", "t_prog"):
                self.wait(e, s)

    def ph_ternarize(self):
        t, a = self.t, self.a
        pp = t["pp"]
        sc = t["SC"]
        self.phase_barrier()
        for wi in range(2):
            pieces = a["tz_src"][wi]
            nel = float(a["tz_nel"][wi])
            for s_i, (src_ap, _dst, w) in enumerate(pieces):
                third = s_i % 3
                scv = sc[:, third * 2048:third * 2048 + w]
                if s_i >= 3:
                    self.wait("sync", "tz_fr", self.c.val("tz_fr") - 2)
                ldv = self.dma("sync", scv, src_ap, "tz_ld")
                self.wait("scalar", "tz_ld", ldv)
                self.wait("scalar", "t_prog", self.c.val("t_prog"))
                self.op("scalar", lambda E, scv=scv, w=w: E.activation(
                    out=t["wsl"][0][:, 0:w], in_=scv, func=AF.Abs), sem="s_prog")
                self.op("scalar", lambda E: E.nop(), sem="tz_fr")
                self.wait("tensor", "s_prog")
                for n in range(w // 512):
                    self.op("tensor", lambda E, n=n: E.matmul(
                        pp[0:1, (n % 4) * 512:(n % 4) * 512 + 512],
                        t["ones_bf"][:, 0:1],
                        t["wsl"][0][:, n * 512:(n + 1) * 512],
                        start=True, stop=True),
                        sem="t_prog" if n == w // 512 - 1 else None)
                self.wait("vector", "t_prog")
                self.op("vector", lambda E, w=w, s_i=s_i: E.tensor_scalar(
                    out=t["pool"][0:1, 0:2 * w].bitcast(F32),
                    in0=pp[0:1, 0:w], scalar1=1.0, scalar2=0.0,
                    op0=Op.mult, op1=Op.add,
                    accum_out=t["redtmp"][0:1, 50 + s_i:51 + s_i]), sem="v_prog")
                self.wait("tensor", "v_prog", self.c.val("v_prog"))
            npc = len(pieces)
            self.op("vector", lambda E, npc=npc: E.tensor_reduce(
                out=t["redtmp"][0:1, 5:6], in_=t["redtmp"][0:1, 50:50 + npc],
                axis=AX.X, op=Op.add), drain=True)
            self.op("vector", lambda E: E.memset(
                t["pool"][0:1, 0:512], 0.0))
            self.op("vector", lambda E, nel=nel: E.tensor_scalar(
                out=t["redtmp"][0:1, 7:8], in0=t["redtmp"][0:1, 5:6],
                scalar1=1.0 / nel, scalar2=1e-5, op0=Op.mult, op1=Op.max),
                drain=True)
            self.op("vector", lambda E: E.memset(
                t["pool"][0:1, 0:512], 0.0))
            self.op("vector", lambda E: E.tensor_scalar(
                out=t["redtmp"][0:1, 44:45], in0=t["redtmp"][0:1, 7:8],
                scalar1=-0.7, scalar2=None, op0=Op.mult), drain=True)
            self.op("vector", lambda E: E.tensor_scalar(
                out=t["redtmp"][0:1, 45:46], in0=t["redtmp"][0:1, 7:8],
                scalar1=0.7, scalar2=None, op0=Op.mult), drain=True)
            self.op("vector", lambda E: E.tensor_scalar(
                out=t["redtmp"][0:1, 46:47], in0=t["redtmp"][0:1, 7:8],
                scalar1=0.5, scalar2=None, op0=Op.mult), sem="v_prog")
            self.wait("tensor", "v_prog")
            self.op("tensor", lambda E: E.matmul(
                pp[:, 1024:1027], t["ones_row"][0:1, :],
                t["redtmp"][0:1, 44:47], start=True, stop=True), sem="t_prog")
            self.wait("vector", "t_prog")
            self.op("vector", lambda E: E.tensor_copy(
                t["bc4"][:, 0:3], pp[:, 1024:1027]), sem="v_prog")
            self.wait("scalar", "v_prog")
            self.wait("sync", "t_prog", self.c.val("t_prog"))
            for s_i, (src_ap, dst_ap, w) in enumerate(pieces):
                third = s_i % 3
                scv = sc[:, third * 2048:third * 2048 + w]
                if s_i >= 3:
                    self.wait("sync", "tz_fr", self.c.val("tz_fr") - 2)
                ldv = self.dma("sync", scv, src_ap, "tz_ld")
                self.wait("scalar", "tz_ld", ldv)
                self.wait("scalar", "v_prog", self.c.val("v_prog"))
                self.op("scalar", lambda E, scv=scv, w=w: E.sign(
                    out=t["wsl"][0][:, 0:w], in_=scv, bias=t["bc4"][:, 0:1]))
                self.op("scalar", lambda E, scv=scv, w=w: E.sign(
                    out=t["wsl"][1][:, 0:w], in_=scv, bias=t["bc4"][:, 1:2]),
                    sem="s_prog")
                self.wait("vector", "s_prog")
                self.op("vector", lambda E, w=w: E.tensor_add(
                    t["wsl"][2][:, 0:w], t["wsl"][0][:, 0:w], t["wsl"][1][:, 0:w]))
                self.op("vector", lambda E, w=w: E.tensor_scalar(
                    out=t["wsl"][2][:, 0:w], in0=t["wsl"][2][:, 0:w],
                    scalar1=t["bc4"][:, 2:3], scalar2=None, op0=Op.mult),
                    sem="v_prog")
                self.wait("gpsimd", "v_prog")
                stv = self.dma("gpsimd", dst_ap, t["wsl"][2][:, 0:w], "tz_st")
                self.wait("scalar", "tz_st", stv)
                self.op("scalar", lambda E: E.nop(), sem="tz_fr")
            self.wait("sync", "tz_st")

    def ph_ffn(self):
        t, a = self.t, self.a
        pp = t["pp"]
        xres3 = t["xres"].rearrange("p (c t) -> p c t", c=NCH)
        x3 = t["xln"].rearrange("p (c s) -> p c s", c=NCH)
        pool = t["pool"]
        wp = [pool[:, 20480 + i * 5120:20480 + (i + 1) * 5120] for i in range(2)]
        self.phase_barrier()
        self.ln_stats(xres3, (pool.rearrange("p (c t) -> p c t", c=NCH)[:, 0, :],
                              pool.rearrange("p (c t) -> p c t", c=NCH)[:, 1, :]))
        for nt in range(4):
            t0 = nt * 512
            self.ln_bcast(t0, pp[:, 1024:1536], pp[:, 1536:2048])
            for ci in range(NCH):
                self.op("vector", lambda E, ci=ci, t0=t0: E.tensor_tensor(
                    out=t["SC"][:, 0:512], in0=xres3[:, ci, t0:t0 + 512],
                    in1=pp[:, 1024:1536], op=Op.subtract))
                self.op("vector", lambda E, ci=ci: E.tensor_tensor(
                    out=x3[:, ci, 0:512], in0=t["SC"][:, 0:512],
                    in1=pp[:, 1536:2048], op=Op.mult),
                    sem="v_prog" if ci == NCH - 1 else None)
            self.wait("tensor", "v_prog")
            for m in range(40):
                g = self.c.bump("f_seq", 1) - 1
                buf = t["wsl"][g % 3]
                if self.on("sync") and g >= 3:
                    self.E.wait_ge(self.sems["f_fr"], g - 2)
                ldv = self.dma("sync", buf[:, :], a["wfc_t_sl"][m], "f_ld")
                self.wait("tensor", "f_ld", ldv)
                bk = m % 2
                gate = self.pp01_gate[bk]
                if gate is not None:
                    self.wait("tensor", gate[0], gate[1])
                ps = pp[:, bk * 512:bk * 512 + 512]
                for k in range(16):
                    self.op("tensor", lambda E, ps=ps, k=k, buf=buf: E.matmul(
                        ps, buf[:, k * 128:(k + 1) * 128],
                        x3[:, k, 0:512], start=(k == 0), stop=(k == 15)),
                        sem="t_prog" if k == 15 else None)
                self.op("tensor", lambda E: E.nop(), sem="f_fr")
                self.wait("scalar", "t_prog")
                self.wait("scalar", "v_prog", self.c.val("v_prog"))
                self.op("scalar", lambda E, ps=ps: E.activation(
                    out=t["SC"][:, 2048:2560], in_=ps, func=AF.Relu),
                    sem="s_prog")
                self.pp01_gate[bk] = ("s_prog", self.c.val("s_prog"))
                self.wait("vector", "s_prog")
                self.op("vector", lambda E, m=m: E.tensor_mul(
                    pool[:, m * 512:(m + 1) * 512], t["SC"][:, 2048:2560],
                    t["SC"][:, 2048:2560]), sem="v_prog")
            self.wait("tensor", "v_prog")
            for mo in range(16):
                g = self.c.bump("fp_seq", 1) - 1
                buf = wp[g % 2]
                if self.on("sync") and g >= 2:
                    self.E.wait_ge(self.sems["fp_fr"], g - 1)
                ldv = self.dma("sync", buf, a["wproj_t_sl"][mo], "fp_ld")
                self.wait("tensor", "fp_ld", ldv)
                bk = mo % 2
                gate = self.pp01_gate[bk]
                if gate is not None:
                    self.wait("tensor", gate[0], gate[1])
                ps = pp[:, bk * 512:bk * 512 + 512]
                for k in range(40):
                    self.op("tensor", lambda E, ps=ps, k=k, buf=buf: E.matmul(
                        ps, buf[:, k * 128:(k + 1) * 128],
                        pool[:, k * 512:(k + 1) * 512],
                        start=(k == 0), stop=(k == 39)),
                        sem="t_prog" if k == 39 else None)
                self.op("tensor", lambda E: E.nop(), sem="fp_fr")
                self.wait("vector", "t_prog")
                self.op("vector", lambda E, mo=mo, t0=t0, ps=ps: E.tensor_tensor(
                    out=xres3[:, mo, t0:t0 + 512],
                    in0=xres3[:, mo, t0:t0 + 512], in1=ps, op=Op.add),
                    sem="v_prog")
                self.pp01_gate[bk] = ("v_prog", self.c.val("v_prog"))

    def ph_attn_b(self):
        t, a = self.t, self.a
        pp = t["pp"]
        sc = t["SC"]
        xres3 = t["xres"].rearrange("p (c t) -> p c t", c=NCH)
        x3 = t["xln"].rearrange("p (c s) -> p c s", c=NCH)
        pool = t["pool"]
        SLAB = pool[:, 0:6144]
        WSM = pool[:, 6144:12288]
        KTM = pool[:, 12288:13312]
        VTM = pool[:, 13312:14336]
        KFM = pool[:, 14336:15360]
        VFM = pool[:, 15360:16384]
        SCR = [pool[:, 16384 + i * 1024:16384 + (i + 1) * 1024] for i in range(4)]
        DWF = pool[:, 20480:32768].bitcast(F32)
        self.phase_barrier()
        self.dma("gpsimd", WSM, a["wsmall"], "dma_g")
        self.dma("gpsimd", t["ident"][:, :], a["ident"], "dma_g")
        self.dma("gpsimd", t["pool"][:, 19456:20224], a["lrproj"], "dma_g")
        self.wait("tensor", "dma_g")
        self.wait("vector", "dma_g")
        self.op("vector", lambda E: E.memset(DWF, 0.0), sem="v_prog")
        self.ln_stats(xres3, (SCR[0], SCR[1]))
        for tile in range(2):
            t0 = tile * 1024
            self.build_xln_ab(xres3, x3, t0)
            self.lr_tile(x3)
            for h in range(H):
                self.ab_head_ph1(tile, h, x3)
        self.wait("gpsimd", "v_prog")
        self.dma("gpsimd", a["cc_ar_in"], DWF, "dma_g")
        self.wait("gpsimd", "dma_g")
        if CC_BYPASS:
            self.dma("gpsimd", DWF, a["cc_ar_in"], "dma_g")
        else:
            ccv = self.c.bump("cc", 1)
            if self.on("gpsimd"):
                self.E.collective_compute(
                    "AllReduce", Op.add,
                    replica_groups=[[0, 1], [2, 3], [4, 5], [6, 7]],
                    ins=[a["cc_ar_in"]], outs=[a["cc_ar_out"]],
                ).then_inc(self.sems["cc"], 1)
            self.wait("gpsimd", "cc", ccv)
            self.dma("gpsimd", DWF, a["cc_ar_out"], "dma_g")
        self.wait("vector", "dma_g")
        self.wait("scalar", "dma_g")
        self.wait("tensor", "dma_g")
        self.ns_renorm()
        for tile in range(2):
            t0 = tile * 1024
            for h in range(H):
                self.ab_head_ph2(tile, h)
            self.ab_oproj(tile, xres3)

    def build_xln_ab(self, xres3, x3, t0):
        t = self.t
        pp = t["pp"]
        for hf in range(2):
            tt = t0 + hf * 512
            self.ln_bcast(tt, pp[:, 1024:1536], pp[:, 1536:2048])
            for ci in range(NCH):
                self.op("vector", lambda E, ci=ci, tt=tt: E.tensor_tensor(
                    out=t["SC"][:, 0:512], in0=xres3[:, ci, tt:tt + 512],
                    in1=pp[:, 1024:1536], op=Op.subtract))
                self.op("vector", lambda E, ci=ci, hf=hf: E.tensor_tensor(
                    out=x3[:, ci, hf * 512:hf * 512 + 512], in0=t["SC"][:, 0:512],
                    in1=pp[:, 1536:2048], op=Op.mult),
                    sem="v_prog" if ci == NCH - 1 else None)
        self.wait("tensor", "v_prog")

    def lr_tile(self, x3):
        t = self.t
        pp = t["pp"]
        for tc in range(8):
            ps = pp[:, (tc % 4) * 512:(tc % 4) * 512 + 48]
            for k in range(NCH):
                self.op("tensor", lambda E, ps=ps, k=k, tc=tc: E.matmul(
                    ps, x3[:, k, tc * 128:(tc + 1) * 128],
                    t["pool"][:, 19456 + k * 48:19456 + (k + 1) * 48],
                    start=(k == 0), stop=(k == NCH - 1)),
                    sem="t_prog" if k == NCH - 1 else None)
            self.wait("scalar", "t_prog")
            self.wait("scalar", "v_prog", self.c.val("v_prog"))
            lt = t["SC"][:, 5248 + tc * 48:5248 + (tc + 1) * 48]
            self.op("scalar", lambda E, ps=ps: E.activation(
                out=t["SC"][:, 0:48], in_=ps, func=AF.Abs,
                bias=self.K("base_lr")), drain=True)
            self.op("scalar", lambda E: E.activation(
                out=t["SC"][:, 0:48], in_=t["SC"][:, 0:48], func=AF.Exp,
                scale=self.K("neg_one")), drain=True)
            self.op("scalar", lambda E: E.activation(
                out=t["SC"][:, 0:48], in_=t["SC"][:, 0:48], func=AF.Ln,
                bias=self.K("one")), sem="s_prog")
            self.wait("vector", "s_prog")
            self.op("vector", lambda E, ps=ps: E.tensor_scalar(
                out=t["SC"][:, 64:112], in0=ps, scalar1=self.K("base_lr"),
                scalar2=0.0, op0=Op.add, op1=Op.max), drain=True)
            self.op("vector", lambda E, lt=lt: E.tensor_add(
                lt, t["SC"][:, 0:48], t["SC"][:, 64:112]), sem="v_prog")
            self.wait("tensor", "v_prog", self.c.val("v_prog"))

    def ab_head_ph1(self, tile, h, x3):
        t, a = self.t, self.a
        pp = t["pp"]
        sc = t["SC"]
        pool = t["pool"]
        SLAB = pool[:, 0:6144]
        WSM3 = pool[:, 6144:12288].rearrange("p (b j) -> p b j", b=48)
        KTM = pool[:, 12288:13312]
        VTM = pool[:, 13312:14336]
        KFM = pool[:, 14336:15360]
        VFM = pool[:, 15360:16384]
        SCR = [pool[:, 16384 + i * 1024:16384 + (i + 1) * 1024] for i in range(4)]
        DW3 = pool[:, 20480:32768].bitcast(F32).rearrange(
            "p (b j) -> p b j", b=48)
        kvqs = sc[:, 0:3072].rearrange("p (c e) -> p c e", c=8)
        l2v = sc[:, 5120:5152]
        if self.ab_gate is not None:
            self.wait("gpsimd", "t_prog", self.ab_gate)
        ldv = self.dma("gpsimd", SLAB, a["wkvq"][h], "ab_ld")
        self.wait("tensor", "ab_ld", ldv)
        for grp in range(2):
            if grp == 1:
                self.wait("tensor", "s_prog")
            for tc4 in range(4):
                tc = grp * 4 + tc4
                ps = pp[:, tc4 * 512:tc4 * 512 + 384]
                for k in range(NCH):
                    self.op("tensor", lambda E, ps=ps, k=k, tc=tc: E.matmul(
                        ps, x3[:, k, tc * 128:(tc + 1) * 128],
                        SLAB[:, k * 384:(k + 1) * 384],
                        start=(k == 0), stop=(k == NCH - 1)),
                        sem="t_prog" if (k == NCH - 1 and tc4 == 3) else None)
            self.wait("scalar", "t_prog")
            self.wait("scalar", "v_prog", self.c.val("v_prog"))
            self.op("scalar", lambda E, grp=grp: E.activation(
                out=kvqs[:, grp * 4:(grp + 1) * 4, :],
                in_=pp[:, 0:2048].rearrange("p (c e) -> p c e", c=4)
                    [:, :, 0:384], func=AF.Silu), sem="s_prog")
        self.ab_gate = self.c.val("t_prog")
        self.wait("vector", "s_prog")
        self.wait("scalar", "v_prog", self.c.val("v_prog"))
        self.op("scalar", lambda E: E.activation(
            out=sc[:, 3072:4096], in_=kvqs[:, :, 0:128], func=AF.Square))
        self.op("scalar", lambda E: E.activation(
            out=sc[:, 4096:5120], in_=kvqs[:, :, 256:384], func=AF.Square),
            sem="s_prog")
        self.wait("vector", "s_prog")
        self.op("vector", lambda E: E.tensor_reduce(
            out=l2v[:, 0:8], in_=sc[:, 3072:4096].rearrange(
                "p (c e) -> p c e", c=8), axis=AX.X, op=Op.add))
        self.op("vector", lambda E: E.tensor_reduce(
            out=l2v[:, 8:16], in_=sc[:, 4096:5120].rearrange(
                "p (c e) -> p c e", c=8), axis=AX.X, op=Op.add), sem="v_prog")
        self.wait("scalar", "v_prog")
        self.op("scalar", lambda E: E.activation(
            out=l2v[:, 16:32], in_=l2v[:, 0:16], func=AF.Sqrt),
            sem="s_prog")
        self.wait("vector", "s_prog")
        self.op("vector", lambda E: E.tensor_scalar(
            out=l2v[:, 16:32], in0=l2v[:, 16:32],
            scalar1=self.K("eps_l2"), scalar2=None, op0=Op.add), drain=True)
        self.op("vector", lambda E: E.reciprocal(
            out=l2v[:, 16:32], in_=l2v[:, 16:32]), drain=True)
        self.op("vector", lambda E: E.tensor_tensor(
            out=KTM.rearrange("p (c e) -> p c e", c=8), in0=kvqs[:, :, 0:128],
            in1=l2v[:, 16:24].rearrange("p (c o) -> p c o", o=1)
                .broadcast_to((128, 8, 128)), op=Op.mult))
        self.op("vector", lambda E: E.tensor_copy(
            VTM.rearrange("p (c e) -> p c e", c=8), kvqs[:, :, 128:256]))
        self.op("vector", lambda E: E.tensor_tensor(
            out=sc[:, 3072:4096].rearrange("p (c e) -> p c e", c=8),
            in0=kvqs[:, :, 256:384],
            in1=l2v[:, 24:32].rearrange("p (c o) -> p c o", o=1)
                .broadcast_to((128, 8, 128)), op=Op.mult), sem="v_prog")
        self.wait("gpsimd", "v_prog")
        qsv = self.dma("gpsimd", a["qstore"][tile * H + h], sc[:, 3072:4096],
                       "q_st")
        self.wait("tensor", "v_prog")
        for tc in range(8):
            self.op("tensor", lambda E, tc=tc: E.transpose(
                t["ppb"][:, tc * 128:(tc + 1) * 128],
                KTM[:, tc * 128:(tc + 1) * 128], t["ident"][:, :]))
            self.op("tensor", lambda E, tc=tc: E.transpose(
                t["ppb"][:, 1024 + tc * 128:1024 + (tc + 1) * 128],
                VTM[:, tc * 128:(tc + 1) * 128], t["ident"][:, :]),
                sem="t_prog" if tc == 7 else None)
        self.wait("vector", "t_prog")
        self.op("vector", lambda E: E.tensor_copy(
            pool[:, 14336:16384], t["ppb"][:, 0:2048]), sem="v_prog")
        self.wait("tensor", "v_prog", self.c.val("v_prog"))
        for grp in range(2):
            for tc4 in range(4):
                tc = grp * 4 + tc4
                ps = pp[:, tc4 * 512:tc4 * 512 + 512]
                self.op("tensor", lambda E, ps=ps, tc=tc, h=h: E.matmul(
                    ps[:, 0:128], KFM[:, tc * 128:(tc + 1) * 128],
                    WSM3[:, h, :], start=True, stop=True))
                self.op("tensor", lambda E, ps=ps, tc=tc, h=h: E.matmul(
                    ps[:, 128:256], KFM[:, tc * 128:(tc + 1) * 128],
                    WSM3[:, 16 + h, :], start=True, stop=True))
                self.op("tensor", lambda E, ps=ps, tc=tc, h=h: E.matmul(
                    ps[:, 256:384], VFM[:, tc * 128:(tc + 1) * 128],
                    WSM3[:, 32 + h, :], start=True, stop=True),
                    sem="t_prog" if tc4 == 3 else None)
            self.wait("vector", "t_prog")
            pv = pp[:, 0:2048].rearrange("p (c e) -> p c e", c=4)
            for j, base in ((0, 0), (1, 1024), (2, 2048)):
                self.op("vector", lambda E, j=j, base=base, grp=grp: E.tensor_copy(
                    sc[:, base:base + 1024].rearrange("p (c e) -> p c e", c=8)
                        [:, grp * 4:(grp + 1) * 4, :],
                    pv[:, :, j * 128:(j + 1) * 128]),
                    sem="v_prog" if j == 2 else None)
            self.wait("tensor", "v_prog", self.c.val("v_prog"))
        A, Bq, C = sc[:, 0:1024], sc[:, 1024:2048], sc[:, 2048:3072]
        Dq, Eq = sc[:, 3072:4096], sc[:, 4096:5120]
        self.wait("scalar", "v_prog", self.c.val("v_prog"))
        self.wait("scalar", "q_st", qsv)
        self.op("scalar", lambda E: E.activation(out=Dq, in_=A, func=AF.Silu),
                sem="s_prog")
        self.wait("vector", "s_prog")
        self.op("vector", lambda E: E.tensor_mul(Eq, Dq, Bq))
        self.op("vector", lambda E: E.tensor_tensor(
            out=SCR[1].rearrange("p e -> p e"), in0=C, in1=Dq, op=Op.mult))
        self.op("vector", lambda E: E.tensor_mul(C, C, Bq), sem="v_prog")
        self.wait("scalar", "v_prog")
        self.op("scalar", lambda E: E.activation(out=Bq, in_=A, func=AF.Sigmoid),
                sem="s_prog")
        self.wait("vector", "s_prog")
        self.op("vector", lambda E: E.tensor_mul(A, Dq, Bq))
        self.op("vector", lambda E: E.tensor_sub(A, Dq, A))
        self.op("vector", lambda E: E.tensor_add(A, A, Bq))
        self.op("vector", lambda E: E.tensor_tensor(
            out=SCR[0].rearrange("p e -> p e"), in0=C, in1=A, op=Op.mult),
            sem="v_prog")
        lr3 = sc[:, 5248:5632].rearrange("p (c e) -> p c e", c=8)
        self.op("vector", lambda E, h=h: E.tensor_tensor(
            out=SCR[2].rearrange("p (c e) -> p c e", c=8),
            in0=Eq.rearrange("p (c e) -> p c e", c=8),
            in1=lr3[:, :, 3 * h + 1:3 * h + 2].broadcast_to((128, 8, 128)),
            op=Op.mult), sem="v_prog")
        self.wait("tensor", "v_prog")
        self.dw_mm(16 + h, VTM, SCR[2])
        self.op("vector", lambda E, h=h: E.tensor_tensor(
            out=SCR[2].rearrange("p (c e) -> p c e", c=8),
            in0=KTM.rearrange("p (c e) -> p c e", c=8),
            in1=lr3[:, :, 3 * h:3 * h + 1].broadcast_to((128, 8, 128)),
            op=Op.mult), sem="v_prog")
        self.wait("tensor", "v_prog")
        self.dw_mm(h, SCR[0], SCR[2])
        self.op("vector", lambda E, h=h: E.tensor_tensor(
            out=SCR[2].rearrange("p (c e) -> p c e", c=8),
            in0=KTM.rearrange("p (c e) -> p c e", c=8),
            in1=lr3[:, :, 3 * h + 2:3 * h + 3].broadcast_to((128, 8, 128)),
            op=Op.mult), sem="v_prog")
        self.wait("tensor", "v_prog")
        self.dw_mm(32 + h, SCR[1], SCR[2])

    def dw_mm(self, block, lhs_tm, rhs_tm):
        t = self.t
        pp = t["pp"]
        DW3 = t["pool"][:, 20480:32768].bitcast(F32).rearrange(
            "p (b j) -> p b j", b=48)
        bk = self.dw_bank
        self.dw_bank ^= 1
        gate = self.dw_gate[bk]
        if gate is not None:
            self.wait("tensor", "v_prog", gate)
        ps = pp[:, 2048 + bk * 512:2048 + bk * 512 + 128]
        for tc in range(8):
            self.op("tensor", lambda E, ps=ps, tc=tc: E.matmul(
                ps, lhs_tm[:, tc * 128:(tc + 1) * 128],
                rhs_tm[:, tc * 128:(tc + 1) * 128],
                start=(tc == 0), stop=(tc == 7)),
                sem="t_prog" if tc == 7 else None)
        self.wait("vector", "t_prog")
        self.op("vector", lambda E, block=block, ps=ps: E.tensor_tensor(
            out=DW3[:, block, :], in0=DW3[:, block, :], in1=ps, op=Op.add),
            sem="v_prog")
        self.dw_gate[bk] = self.c.val("v_prog")

    def transpose48(self, src3, dst3):
        t = self.t
        for grp in range(3):
            self.wait("tensor", "v_prog", self.c.val("v_prog"))
            for b16 in range(16):
                b = grp * 16 + b16
                self.op("tensor", lambda E, b=b, b16=b16: E.transpose(
                    t["ppb"][:, b16 * 128:(b16 + 1) * 128], src3[:, b, :],
                    t["ident"][:, :]), sem="t_prog" if b16 == 15 else None)
            self.wait("vector", "t_prog")
            self.op("vector", lambda E, grp=grp: E.tensor_copy(
                dst3[:, grp * 16:(grp + 1) * 16, :],
                t["ppb"][:, 0:2048].rearrange("p (c e) -> p c e", c=16)),
                sem="v_prog")

    def mm48(self, lhs3, rhs3, out3, post=None):
        t = self.t
        pp = t["pp"]
        for grp in range(12):
            self.wait("tensor", "v_prog", self.c.val("v_prog"))
            for b4 in range(4):
                b = grp * 4 + b4
                self.op("tensor", lambda E, b=b, b4=b4: E.matmul(
                    pp[:, b4 * 512:b4 * 512 + 128], lhs3[:, b, :],
                    rhs3[:, b, :], start=True, stop=True),
                    sem="t_prog" if b4 == 3 else None)
            self.wait("vector", "t_prog")
            pv = pp[:, 0:2048].rearrange("p (c e) -> p c e", c=4)[:, :, 0:128]
            if post is None:
                self.op("vector", lambda E, grp=grp: E.tensor_copy(
                    out3[:, grp * 4:(grp + 1) * 4, :], pv), sem="v_prog")
            else:
                post(grp, pv)

    def ns_renorm(self):
        t, a = self.t, self.a
        pp = t["pp"]
        sc = t["SC"]
        pool = t["pool"]
        XB3 = pool[:, 0:6144].rearrange("p (b j) -> p b j", b=48)
        XTB3 = pool[:, 6144:12288].rearrange("p (b j) -> p b j", b=48)
        AB3 = pool[:, 12288:18432].rearrange("p (b j) -> p b j", b=48)
        DWF = pool[:, 20480:32768].bitcast(F32)
        DW3 = DWF.rearrange("p (b j) -> p b j", b=48)
        ns48 = t["xln"][:, 0:320].bitcast(F32)
        self.phase_barrier()
        ldv = self.dma("gpsimd", sc[:, :], a["w012"], "dma_g")
        self.op("scalar", lambda E: E.activation(
            out=pool[:, 0:12288].bitcast(F32), in_=DWF, func=AF.Square),
            sem="s_prog")
        self.wait("vector", "s_prog")
        self.op("vector", lambda E: E.tensor_reduce(
            out=ns48[:, 0:48],
            in_=pool[:, 0:12288].bitcast(F32).rearrange("p (b j) -> p b j", b=48),
            axis=AX.X, op=Op.add), sem="v_prog")
        self.row48_recip_bcast(ns48[:, 0:48], "eps_ns", pp, 512)
        self.op("vector", lambda E: E.tensor_tensor(
            out=XB3, in0=DW3,
            in1=pp[:, 512:560].rearrange("p (b o) -> p b o", o=1)
                .broadcast_to((128, 48, 128)), op=Op.mult), sem="v_prog")
        self.transpose48(XB3, XTB3)
        for it, (ca, cb, cc_) in enumerate(NS_COEFS):
            self.mm48(XTB3, XTB3, AB3)
            for grp in range(12):
                self.wait("tensor", "v_prog", self.c.val("v_prog"))
                for b4 in range(4):
                    b = grp * 4 + b4
                    self.op("tensor", lambda E, b=b, b4=b4: E.matmul(
                        pp[:, b4 * 512:b4 * 512 + 128], AB3[:, b, :],
                        AB3[:, b, :], start=True, stop=True),
                        sem="t_prog" if b4 == 3 else None)
                self.wait("vector", "t_prog")
                pv = pp[:, 0:2048].rearrange("p (c e) -> p c e", c=4)[:, :, 0:128]
                tmp3 = pool[:, 18944:19456].rearrange("p (c e) -> p c e", c=4)
                self.op("vector", lambda E, pv=pv, cc_=cc_, tmp3=tmp3: E.tensor_scalar(
                    out=tmp3, in0=pv, scalar1=cc_, scalar2=None, op0=Op.mult))
                self.op("vector", lambda E, tmp3=tmp3, cb=cb, grp=grp: E.scalar_tensor_tensor(
                    out=AB3[:, grp * 4:(grp + 1) * 4, :],
                    in0=AB3[:, grp * 4:(grp + 1) * 4, :], scalar=cb, in1=tmp3,
                    op0=Op.mult, op1=Op.add), sem="v_prog")
            for grp in range(12):
                self.wait("tensor", "v_prog", self.c.val("v_prog"))
                for b4 in range(4):
                    b = grp * 4 + b4
                    self.op("tensor", lambda E, b=b, b4=b4: E.matmul(
                        pp[:, b4 * 512:b4 * 512 + 128], AB3[:, b, :],
                        XB3[:, b, :], start=True, stop=True))
                    self.op("tensor", lambda E, b=b, b4=b4: E.matmul(
                        pp[:, b4 * 512 + 128:b4 * 512 + 256], XB3[:, b, :],
                        AB3[:, b, :], start=True, stop=True),
                        sem="t_prog" if b4 == 3 else None)
                self.wait("vector", "t_prog")
                pv = pp[:, 0:2048].rearrange("p (c e) -> p c e", c=4)
                self.op("vector", lambda E, pv=pv, ca=ca, grp=grp: E.scalar_tensor_tensor(
                    out=XB3[:, grp * 4:(grp + 1) * 4, :],
                    in0=XB3[:, grp * 4:(grp + 1) * 4, :], scalar=ca,
                    in1=pv[:, :, 0:128], op0=Op.mult, op1=Op.add))
                self.op("vector", lambda E, pv=pv, ca=ca, grp=grp: E.scalar_tensor_tensor(
                    out=XTB3[:, grp * 4:(grp + 1) * 4, :],
                    in0=XTB3[:, grp * 4:(grp + 1) * 4, :], scalar=ca,
                    in1=pv[:, :, 128:256], op0=Op.mult, op1=Op.add),
                    sem="v_prog")
        self.wait("vector", "dma_g", ldv)
        self.op("vector", lambda E: E.tensor_tensor(
            out=DWF, in0=sc[:, :], in1=pool[:, 0:6144], op=Op.add),
            sem="v_prog")
        FS = pool[:, 12288:18432].bitcast(F32)
        FS3 = FS.rearrange("p (b j) -> p b j", b=24)
        self.wait("scalar", "v_prog")
        for hf in range(2):
            self.op("scalar", lambda E, hf=hf: E.activation(
                out=FS, in_=sc[:, hf * 3072:(hf + 1) * 3072], func=AF.Square),
                sem="s_prog")
            self.wait("vector", "s_prog")
            self.op("vector", lambda E, hf=hf: E.tensor_reduce(
                out=ns48[:, 48 + hf * 24:48 + (hf + 1) * 24], in_=FS3,
                axis=AX.X, op=Op.add), sem="v_prog")
            self.wait("scalar", "v_prog")
        for hf in range(2):
            self.op("scalar", lambda E, hf=hf: E.activation(
                out=FS, in_=DWF[:, hf * 3072:(hf + 1) * 3072], func=AF.Square),
                sem="s_prog")
            self.wait("vector", "s_prog")
            self.op("vector", lambda E, hf=hf: E.tensor_reduce(
                out=ns48[:, hf * 24:(hf + 1) * 24], in_=FS3,
                axis=AX.X, op=Op.add), sem="v_prog")
            self.wait("scalar", "v_prog")
        self.op("scalar", lambda E: E.activation(
            out=ns48[:, 96:144], in_=ns48[:, 48:96], func=AF.Sqrt), drain=True)
        self.op("scalar", lambda E: E.activation(
            out=ns48[:, 0:48], in_=ns48[:, 0:48], func=AF.Sqrt),
            sem="s_prog")
        self.wait("vector", "s_prog")
        self.op("vector", lambda E: E.tensor_scalar(
            out=ns48[:, 0:48], in0=ns48[:, 0:48],
            scalar1=self.K("eps_l2"), scalar2=None, op0=Op.add), drain=True)
        self.op("vector", lambda E: E.reciprocal(
            out=ns48[:, 0:48], in_=ns48[:, 0:48]), drain=True)
        self.op("vector", lambda E: E.tensor_mul(
            ns48[:, 96:144], ns48[:, 96:144], ns48[:, 0:48]), drain=True)
        self.op("vector", lambda E: E.tensor_tensor(
            out=XB3, in0=DW3,
            in1=ns48[:, 96:144].rearrange("p (b o) -> p b o", o=1)
                .broadcast_to((128, 48, 128)), op=Op.mult), sem="v_prog")
        self.transpose48(XB3, XTB3)

    def row48_recip_bcast(self, src48, epsname, pp, col0):
        t = self.t
        pool = self.t["pool"]
        stg = pool[:, 18432:18480]
        self.wait("vector", "v_prog", self.c.val("v_prog"))
        self.op("vector", lambda E, stg=stg, src48=src48: E.tensor_copy(
            stg, src48), sem="v_prog")
        self.wait("tensor", "v_prog")
        self.op("tensor", lambda E, stg=stg: E.matmul(
            pp[0:1, 0:48], t["ones_bf"][:, 0:1], stg,
            start=True, stop=True), sem="t_prog")
        self.wait("scalar", "t_prog")
        row = t["redtmp"][0:1, 96:144]
        self.op("scalar", lambda E, row=row: E.activation(
            out=row, in_=pp[0:1, 0:48], func=AF.Sqrt), drain=True)
        self.op("scalar", lambda E, row=row, epsname=epsname: E.activation(
            out=row, in_=row, func=AF.Ln, bias=self.K1(epsname)), drain=True)
        self.op("scalar", lambda E, row=row: E.activation(
            out=row, in_=row, func=AF.Exp, scale=self.K1("neg_one")),
            sem="s_prog")
        self.wait("vector", "s_prog")
        self.op("vector", lambda E, row=row: E.tensor_copy(
            t["stats"][0:1, 0:48], row), sem="v_prog")
        self.wait("tensor", "v_prog")
        self.op("tensor", lambda E, col0=col0: E.matmul(
            pp[:, col0:col0 + 48], t["ones_row_bf"][0:1, :],
            t["stats"][0:1, 0:48], start=True, stop=True), sem="t_prog")
        self.wait("vector", "t_prog")

    def ab_head_ph2(self, tile, h):
        t, a = self.t, self.a
        pp = t["pp"]
        sc = t["SC"]
        pool = t["pool"]
        WFIN3 = pool[:, 0:6144].rearrange("p (b j) -> p b j", b=48)
        WFINT3 = pool[:, 6144:12288].rearrange("p (b j) -> p b j", b=48)
        Q = pool[:, 12288:13312]
        QFM = pool[:, 13312:14336]
        OG = pool[:, 14336:15360]
        OGFM = pool[:, 15360:16384]
        OALL = pool[:, 16384:32768]
        self.wait("gpsimd", "t_prog")
        self.wait("gpsimd", "v_prog")
        ldv = self.dma("gpsimd", Q, a["qstore"][tile * H + h], "ab_ld")
        self.wait("tensor", "ab_ld", ldv)
        self.wait("tensor", "v_prog")
        for tc in range(8):
            self.op("tensor", lambda E, tc=tc: E.transpose(
                t["ppb"][:, tc * 128:(tc + 1) * 128],
                Q[:, tc * 128:(tc + 1) * 128], t["ident"][:, :]),
                sem="t_prog" if tc == 7 else None)
        self.wait("vector", "t_prog")
        self.op("vector", lambda E: E.tensor_copy(QFM, t["ppb"][:, 0:1024]),
                sem="v_prog")
        self.wait("tensor", "v_prog", self.c.val("v_prog"))
        for grp in range(2):
            for tc4 in range(4):
                tc = grp * 4 + tc4
                ps = pp[:, tc4 * 512:tc4 * 512 + 512]
                self.op("tensor", lambda E, ps=ps, tc=tc, h=h: E.matmul(
                    ps[:, 0:128], QFM[:, tc * 128:(tc + 1) * 128],
                    WFINT3[:, h, :], start=True, stop=True))
                self.op("tensor", lambda E, ps=ps, tc=tc, h=h: E.matmul(
                    ps[:, 128:256], QFM[:, tc * 128:(tc + 1) * 128],
                    WFINT3[:, 32 + h, :], start=True, stop=True),
                    sem="t_prog" if tc4 == 3 else None)
            self.wait("scalar", "t_prog")
            self.wait("scalar", "v_prog", self.c.val("v_prog"))
            pv = pp[:, 0:2048].rearrange("p (c e) -> p c e", c=4)
            self.op("scalar", lambda E, pv=pv, grp=grp: E.activation(
                out=sc[:, 0:1024].rearrange("p (c e) -> p c e", c=8)
                    [:, grp * 4:(grp + 1) * 4, :],
                in_=pv[:, :, 0:128], func=AF.Silu), sem="s_prog")
            self.wait("vector", "s_prog")
            self.op("vector", lambda E, pv=pv, grp=grp: E.tensor_copy(
                sc[:, 1024:2048].rearrange("p (c e) -> p c e", c=8)
                    [:, grp * 4:(grp + 1) * 4, :], pv[:, :, 128:256]),
                sem="v_prog")
            self.wait("tensor", "v_prog", self.c.val("v_prog"))
        self.op("vector", lambda E: E.tensor_tensor(
            out=OG, in0=sc[:, 0:1024], in1=sc[:, 1024:2048], op=Op.mult),
            sem="v_prog")
        self.wait("tensor", "v_prog")
        for tc in range(8):
            self.op("tensor", lambda E, tc=tc: E.transpose(
                t["ppb"][:, tc * 128:(tc + 1) * 128],
                OG[:, tc * 128:(tc + 1) * 128], t["ident"][:, :]),
                sem="t_prog" if tc == 7 else None)
        self.wait("vector", "t_prog")
        self.op("vector", lambda E: E.tensor_copy(OGFM, t["ppb"][:, 0:1024]),
                sem="v_prog")
        self.wait("tensor", "v_prog", self.c.val("v_prog"))
        for n2 in range(2):
            self.op("tensor", lambda E, n2=n2, h=h: E.matmul(
                pp[:, n2 * 512:(n2 + 1) * 512], WFINT3[:, 16 + h, :],
                OGFM[:, n2 * 512:(n2 + 1) * 512], start=True, stop=True),
                sem="t_prog" if n2 == 1 else None)
        self.wait("scalar", "t_prog")
        self.wait("scalar", "v_prog", self.c.val("v_prog"))
        self.op("scalar", lambda E: E.activation(
            out=pool[:, 12288:13312], in_=pp[:, 0:1024], func=AF.Square),
            sem="s_prog")
        self.wait("tensor", "s_prog")
        self.op("tensor", lambda E: E.matmul(
            pp[0:1, 1024:1536], t["ones_bf"][:, 0:1], pool[:, 12288:12800],
            start=True, stop=True))
        self.op("tensor", lambda E: E.matmul(
            pp[0:1, 1536:2048], t["ones_bf"][:, 0:1], pool[:, 12800:13312],
            start=True, stop=True), sem="t_prog")
        self.wait("scalar", "t_prog")
        row = sc[0:1, 4096:5120]
        self.op("scalar", lambda E, row=row: E.activation(
            out=row[0:1, 0:512], in_=pp[0:1, 1024:1536], func=AF.Ln,
            scale=self.K1("inv_hd"), bias=self.K1("eps_ln")), drain=True)
        self.op("scalar", lambda E, row=row: E.activation(
            out=row[0:1, 512:1024], in_=pp[0:1, 1536:2048], func=AF.Ln,
            scale=self.K1("inv_hd"), bias=self.K1("eps_ln")), drain=True)
        self.op("scalar", lambda E, row=row: E.activation(
            out=row, in_=row, func=AF.Exp, scale=self.K1("neg_half")),
            sem="s_prog")
        self.wait("vector", "s_prog")
        self.op("vector", lambda E, row=row: E.tensor_copy(
            t["stats"][0:1, 0:1024], row), sem="v_prog")
        self.wait("tensor", "v_prog")
        self.op("tensor", lambda E: E.matmul(
            pp[:, 1024:1536], t["ones_row_bf"][0:1, :],
            t["stats"][0:1, 0:512], start=True, stop=True))
        self.op("tensor", lambda E: E.matmul(
            pp[:, 1536:2048], t["ones_row_bf"][0:1, :],
            t["stats"][0:1, 512:1024], start=True, stop=True),
            sem="t_prog")
        self.wait("vector", "t_prog")
        self.op("vector", lambda E: E.tensor_copy(sc[:, 0:1024], pp[:, 0:1024]),
                sem="v_prog")
        self.op("vector", lambda E, h=h: E.tensor_tensor(
            out=OALL[:, h * 1024:(h + 1) * 1024], in0=sc[:, 0:1024],
            in1=pp[:, 1024:2048], op=Op.mult), sem="v_prog")
        self.wait("tensor", "v_prog", self.c.val("v_prog"))

    def ab_oproj(self, tile, xres3):
        t, a = self.t, self.a
        pp = t["pp"]
        pool = t["pool"]
        OALL = pool[:, 16384:32768]
        wps = [pool[:, 12288 + i * 2048:12288 + (i + 1) * 2048] for i in range(2)]
        t0 = tile * 1024
        self.wait("gpsimd", "t_prog")
        self.wait("gpsimd", "v_prog")
        self.pp01_gate[0] = ("v_prog", self.c.val("v_prog"))
        self.pp01_gate[1] = ("v_prog", self.c.val("v_prog"))
        for mo in range(16):
            g = self.c.bump("op_seq", 1) - 1
            buf = wps[g % 2]
            if g - 2 in self.op_gate:
                self.wait("gpsimd", "t_prog", self.op_gate[g - 2])
            ldv = self.dma("gpsimd", buf, a["woproj"][mo], "op_ld")
            self.wait("vector", "op_ld", ldv)
            self.op("vector", lambda E, buf=buf: E.tensor_scalar(
                out=buf, in0=buf, scalar1=t["gamma"][:, 0:1], scalar2=None,
                op0=Op.mult), sem="v_prog")
            self.wait("tensor", "v_prog", self.c.val("v_prog"))
            for n2 in range(2):
                bk = n2
                gate = self.pp01_gate[bk]
                if gate is not None:
                    self.wait("tensor", gate[0], gate[1])
                ps = pp[:, bk * 512:bk * 512 + 512]
                for k in range(NCH):
                    self.op("tensor", lambda E, ps=ps, k=k, n2=n2, buf=buf: E.matmul(
                        ps, buf[:, k * 128:(k + 1) * 128],
                        OALL[:, k * 1024 + n2 * 512:k * 1024 + n2 * 512 + 512],
                        start=(k == 0), stop=(k == NCH - 1)),
                        sem="t_prog" if k == NCH - 1 else None)
                self.wait("vector", "t_prog")
                self.op("vector", lambda E, mo=mo, t0=t0, n2=n2, ps=ps: E.tensor_tensor(
                    out=xres3[:, mo, t0 + n2 * 512:t0 + n2 * 512 + 512],
                    in0=xres3[:, mo, t0 + n2 * 512:t0 + n2 * 512 + 512],
                    in1=ps, op=Op.add), sem="v_prog")
                self.pp01_gate[bk] = ("v_prog", self.c.val("v_prog"))
            self.op_gate[g] = self.c.val("t_prog")

    def ph_store(self):
        t, a = self.t, self.a
        self.wait("gpsimd", "v_prog")
        self.dma("gpsimd", a["out"], t["xres"][:, :], "dma_st")
        self.wait("sync", "dma_st")


def build_nc():
    nc = bass.Bass("TRN2", target_bir_lowering=False, debug=False,
                   detect_race_conditions=False)
    a = {}
    a["x"] = nc.dram_tensor("x", [128, NCH, T], F32, kind="ExternalInput").ap()
    a["prev"] = nc.dram_tensor("prev", [128, 16], F32, kind="ExternalInput").ap()
    a["consts"] = nc.dram_tensor("consts", [128, len(CONSTS)], F32,
                                 kind="ExternalInput").ap()
    a["flags"] = nc.dram_tensor("flags", [128, 2], F32, kind="ExternalInput").ap()
    a["offs8"] = nc.dram_tensor("offs8", [128, 8], F32, kind="ExternalInput").ap()
    a["gamma"] = nc.dram_tensor("gamma", [128, 1], F32, kind="ExternalInput").ap()
    a["ones"] = nc.dram_tensor("ones", [128, 2], F32, kind="ExternalInput").ap()
    a["ones_row"] = nc.dram_tensor("ones_row", [1, 128], F32,
                                   kind="ExternalInput").ap()
    wact = nc.dram_tensor("wact", [16, 128, 32 * 128], F32, kind="ExternalInput")
    wrec = nc.dram_tensor("wrec", [16, 128, 32 * 128], F32, kind="ExternalInput")
    wval = nc.dram_tensor("wval", [16, 128, 16 * 128], F32, kind="ExternalInput")
    a["wact"] = [wact.ap()[m] for m in range(16)]
    a["wrec"] = [wrec.ap()[m] for m in range(16)]
    a["wval"] = [wval.ap()[m] for m in range(16)]
    wfc = nc.dram_tensor("wfc", [40, 128, 2048], F32, kind="ExternalInput")
    wproj = nc.dram_tensor("wproj", [16, 128, 5120], F32, kind="ExternalInput")
    wfc_t = nc.dram_tensor("wfc_t", [40, 128, 2048], BF16)
    wproj_t = nc.dram_tensor("wproj_t", [16, 128, 5120], BF16)
    a["wfc_t_sl"] = [wfc_t.ap()[m] for m in range(40)]
    a["wproj_t_sl"] = [wproj_t.ap()[m] for m in range(16)]
    tz0 = []
    for m in range(40):
        for o in range(0, 2048, 1024):
            tz0.append((wfc.ap()[m][:, o:o + 1024], wfc_t.ap()[m][:, o:o + 1024], 1024))
    tz1 = []
    for m in range(16):
        for o in range(0, 5120, 1024):
            tz1.append((wproj.ap()[m][:, o:o + 1024], wproj_t.ap()[m][:, o:o + 1024], 1024))
    a["tz_src"] = [tz0, tz1]
    a["tz_nel"] = [40 * 128 * 2048, 16 * 128 * 5120]
    wkvq = nc.dram_tensor("wkvq", [16, 128, 6144], F32, kind="ExternalInput")
    a["wkvq"] = [wkvq.ap()[hh] for hh in range(16)]
    a["wsmall"] = nc.dram_tensor("wsmall", [128, 6144], F32,
                                 kind="ExternalInput").ap()
    a["w012"] = nc.dram_tensor("w012", [128, 6144], F32,
                               kind="ExternalInput").ap()
    woproj = nc.dram_tensor("woproj", [16, 128, 2048], F32,
                            kind="ExternalInput")
    a["woproj"] = [woproj.ap()[mm] for mm in range(16)]
    a["lrproj"] = nc.dram_tensor("lrproj", [128, 768], F32,
                                 kind="ExternalInput").ap()
    a["ident"] = nc.dram_tensor("ident", [128, 128], F32,
                                kind="ExternalInput").ap()
    qstore = nc.dram_tensor("qstore", [32, 128, 1024], BF16)
    a["qstore"] = [qstore.ap()[i] for i in range(32)]
    a["cc_ar_in"] = nc.dram_tensor("cc_ar_in", [128, 6144], F32).ap()
    a["cc_ar_out"] = nc.dram_tensor("cc_ar_out", [128, 6144], F32).ap()
    a["out"] = nc.dram_tensor("out", [128, NCH * T], F32,
                              kind="ExternalOutput").ap()
    a["cc_ag_in"] = nc.dram_tensor("cc_ag_in", [128, 24], F32).ap()
    ago = nc.dram_tensor("cc_ag_out", [256, 24], F32)
    a["cc_ag_out"] = ago.ap()
    a["cc_ag_out_r0"] = ago.ap()[0:128, :]

    stack = contextlib.ExitStack()
    t = {}

    def sb(name, shape, dt=F32):
        t[name] = stack.enter_context(nc.sbuf_tensor("sb_" + name, shape, dt))
        return t[name]

    sb("consts", [128, len(CONSTS)])
    sb("flags", [128, 2])
    sb("offs8", [128, 8])
    sb("gamma", [128, 1])
    sb("prev", [128, 16])
    sb("prev_ln", [128, 32], BF16)
    sb("prev_bf", [128, 16], BF16)
    sb("prevsq_bf", [128, 16], BF16)
    sb("prevstat_bf", [1, 2], BF16)
    sb("redtmp", [1, 160])
    sb("bc4", [128, 4])
    sb("ones_bf", [128, 2], BF16)
    sb("ones_f32", [128, 2])
    sb("ones_row", [1, 128])
    sb("ones_row_bf", [1, 128], BF16)
    sb("xres", [128, NCH * T], BF16)
    sb("pool", [128, NCH * T], BF16)
    sb("xln", [128, NCH * XSTR], BF16)
    sb("xln_last", [128, 32], BF16)
    sb("SC", [128, 6144])
    sb("stats", [1, 2 * T], BF16)
    sb("summ", [128, 24])
    sb("peers", [128, 24])
    sb("carry", [128, 24])
    t["wsl"] = [sb("wsl0", [128, 2048], BF16), sb("wsl1", [128, 2048], BF16),
                sb("wsl2", [128, 2048], BF16)]
    sb("ident", [128, 128], BF16)
    sb("tzsl", [128, 192])
    sb("bc6", [128, 4])

    pstack = contextlib.ExitStack()
    t["pp"] = pstack.enter_context(nc.psum_tensor("pp", [128, 3072], F32))
    t["ppb"] = pstack.enter_context(nc.psum_tensor("ppb", [128, 2048], BF16))

    sems = {}
    semstack = contextlib.ExitStack()
    for s in ["boot", "dma_s", "dma_g", "dma_st", "v_prog", "s_prog", "t_prog",
              "t_bc", "g_prog", "cc", "lnx2", "w_ld", "w_fr", "wr_ld", "wr_fr",
              "wv_ld", "wv_fr", "tz_ld", "tz_st", "tz_fr", "f_ld", "f_fr",
              "fp_ld", "fp_fr", "ab_ld", "q_st", "op_ld", "tz_s", "tz_v"]:
        sems[s] = semstack.enter_context(nc.semaphore(s))

    kb = KB(nc, sems, t, a)
    with nc.Block() as block:
        @block.sync
        def _(E):
            kb.emit("sync", E)

        @block.vector
        def _(E):
            kb.emit("vector", E)

        @block.scalar
        def _(E):
            kb.emit("scalar", E)

        @block.tensor
        def _(E):
            kb.emit("tensor", E)

        @block.gpsimd
        def _(E):
            kb.emit("gpsimd", E)

    stack.close()
    pstack.close()
    semstack.close()
    return nc


def make_in_maps(inputs):
    x = np.asarray(inputs["L0_attn_a_x"], dtype=np.float32)
    B_, S, C = x.shape
    Th = S // 2

    consts = np.zeros((128, len(CONSTS)), dtype=np.float32)
    for k, v in CONSTS.items():
        consts[:, CIDX[k]] = v
    ones = np.zeros((128, 2), dtype=np.float32)
    ones[:, 0] = 1.0
    ones_row = np.ones((1, 128), dtype=np.float32)

    W_act = np.asarray(inputs["L0_attn_a_W_act"], dtype=np.float32)
    perm = np.concatenate([np.arange(0, C, 2), np.arange(1, C, 2)])
    wact_sl = np.ascontiguousarray(
        np.ascontiguousarray(W_act[perm].T).reshape(32, 128, 16, 128)
        .transpose(2, 1, 0, 3)).reshape(16, 128, 32 * 128)
    W_rec = np.asarray(inputs["L0_attn_a_W_rec"], dtype=np.float32)
    wrec_sl = np.ascontiguousarray(
        np.ascontiguousarray(W_rec.T).reshape(32, 128, 16, 128)
        .transpose(2, 1, 0, 3)).reshape(16, 128, 32 * 128)
    W_val = np.asarray(inputs["L0_attn_a_W_val"], dtype=np.float32)
    wval_sl = np.ascontiguousarray(
        np.ascontiguousarray(W_val[:, perm].T).reshape(16, 128, 16, 128)
        .transpose(2, 1, 0, 3)).reshape(16, 128, 16 * 128)

    W_fc = np.asarray(inputs["L0_ffn_w_fc"], dtype=np.float32)
    wfc_sl = np.ascontiguousarray(
        np.ascontiguousarray(W_fc.T).reshape(16, 128, 40, 128)
        .transpose(2, 1, 0, 3)).reshape(40, 128, 2048)
    W_proj = np.asarray(inputs["L0_ffn_w_proj"], dtype=np.float32)
    wproj_sl = np.ascontiguousarray(
        np.ascontiguousarray(W_proj.T).reshape(40, 128, 16, 128)
        .transpose(2, 1, 0, 3)).reshape(16, 128, 5120)
    gamma = np.zeros((128, 1), dtype=np.float32)
    gamma[:, 0] = np.asarray(inputs["L0_attn_b_o_norm_w"], dtype=np.float32)

    tq = np.asarray(inputs["L0_attn_b_to_qkv_w"], dtype=np.float32)
    kvq_sl = np.empty((16, 128, 6144), np.float32)
    for hh in range(16):
        cat = np.concatenate([tq[2048 + hh * 128:2048 + (hh + 1) * 128],
                              tq[4096 + hh * 128:4096 + (hh + 1) * 128],
                              tq[hh * 128:(hh + 1) * 128]], axis=0)
        kvq_sl[hh] = cat.reshape(384, 16, 128).transpose(2, 1, 0).reshape(128, 6144)
    w0 = np.asarray(inputs["L0_attn_b_w0"], np.float32)
    w1 = np.asarray(inputs["L0_attn_b_w1"], np.float32)
    w2 = np.asarray(inputs["L0_attn_b_w2"], np.float32)
    w012 = np.ascontiguousarray(
        np.concatenate([w0, w1, w2], axis=0).transpose(1, 0, 2)).reshape(128, 6144)
    wsmall = np.ascontiguousarray(
        np.concatenate([w0.transpose(0, 2, 1), w2.transpose(0, 2, 1), w1],
                       axis=0).transpose(1, 0, 2)).reshape(128, 6144)
    op_ = np.asarray(inputs["L0_attn_b_o_proj_w"], np.float32)
    wop = np.ascontiguousarray(
        op_.reshape(16, 128, 16, 128).transpose(0, 3, 2, 1)).reshape(16, 128, 2048)
    lrp = np.asarray(inputs["L0_attn_b_lr_proj_w"], np.float32)
    lrh = np.ascontiguousarray(
        lrp.T.reshape(16, 128, 48).transpose(1, 0, 2)).reshape(128, 768)
    identm = np.eye(128, dtype=np.float32)

    in_maps = []
    for core in range(8):
        b_, half = core // 2, core % 2
        xs = np.ascontiguousarray(x[b_, half * Th:(half + 1) * Th].T)
        x_l = np.ascontiguousarray(xs.reshape(16, 128, Th).transpose(1, 0, 2))
        prev = (np.zeros(C, dtype=np.float32) if half == 0
                else np.ascontiguousarray(x[b_, Th - 1]))
        prev_l = np.ascontiguousarray(prev.reshape(16, 128).T)
        flags = np.zeros((128, 2), dtype=np.float32)
        flags[:, 0] = float(half)
        offs8 = np.full((128, 8), (float(half) - 1.0) * 1e30, dtype=np.float32)
        in_maps.append({
            "x": x_l, "prev": prev_l, "consts": consts, "flags": flags,
            "offs8": offs8, "gamma": gamma, "ones": ones, "ones_row": ones_row,
            "wact": wact_sl, "wrec": wrec_sl, "wval": wval_sl,
            "wfc": wfc_sl, "wproj": wproj_sl,
            "wkvq": kvq_sl, "wsmall": wsmall, "w012": w012,
            "woproj": wop, "lrproj": lrh, "ident": identm,
        })
    return in_maps


def kernel(**inputs):
    x = np.asarray(inputs["L0_attn_a_x"], dtype=np.float32)
    B_, S, C = x.shape
    Th = S // 2
    in_maps = make_in_maps(inputs)
    nc = build_nc()
    res = run_bass_kernel_spmd(nc, in_maps, core_ids=list(range(8)))
    out = np.empty((B_, S, C), dtype=np.float32)
    for core in range(8):
        b_, half = core // 2, core % 2
        o = res.results[core]["out"].reshape(128, 16, Th)
        out[b_, half * Th:(half + 1) * Th] = (
            o.transpose(1, 0, 2).reshape(C, Th).T)
    return out


# version 16 (final)
# speedup vs baseline: 1.0662x; 1.0321x over previous
import contextlib
import numpy as np
import concourse.bass as bass
import concourse.mybir as mybir
from concourse.bass_utils import run_bass_kernel_spmd
from concourse.alu_op_type import AluOpType as Op

F32 = mybir.dt.float32
BF16 = mybir.dt.bfloat16
AF = mybir.ActivationFunctionType
AX = mybir.AxisListType

D = 2048
T = 2048
NCH = 16
H = 16
HD = 128
FF = 5120
XSTR = 1026
PI = float(np.pi)
NS_COEFS = [(4.0848, -6.8946, 2.927), (3.9505, -6.3029, 2.6377),
            (3.7418, -5.5913, 2.3037), (2.8769, -3.1427, 1.2046),
            (2.8366, -3.0525, 1.2012)]
BASE_LR_INV = float(0.01 + np.log(-np.expm1(-0.01)))
CC_BYPASS = False

CONSTS = {
    "half_pi": PI / 2, "half": 0.5, "neg_half": -0.5,
    "inv2pi": 1.0 / (2 * PI), "neg2pi": -2 * PI, "magic": 12582912.0,
    "quarter": 0.25, "eps_ln": 1e-5, "eps_l2": 1e-5, "eps_ns": 1e-7,
    "base_lr": BASE_LR_INV, "p7": 0.7,
    "quarter_pi": PI / 4, "t_quarter_pi": T * PI / 4,
    "neg_3half_pi": -1.5 * PI, "pi": PI, "neg_pi": -PI, "neg_one": -1.0,
    "one": 1.0, "inv_hd": 1.0 / HD,
}
CIDX = {k: i for i, k in enumerate(CONSTS)}


class Counters:
    def __init__(self):
        self.v = {}

    def bump(self, sem, n):
        self.v[sem] = self.v.get(sem, 0) + n
        return self.v[sem]

    def val(self, sem):
        return self.v.get(sem, 0)


class KB:

    def __init__(self, nc, sems, t, a):
        self.nc = nc
        self.sems = sems
        self.t = t
        self.a = a

    def on(self, eng):
        return self.eng == eng

    def wait(self, eng, sem, val=None):
        if self.on(eng):
            self.E.wait_ge(self.sems[sem], self.c.val(sem) if val is None else val)

    def dma(self, queue, out, in_, sem):
        v = self.c.bump(sem, 16)
        if self.on(queue):
            self.E.dma_start(out=out, in_=in_).then_inc(self.sems[sem], 16)
        return v

    def op(self, eng, fn, sem=None, drain=False):
        if sem is not None:
            self.c.bump(sem, 1)
        if self.on(eng):
            ins = fn(self.E)
            if sem is not None:
                if eng == "sync":
                    ins.then_inc(self.sems[sem], 1)
                else:
                    self.E.drain(fusable=True).then_inc(self.sems[sem], 1)
            elif (drain or not self.no_drain) and eng in ("vector", "scalar"):
                self.E.drain(fusable=True)

    def K(self, name):
        return self.t["consts"][:, CIDX[name]:CIDX[name] + 1]

    def K1(self, name):
        return self.t["consts"][0:1, CIDX[name]:CIDX[name] + 1]

    def emit(self, eng, E):
        self.eng = eng
        self.E = E
        self.c = Counters()
        self.ab_gate = None
        self.op_gate = {}
        self.pp01_gate = {0: None, 1: None}
        self.dw_gate = {0: None, 1: None}
        self.dw_bank = 0
        self.no_drain = True
        self.tz_pieces = self.a["tz_src"][0] + self.a["tz_src"][1]
        self.tz_p1 = 0
        self.tz_p2 = 0
        self.tz_pending = None
        self.ph_load()
        self.ph_attn_a()
        self.wait("sync", "tz_st")
        self.ph_ffn()
        self.ph_attn_b()
        self.ph_ffn()
        self.ph_store()

    def ph_load(self):
        t, a = self.t, self.a
        self.dma("sync", t["consts"][:, :], a["consts"], "dma_s")
        self.dma("sync", t["flags"][:, :], a["flags"], "dma_s")
        self.dma("sync", t["offs8"][:, :], a["offs8"], "dma_s")
        self.dma("sync", t["prev"][:, :], a["prev"], "dma_s")
        self.dma("sync", t["gamma"][:, :], a["gamma"], "dma_s")
        self.dma("sync", t["ones_f32"][:, :], a["ones"], "dma_s")
        self.dma("sync", t["ones_row"][:, :], a["ones_row"], "dma_s")
        self.dma("gpsimd", t["ones_bf"][:, :], a["ones"], "dma_g")
        self.dma("gpsimd", t["ones_row_bf"][:, :], a["ones_row"], "dma_g")
        self.dma("gpsimd", t["gamma_row_bf"][:, :], a["gamma_row"], "dma_g")
        self.dma("gpsimd", t["xres"][:, :], a["x"], "dma_g")
        for e in ("vector", "scalar", "tensor", "gpsimd", "sync"):
            self.wait(e, "dma_s")
            self.wait(e, "dma_g")
        self.op("vector", lambda E: E.memset(self.t["xln"][:, :], 0.0),
                sem="v_prog")

    def ln_stats(self, src3, x2buf):
        for t0 in (0, 1024):
            self.ln_stats_half(src3, x2buf, t0)

    def ln_stats_half(self, src3, x2buf, t0):
        t = self.t
        pp = t["pp"]
        HT = 1024
        for ci in range(NCH):
            xb = x2buf[ci % 2]
            if ci >= 2:
                self.wait("scalar", "lnx2", self.c.val("lnx2") - 1)
            self.wait("scalar", "v_prog", self.c.val("v_prog"))
            self.op("scalar", lambda E, ci=ci, xb=xb: E.activation(
                out=xb[:, 0:HT], in_=src3[:, ci, t0:t0 + HT], func=AF.Square),
                sem="s_prog")
            self.wait("tensor", "s_prog")
            for n in range(2):
                self.op("tensor", lambda E, ci=ci, n=n: E.matmul(
                    pp[0:1, n * 512:(n + 1) * 512], t["ones_bf"][:, 0:1],
                    src3[:, ci, t0 + n * 512:t0 + (n + 1) * 512],
                    start=(ci == 0), stop=(ci == NCH - 1)))
                self.op("tensor", lambda E, ci=ci, n=n, xb=xb: E.matmul(
                    pp[0:1, 1024 + n * 512:1024 + (n + 1) * 512],
                    t["ones_bf"][:, 0:1], xb[:, n * 512:(n + 1) * 512],
                    start=(ci == 0), stop=(ci == NCH - 1)),
                    sem="lnx2" if n == 1 else None)
        self.wait("vector", "lnx2")
        sc = t["SC"]
        self.op("vector", lambda E: E.tensor_scalar(
            out=sc[0:1, 0:HT], in0=pp[0:1, 0:HT], scalar1=1.0 / D, scalar2=None,
            op0=Op.mult))
        self.op("vector", lambda E: E.tensor_scalar(
            out=sc[0:1, 1024:1024 + HT], in0=pp[0:1, 1024:1024 + HT],
            scalar1=1.0 / D, scalar2=None, op0=Op.mult), sem="v_prog")
        self.wait("scalar", "v_prog")
        self.op("scalar", lambda E: E.activation(
            out=sc[0:1, 2048:2048 + HT], in_=sc[0:1, 0:HT], func=AF.Square),
            sem="s_prog")
        self.wait("vector", "s_prog")
        self.op("vector", lambda E: E.tensor_sub(
            sc[0:1, 1024:1024 + HT], sc[0:1, 1024:1024 + HT],
            sc[0:1, 2048:2048 + HT]), sem="v_prog")
        self.wait("scalar", "v_prog")
        self.op("scalar", lambda E: E.activation(
            out=sc[0:1, 1024:1024 + HT], in_=sc[0:1, 1024:1024 + HT], func=AF.Ln,
            bias=self.K1("eps_ln")))
        self.op("scalar", lambda E: E.activation(
            out=sc[0:1, 1024:1024 + HT], in_=sc[0:1, 1024:1024 + HT], func=AF.Exp,
            scale=self.K1("neg_half")), sem="s_prog")
        self.wait("vector", "s_prog")
        self.op("vector", lambda E: E.tensor_copy(
            t["stats"][0:1, t0:t0 + HT], sc[0:1, 0:HT]))
        self.op("vector", lambda E: E.tensor_copy(
            t["stats"][0:1, T + t0:T + t0 + HT], sc[0:1, 1024:1024 + HT]),
            sem="v_prog")
        self.wait("tensor", "v_prog")

    def ln_bcast(self, t0, ps_mean, ps_rstd):
        t = self.t
        self.wait("tensor", "v_prog")
        self.op("tensor", lambda E: E.matmul(
            ps_mean, t["ones_row_bf"][0:1, :], t["stats"][0:1, t0:t0 + 512],
            start=True, stop=True))
        self.op("tensor", lambda E: E.matmul(
            ps_rstd, t["ones_row_bf"][0:1, :],
            t["stats"][0:1, T + t0:T + t0 + 512], start=True, stop=True),
            sem="t_bc")
        self.wait("vector", "t_bc")

    def build_xln_tile(self, src3, nt, first_ln):
        t = self.t
        pp = t["pp"]
        x3 = t["xln"].rearrange("p (c s) -> p c s", c=NCH)
        for hf in range(2):
            t0 = nt * 1024 + hf * 512
            pm, pr = pp[:, 1024:1536], pp[:, 1536:2048]
            self.ln_bcast(t0, pm, pr)
            for ci in range(NCH):
                o = 1 + hf * 512
                self.op("vector", lambda E, ci=ci, o=o, t0=t0: E.tensor_tensor(
                    out=t["SC"][:, 0:512], in0=src3[:, ci, t0:t0 + 512],
                    in1=pm, op=Op.subtract))
                self.op("vector", lambda E, ci=ci, o=o: E.tensor_tensor(
                    out=x3[:, ci, o:o + 512], in0=t["SC"][:, 0:512],
                    in1=pr, op=Op.mult),
                    sem="v_prog" if ci == NCH - 1 else None)
        self.op("vector", lambda E: E.tensor_copy(
            t["xln_last"].rearrange("p (c o) -> p c o", o=2)[:, :, 0:1],
            x3[:, :, 1024:1025]), sem="v_prog")

    def prev_ln(self):
        t = self.t
        pp = self.t["pp"]
        sc = t["SC"]
        saved_nd = self.no_drain
        self.no_drain = False
        self.wait("vector", "s_prog", self.c.val("s_prog"))
        self.op("vector", lambda E: E.tensor_copy(t["prev_bf"][:, :],
                                                  t["prev"][:, :]), sem="v_prog")
        self.wait("scalar", "v_prog")
        self.op("scalar", lambda E: E.activation(
            out=t["prevsq_bf"][:, :], in_=t["prev"][:, :], func=AF.Square),
            sem="s_prog")
        self.wait("tensor", "s_prog")
        self.wait("tensor", "v_prog")
        self.op("tensor", lambda E: E.matmul(
            pp[0:1, 0:16], t["ones_bf"][:, 0:1], t["prev_bf"][:, :],
            start=True, stop=True))
        self.op("tensor", lambda E: E.matmul(
            pp[0:1, 512:528], t["ones_bf"][:, 0:1], t["prevsq_bf"][:, :],
            start=True, stop=True), sem="t_prog")
        self.wait("vector", "t_prog")
        self.op("vector", lambda E: E.tensor_scalar(
            out=t["redtmp"][0:1, 8:24], in0=pp[0:1, 0:16], scalar1=1.0,
            scalar2=0.0, op0=Op.mult, op1=Op.add,
            accum_out=t["redtmp"][0:1, 0:1]))
        self.op("vector", lambda E: E.tensor_copy(
            t["prev_bf"][:, :], t["prev"][:, :]))
        self.op("vector", lambda E: E.tensor_scalar(
            out=t["redtmp"][0:1, 24:40], in0=pp[0:1, 512:528], scalar1=1.0,
            scalar2=0.0, op0=Op.mult, op1=Op.add,
            accum_out=t["redtmp"][0:1, 1:2]))
        self.op("vector", lambda E: E.tensor_scalar(
            out=t["redtmp"][0:1, 2:4], in0=t["redtmp"][0:1, 0:2],
            scalar1=1.0 / D, scalar2=None, op0=Op.mult), sem="v_prog")
        self.wait("scalar", "v_prog")
        self.op("scalar", lambda E: E.activation(
            out=t["redtmp"][0:1, 4:5], in_=t["redtmp"][0:1, 2:3],
            func=AF.Square), sem="s_prog")
        self.wait("vector", "s_prog")
        self.op("vector", lambda E: E.tensor_sub(
            t["redtmp"][0:1, 3:4], t["redtmp"][0:1, 3:4],
            t["redtmp"][0:1, 4:5]), sem="v_prog")
        self.wait("scalar", "v_prog")
        self.op("scalar", lambda E: E.activation(
            out=t["redtmp"][0:1, 3:4], in_=t["redtmp"][0:1, 3:4], func=AF.Ln,
            bias=self.K1("eps_ln")), drain=True)
        self.op("scalar", lambda E: E.activation(
            out=t["redtmp"][0:1, 3:4], in_=t["redtmp"][0:1, 3:4], func=AF.Exp,
            scale=self.K1("neg_half")), sem="s_prog")
        self.wait("vector", "s_prog")
        self.op("vector", lambda E: E.tensor_copy(
            t["prevstat_bf"][0:1, :], t["redtmp"][0:1, 2:4]), sem="v_prog")
        self.wait("tensor", "v_prog")
        self.op("tensor", lambda E: E.matmul(
            pp[:, 1024:1026], t["ones_row_bf"][0:1, :], t["prevstat_bf"][0:1, :],
            start=True, stop=True), sem="t_prog")
        self.wait("vector", "t_prog")
        self.op("vector", lambda E: E.tensor_scalar(
            out=t["prev_ln"].rearrange("p (c o) -> p c o", o=2)[:, :, 0:1]
                .rearrange("p c o -> p (c o)"),
            in0=t["prev"][:, :],
            scalar1=pp[:, 1024:1025], scalar2=pp[:, 1025:1026],
            op0=Op.subtract, op1=Op.mult), sem="v_prog")
        self.no_drain = saved_nd

    def ph_attn_a(self):
        t, a = self.t, self.a
        xres3 = t["xres"].rearrange("p (c t) -> p c t", c=NCH)
        xv3 = t["pool"].rearrange("p (c t) -> p c t", c=NCH)
        x2pair = (t["pool"].rearrange("p (c t) -> p c t", c=NCH)[:, 0, :],
                  t["pool"].rearrange("p (c t) -> p c t", c=NCH)[:, 1, :])
        self.ln_stats(xres3, x2pair)
        self.prev_ln()
        self.tz_p1_gate_v = self.c.val("v_prog")
        for nt in range(2):
            self.build_xln_tile(xres3, nt, first_ln=True)
            self.wact_sweep(nt, xv3)
        self.tz_pump_p1(len(self.tz_pieces))
        self.wait("scalar", "tz_fr")
        self.tz_am()
        self.scan_phase(xv3)
        self.tz_p2_gate_v = self.c.val("v_prog")
        self.tz_p2_gate_s = self.c.val("s_prog")
        for nt in range(2):
            self.build_xln_tile(xres3, nt, first_ln=False)
            self.attn_a_epilogue(nt, xres3, xv3)
        self.tz_pump_p2(len(self.tz_pieces))
        self.tz_flush_p2()

    def wact_sweep(self, nt, xv3):
        t, a = self.t, self.a
        pp = t["pp"]
        x3 = t["xln"].rearrange("p (c s) -> p c s", c=NCH)
        self.wait("tensor", "v_prog")
        if nt == 0:
            self.pp01_gate[0] = ("v_prog", self.c.val("v_prog"))
            self.pp01_gate[1] = ("v_prog", self.c.val("v_prog"))
        for m in range(16):
            for h2 in range(2):
                g = (nt * 16 + m) * 2 + h2
                buf = t["wsl"][g % 3]
                if self.on("gpsimd") and g >= 3:
                    self.E.wait_ge(self.sems["w_fr"], g - 2)
                ldv = self.dma("gpsimd", buf[:, :],
                               a["wact"][m][:, h2 * 2048:(h2 + 1) * 2048], "w_ld")
                if h2 == 0:
                    ld0 = ldv
                else:
                    ld1 = ldv
            for n2 in range(2):
                gate = self.pp01_gate[n2]
                if gate is not None:
                    self.wait("tensor", gate[0], gate[1])
                ps = pp[:, n2 * 512:(n2 + 1) * 512]
                bnd = t["prev_ln"] if nt == 0 else t["xln_last"]
                for h2 in range(2):
                    g = (nt * 16 + m) * 2 + h2
                    buf = t["wsl"][g % 3]
                    self.wait("tensor", "w_ld", ld0 if h2 == 0 else ld1)
                    for kk in range(16):
                        k = h2 * 16 + kk
                        ci = k % 16
                        off = 0 if k < 16 else 1
                        rhs = x3[:, ci, off + n2 * 512:off + n2 * 512 + 512]
                        last = (k == 31 and n2 == 1)
                        self.op("tensor",
                                lambda E, ps=ps, kk=kk, rhs=rhs, buf=buf,
                                k=k, n2=n2: E.matmul(
                                    ps, buf[:, kk * 128:(kk + 1) * 128], rhs,
                                    start=(k == 0), stop=(k == 31 and n2 == 1)),
                                sem="t_prog" if last else None)
                if n2 == 0:
                    g0 = (nt * 16 + m) * 2
                    buf0 = t["wsl"][g0 % 3]
                    for kk in range(16):
                        self.op("tensor",
                                lambda E, ps=ps, kk=kk, buf0=buf0, bnd=bnd:
                                E.matmul(
                                    ps[:, 0:1],
                                    buf0[:, kk * 128:(kk + 1) * 128],
                                    bnd[:, 2 * kk:2 * kk + 1],
                                    start=False, stop=(kk == 15)),
                                sem="t_prog" if kk == 15 else None)
                self.wait("scalar", "t_prog")
                self.wait("scalar", "v_prog", self.c.val("v_prog"))
                self.op("scalar", lambda E, ps=ps: E.activation(
                    out=t["SC"][:, 2048:2560], in_=ps, func=AF.Sigmoid),
                    sem="s_prog")
                self.pp01_gate[n2] = ("s_prog", self.c.val("s_prog"))
                self.wait("vector", "s_prog")
                self.op("vector", lambda E, m=m, nt=nt, n2=n2: E.tensor_scalar(
                    out=xv3[:, m, nt * 1024 + n2 * 512:nt * 1024 + n2 * 512 + 512],
                    in0=t["SC"][:, 2048:2560], scalar1=0.5, scalar2=None,
                    op0=Op.max), sem="v_prog")
            self.op("tensor", lambda E: E.nop(), sem="w_fr")
            self.op("tensor", lambda E: E.nop(), sem="w_fr")
            self.tz_pump_p1(5)

    def chunk_s_theta(self, xv3, c8):
        t = self.t
        sc = t["SC"]
        sc1, sc2, sc3 = sc[:, 0:T], sc[:, 2048:2048 + T], sc[:, 4096:4096 + T]
        self.wait("scalar", "v_prog", self.c.val("v_prog"))
        self.op("scalar", lambda E, c8=c8: E.activation(
            out=sc1, in_=xv3[:, c8, :], func=AF.Square))
        self.op("scalar", lambda E, c8=c8: E.activation(
            out=sc2, in_=xv3[:, 8 + c8, :], func=AF.Square), sem="s_prog")
        self.wait("vector", "s_prog")
        self.op("vector", lambda E: E.tensor_add(sc1, sc1, sc2))
        self.op("vector", lambda E, c8=c8: E.tensor_sub(
            sc2, xv3[:, 8 + c8, :], xv3[:, c8, :]))
        self.op("vector", lambda E, c8=c8: E.tensor_add(
            sc3, xv3[:, 8 + c8, :], xv3[:, c8, :]), sem="v_prog")
        self.wait("scalar", "v_prog")
        self.op("scalar", lambda E: E.activation(out=sc1, in_=sc1, func=AF.Ln))
        self.op("scalar", lambda E: E.activation(out=sc3, in_=sc3, func=AF.Ln))
        self.op("scalar", lambda E: E.activation(
            out=sc3, in_=sc3, func=AF.Exp, scale=self.K("neg_one")), sem="s_prog")
        self.wait("vector", "s_prog")
        self.op("vector", lambda E: E.tensor_mul(sc2, sc2, sc3), sem="v_prog")
        self.wait("scalar", "v_prog")
        self.op("scalar", lambda E: E.activation(
            out=sc2, in_=sc2, func=AF.Arctan), sem="s_prog")
        self.wait("vector", "s_prog")

    def scan_phase(self, xv3):
        t, a = self.t, self.a
        sc = t["SC"]
        sc1, sc2, sc3 = sc[:, 0:T], sc[:, 2048:2048 + T], sc[:, 4096:4096 + T]
        for c8 in range(8):
            self.chunk_s_theta(xv3, c8)
            self.op("vector", lambda E, c8=c8: E.tensor_reduce(
                out=t["summ"][:, 16 + c8:17 + c8], in_=sc2, axis=AX.X, op=Op.add))
            self.op("vector", lambda E, c8=c8: E.tensor_scalar(
                out=t["summ"][:, 16 + c8:17 + c8],
                in0=t["summ"][:, 16 + c8:17 + c8],
                scalar1=self.K("t_quarter_pi"), scalar2=None, op0=Op.add))
            self.op("vector", lambda E: E.tensor_tensor_scan(
                out=sc3, data0=sc1, data1=sc1, initial=-1e30, op0=Op.max,
                op1=Op.bypass))
            self.op("vector", lambda E, c8=c8: E.tensor_copy(
                t["summ"][:, c8:c8 + 1], sc3[:, T - 1:T]))
            self.op("vector", lambda E, c8=c8: E.scalar_tensor_tensor(
                out=sc2, in0=sc1, scalar=1.0, in1=sc3, op0=Op.mult,
                op1=Op.subtract, accum_out=t["summ"][:, 8 + c8:9 + c8]),
                sem="v_prog")
        self.wait("gpsimd", "v_prog")
        self.dma("gpsimd", a["cc_ag_in"], t["summ"][:, 0:24], "dma_g")
        self.wait("gpsimd", "dma_g")
        if CC_BYPASS:
            self.dma("gpsimd", t["peers"][:, 0:24], a["cc_ag_in"], "dma_g")
        else:
            ccv = self.c.bump("cc", 1)
            if self.on("gpsimd"):
                self.E.collective_compute(
                    "AllGather", Op.bypass,
                    replica_groups=[[0, 1], [2, 3], [4, 5], [6, 7]],
                    ins=[a["cc_ag_in"]], outs=[a["cc_ag_out"]],
                ).then_inc(self.sems["cc"], 1)
            self.wait("gpsimd", "cc", ccv)
            self.dma("gpsimd", t["peers"][:, 0:24], a["cc_ag_out_r0"], "dma_g")
        self.wait("vector", "dma_g")
        self.op("vector", lambda E: E.scalar_tensor_tensor(
            out=t["carry"][:, 0:8], in0=t["peers"][:, 0:8],
            scalar=t["flags"][:, 0:1], in1=t["offs8"][:, 0:8],
            op0=Op.mult, op1=Op.add))
        self.op("vector", lambda E: E.tensor_scalar(
            out=t["carry"][:, 8:24], in0=t["peers"][:, 8:24],
            scalar1=t["flags"][:, 0:1], scalar2=None, op0=Op.mult), sem="v_prog")
        for c8 in range(8):
            self.chunk_s_theta(xv3, c8)
            self.op("vector", lambda E: E.tensor_tensor_scan(
                out=sc3, data0=sc1, data1=sc1, initial=-1e30, op0=Op.max,
                op1=Op.bypass))
            self.op("vector", lambda E, c8=c8: E.tensor_scalar(
                out=sc3, in0=sc3, scalar1=t["carry"][:, c8:c8 + 1],
                scalar2=None, op0=Op.max))
            self.op("vector", lambda E: E.tensor_sub(sc1, sc1, sc3))
            self.op("vector", lambda E, c8=c8: E.tensor_tensor_scan(
                out=sc1, data0=sc1, data1=sc1,
                initial=t["carry"][:, 8 + c8:9 + c8], op0=Op.add, op1=Op.bypass))
            self.op("vector", lambda E, c8=c8: E.tensor_tensor_scan(
                out=sc2, data0=sc2,
                data1=self.K("quarter_pi").broadcast_to((128, T)),
                initial=t["carry"][:, 16 + c8:17 + c8], op0=Op.add,
                op1=Op.add), sem="v_prog")
            self.wait("scalar", "v_prog")
            self.op("scalar", lambda E: E.activation(
                out=sc1, in_=sc1, func=AF.Exp, scale=self.K("half")),
                sem="s_prog")
            self.op("vector", lambda E: E.tensor_scalar(
                out=sc3, in0=sc2, scalar1=self.K("inv2pi"),
                scalar2=self.K("magic"), op0=Op.mult, op1=Op.add))
            self.op("vector", lambda E: E.tensor_scalar(
                out=sc3, in0=sc3, scalar1=self.K("magic"), scalar2=None,
                op0=Op.subtract))
            self.op("vector", lambda E: E.scalar_tensor_tensor(
                out=sc3, in0=sc3, scalar=self.K("neg2pi"), in1=sc2,
                op0=Op.mult, op1=Op.add))
            self.op("vector", lambda E: E.tensor_scalar(
                out=sc3, in0=sc3, scalar1=self.K("pi"), scalar2=self.K("neg_pi"),
                op0=Op.min, op1=Op.max), sem="v_prog")
            self.wait("scalar", "v_prog")
            self.op("scalar", lambda E: E.activation(
                out=sc3, in_=sc3, func=AF.Sin), sem="s_prog")
            self.wait("vector", "s_prog")
            self.op("vector", lambda E, c8=c8: E.tensor_tensor(
                out=xv3[:, 8 + c8, :], in0=sc1, in1=sc3, op=Op.mult),
                sem="v_prog")
            self.op("vector", lambda E: E.tensor_scalar(
                out=sc3, in0=sc2, scalar1=self.K("inv2pi"),
                scalar2=self.K("quarter"), op0=Op.mult, op1=Op.add))
            self.op("vector", lambda E: E.tensor_scalar(
                out=sc3, in0=sc3, scalar1=self.K("magic"),
                scalar2=self.K("magic"), op0=Op.add, op1=Op.subtract))
            self.op("vector", lambda E: E.scalar_tensor_tensor(
                out=sc3, in0=sc3, scalar=self.K("neg2pi"), in1=sc2,
                op0=Op.mult, op1=Op.add))
            self.op("vector", lambda E: E.tensor_scalar(
                out=sc3, in0=sc3, scalar1=self.K("half_pi"),
                scalar2=self.K("neg_3half_pi"), op0=Op.min, op1=Op.max),
                sem="v_prog")
            self.wait("scalar", "v_prog")
            self.op("scalar", lambda E: E.activation(
                out=sc3, in_=sc3, func=AF.Sin, bias=self.K("half_pi")),
                sem="s_prog")
            self.wait("vector", "s_prog")
            self.op("vector", lambda E, c8=c8: E.tensor_tensor(
                out=xv3[:, c8, :], in0=sc1, in1=sc3, op=Op.mult), sem="v_prog")

    def attn_a_epilogue(self, nt, xres3, xv3):
        t, a = self.t, self.a
        pp = t["pp"]
        x3 = t["xln"].rearrange("p (c s) -> p c s", c=NCH)
        wv = t["SC"][:, :].bitcast(BF16)[:, 8192:8192 + 2048]
        t0 = nt * 1024
        self.wait("gpsimd", "v_prog")
        self.wait("tensor", "v_prog")
        for m in range(16):
            for h2 in range(2):
                g = (nt * 16 + m) * 2 + h2
                buf = t["wsl"][g % 3]
                if self.on("gpsimd") and g >= 3:
                    self.E.wait_ge(self.sems["wr_fr"], g - 2)
                ldv = self.dma("gpsimd", buf[:, :],
                               a["wrec"][m][:, h2 * 2048:(h2 + 1) * 2048], "wr_ld")
                if h2 == 0:
                    ld0 = ldv
                else:
                    ld1 = ldv
            gv = nt * 16 + m
            if self.on("gpsimd") and gv >= 1:
                self.E.wait_ge(self.sems["wv_fr"], gv)
            ldv2 = self.dma("gpsimd", wv, a["wval"][m], "wv_ld")
            for n2 in range(2):
                gate = self.pp01_gate[0]
                if gate is not None:
                    self.wait("tensor", gate[0], gate[1])
                ps_r = pp[:, 0:512]
                ps_o = pp[:, 512:1024]
                bnd = t["prev_ln"] if nt == 0 else t["xln_last"]
                for h2 in range(2):
                    g = (nt * 16 + m) * 2 + h2
                    buf = t["wsl"][g % 3]
                    self.wait("tensor", "wr_ld", ld0 if h2 == 0 else ld1)
                    for kk in range(16):
                        k = h2 * 16 + kk
                        ci = k % 16
                        off = 0 if k < 16 else 1
                        rhs = x3[:, ci, off + n2 * 512:off + n2 * 512 + 512]
                        last = (k == 31 and n2 == 1)
                        self.op("tensor",
                                lambda E, ps_r=ps_r, kk=kk, rhs=rhs, buf=buf,
                                k=k, n2=n2: E.matmul(
                                    ps_r, buf[:, kk * 128:(kk + 1) * 128], rhs,
                                    start=(k == 0), stop=(k == 31 and n2 == 1)),
                                sem="t_prog" if last else None)
                if n2 == 0:
                    g0 = (nt * 16 + m) * 2
                    buf0 = t["wsl"][g0 % 3]
                    for kk in range(16):
                        self.op("tensor",
                                lambda E, ps_r=ps_r, kk=kk, buf0=buf0, bnd=bnd:
                                E.matmul(
                                    ps_r[:, 0:1],
                                    buf0[:, kk * 128:(kk + 1) * 128],
                                    bnd[:, 2 * kk:2 * kk + 1],
                                    start=False, stop=(kk == 15)),
                                sem="t_prog" if kk == 15 else None)
                self.wait("tensor", "wv_ld", ldv2)
                gate = self.pp01_gate[1]
                if gate is not None:
                    self.wait("tensor", gate[0], gate[1])
                for k in range(16):
                    rhs = xv3[:, k, t0 + n2 * 512:t0 + n2 * 512 + 512]
                    self.op("tensor",
                            lambda E, ps_o=ps_o, k=k, rhs=rhs: E.matmul(
                                ps_o, wv[:, k * 128:(k + 1) * 128], rhs,
                                start=(k == 0), stop=(k == 15)),
                            sem="t_prog" if k == 15 else None)
                self.wait("scalar", "t_prog")
                self.wait("scalar", "v_prog", self.c.val("v_prog"))
                self.op("scalar", lambda E, ps_r=ps_r: E.activation(
                    out=t["SC"][:, 0:512], in_=ps_r, func=AF.Sigmoid),
                    sem="s_prog")
                self.pp01_gate[0] = ("s_prog", self.c.val("s_prog"))
                self.wait("vector", "s_prog")
                self.op("vector", lambda E: E.tensor_scalar(
                    out=t["SC"][:, 0:512], in0=t["SC"][:, 0:512], scalar1=0.5,
                    scalar2=None, op0=Op.max))
                self.op("vector", lambda E: E.tensor_mul(
                    t["SC"][:, 0:512], t["SC"][:, 0:512], t["SC"][:, 0:512]))
                self.op("vector", lambda E: E.tensor_mul(
                    t["SC"][:, 0:512], t["SC"][:, 0:512], t["SC"][:, 0:512]))
                self.op("vector", lambda E, ps_o=ps_o: E.tensor_tensor(
                    out=t["SC"][:, 2048:2560], in0=ps_o, in1=t["SC"][:, 0:512],
                    op=Op.mult))
                self.op("vector", lambda E, m=m, t0=t0, n2=n2: E.tensor_tensor(
                    out=xres3[:, m, t0 + n2 * 512:t0 + n2 * 512 + 512],
                    in0=xres3[:, m, t0 + n2 * 512:t0 + n2 * 512 + 512],
                    in1=t["SC"][:, 2048:2560], op=Op.add), sem="v_prog")
                self.pp01_gate[1] = ("v_prog", self.c.val("v_prog"))
            self.op("tensor", lambda E: E.nop(), sem="wr_fr")
            self.op("tensor", lambda E: E.nop(), sem="wr_fr")
            self.op("tensor", lambda E: E.nop(), sem="wv_fr")
            self.tz_pump_p2(5)

    def tz_pump_p1(self, n):
        t, a = self.t, self.a
        sc = t["SC"]
        SG = (sc[:, 512:1536], sc[:, 2560:3584])
        for _ in range(n):
            i = self.tz_p1
            if i >= len(self.tz_pieces):
                return
            self.tz_p1 += 1
            src_ap, _dst, w = self.tz_pieces[i]
            if i < 2:
                self.wait("sync", "v_prog", self.tz_p1_gate_v)
            else:
                self.wait("sync", "tz_fr", i - 1)
            ldv = self.dma("sync", SG[i % 2], src_ap, "tz_ld")
            self.wait("vector", "tz_ld", ldv)
            self.op("vector", lambda E, i=i, b=SG[i % 2]: E.tensor_reduce(
                out=t["tzsl"][:, i:i + 1], in_=b, axis=AX.X, op=Op.add,
                apply_absolute_value=True), sem="tz_fr")

    def tz_am(self):
        t, a = self.t, self.a
        pp = t["pp"]
        n0 = len(a["tz_src"][0])
        spans = [(0, n0), (n0, len(self.tz_pieces))]
        stg = t["SC"][:, 2560:2656].bitcast(BF16)
        self.wait("vector", "tz_fr")
        npc = len(self.tz_pieces)
        self.op("vector", lambda E, npc=npc: E.tensor_copy(
            stg[:, 0:npc], t["tzsl"][:, 0:npc]), sem="v_prog")
        self.wait("tensor", "v_prog")
        for wi, (lo, hi) in enumerate(spans):
            self.op("tensor", lambda E, wi=wi, lo=lo, hi=hi: E.matmul(
                pp[0:1, 2048 + wi * 512:2048 + wi * 512 + (hi - lo)],
                t["ones_bf"][:, 0:1], stg[:, lo:hi], start=True, stop=True),
                sem="t_prog" if wi == 1 else None)
        self.wait("vector", "t_prog")
        for wi, (lo, hi) in enumerate(spans):
            nel = float(a["tz_nel"][wi])
            self.op("vector", lambda E, wi=wi, lo=lo, hi=hi: E.tensor_scalar(
                out=t["SC"][0:1, 512:512 + (hi - lo)],
                in0=pp[0:1, 2048 + wi * 512:2048 + wi * 512 + (hi - lo)],
                scalar1=1.0, scalar2=0.0, op0=Op.mult, op1=Op.add,
                accum_out=t["redtmp"][0:1, 5 + wi:6 + wi]), drain=True)
            self.op("vector", lambda E: E.memset(
                t["SC"][0:1, 1024:1536], 0.0))
            self.op("vector", lambda E, wi=wi, nel=nel: E.tensor_scalar(
                out=t["redtmp"][0:1, 7 + wi:8 + wi],
                in0=t["redtmp"][0:1, 5 + wi:6 + wi],
                scalar1=1.0 / nel, scalar2=1e-5, op0=Op.mult, op1=Op.max),
                drain=True)
            self.op("vector", lambda E: E.memset(
                t["SC"][0:1, 1024:1536], 0.0))
            self.op("vector", lambda E, wi=wi: E.tensor_scalar(
                out=t["redtmp"][0:1, 44 + 3 * wi:45 + 3 * wi],
                in0=t["redtmp"][0:1, 7 + wi:8 + wi],
                scalar1=-0.7, scalar2=None, op0=Op.mult), drain=True)
            self.op("vector", lambda E, wi=wi: E.tensor_scalar(
                out=t["redtmp"][0:1, 45 + 3 * wi:46 + 3 * wi],
                in0=t["redtmp"][0:1, 7 + wi:8 + wi],
                scalar1=0.7, scalar2=None, op0=Op.mult), drain=True)
            self.op("vector", lambda E, wi=wi: E.tensor_scalar(
                out=t["redtmp"][0:1, 46 + 3 * wi:47 + 3 * wi],
                in0=t["redtmp"][0:1, 7 + wi:8 + wi],
                scalar1=1.0, scalar2=None, op0=Op.mult),
                sem="v_prog" if wi == 1 else None, drain=True)
        self.wait("tensor", "v_prog")
        self.op("tensor", lambda E: E.matmul(
            pp[:, 1024:1030], t["ones_row"][0:1, :],
            t["redtmp"][0:1, 44:50], start=True, stop=True), sem="t_prog")
        self.wait("vector", "t_prog")
        self.op("vector", lambda E: E.tensor_copy(
            t["bc4"][:, 0:3], pp[:, 1024:1027]))
        self.op("vector", lambda E: E.tensor_copy(
            t["bc6"][:, 0:3], pp[:, 1027:1030]), sem="v_prog")
        self.tz_am_t = self.c.val("t_prog")

    def tz_pump_p2(self, n):
        t, a = self.t, self.a
        sc = t["SC"]
        SG = (sc[:, 512:1536], sc[:, 2560:3584])
        n0 = len(a["tz_src"][0])
        for _ in range(n):
            i = self.tz_p2
            if i >= len(self.tz_pieces):
                return
            self.tz_p2 += 1
            src_ap, dst_ap, w = self.tz_pieces[i]
            bcs = t["bc4"] if i < n0 else t["bc6"]
            if i < 2:
                self.wait("sync", "v_prog", self.tz_p2_gate_v)
                self.wait("sync", "s_prog", self.tz_p2_gate_s)
                self.wait("sync", "t_prog", self.tz_am_t)
            else:
                self.wait("sync", "tz_st", (i - 1) * 16)
            ldv = self.dma("sync", SG[i % 2], src_ap, "tz_ld")
            if self.tz_pending is not None:
                pi, pdst, pres = self.tz_pending
                self.wait("sync", "tz_v", pi + 1)
                self.dma("sync", pdst, pres, "tz_st")
                self.tz_pending = None
            self.wait("vector", "tz_ld", ldv)
            sb0 = sc[:, 3584:4096].bitcast(BF16)
            sb1 = sc[:, 1536:2048].bitcast(BF16)
            self.op("vector", lambda E, b=SG[i % 2], sb0=sb0, bcs=bcs:
                    E.tensor_scalar(out=sb0, in0=b, scalar1=bcs[:, 1:2],
                                    scalar2=bcs[:, 2:3], op0=Op.is_gt,
                                    op1=Op.mult))
            self.op("vector", lambda E, b=SG[i % 2], sb1=sb1, bcs=bcs:
                    E.tensor_scalar(out=sb1, in0=b, scalar1=bcs[:, 0:1],
                                    scalar2=bcs[:, 2:3], op0=Op.is_lt,
                                    op1=Op.mult))
            res = SG[i % 2].bitcast(BF16)[:, 0:1024]
            self.op("vector", lambda E, res=res, sb0=sb0, sb1=sb1:
                    E.tensor_sub(res, sb0, sb1), sem="tz_v")
            self.tz_pending = (i, dst_ap, res)

    def tz_flush_p2(self):
        if self.tz_pending is not None:
            pi, pdst, pres = self.tz_pending
            self.wait("sync", "tz_v", pi + 1)
            self.dma("sync", pdst, pres, "tz_st")
            self.tz_pending = None

    def phase_barrier(self):
        for e in ("vector", "scalar", "tensor", "gpsimd", "sync"):
            for s in ("v_prog", "s_prog", "t_prog"):
                self.wait(e, s)

    def ph_ternarize(self):
        t, a = self.t, self.a
        pp = t["pp"]
        sc = t["SC"]
        self.phase_barrier()
        for wi in range(2):
            pieces = a["tz_src"][wi]
            nel = float(a["tz_nel"][wi])
            for s_i, (src_ap, _dst, w) in enumerate(pieces):
                third = s_i % 3
                scv = sc[:, third * 2048:third * 2048 + w]
                if s_i >= 3:
                    self.wait("sync", "tz_fr", self.c.val("tz_fr") - 2)
                ldv = self.dma("sync", scv, src_ap, "tz_ld")
                self.wait("scalar", "tz_ld", ldv)
                self.wait("scalar", "t_prog", self.c.val("t_prog"))
                self.op("scalar", lambda E, scv=scv, w=w: E.activation(
                    out=t["wsl"][0][:, 0:w], in_=scv, func=AF.Abs), sem="s_prog")
                self.op("scalar", lambda E: E.nop(), sem="tz_fr")
                self.wait("tensor", "s_prog")
                for n in range(w // 512):
                    self.op("tensor", lambda E, n=n: E.matmul(
                        pp[0:1, (n % 4) * 512:(n % 4) * 512 + 512],
                        t["ones_bf"][:, 0:1],
                        t["wsl"][0][:, n * 512:(n + 1) * 512],
                        start=True, stop=True),
                        sem="t_prog" if n == w // 512 - 1 else None)
                self.wait("vector", "t_prog")
                self.op("vector", lambda E, w=w, s_i=s_i: E.tensor_scalar(
                    out=t["pool"][0:1, 0:2 * w].bitcast(F32),
                    in0=pp[0:1, 0:w], scalar1=1.0, scalar2=0.0,
                    op0=Op.mult, op1=Op.add,
                    accum_out=t["redtmp"][0:1, 50 + s_i:51 + s_i]), sem="v_prog")
                self.wait("tensor", "v_prog", self.c.val("v_prog"))
            npc = len(pieces)
            self.op("vector", lambda E, npc=npc: E.tensor_reduce(
                out=t["redtmp"][0:1, 5:6], in_=t["redtmp"][0:1, 50:50 + npc],
                axis=AX.X, op=Op.add), drain=True)
            self.op("vector", lambda E: E.memset(
                t["pool"][0:1, 0:512], 0.0))
            self.op("vector", lambda E, nel=nel: E.tensor_scalar(
                out=t["redtmp"][0:1, 7:8], in0=t["redtmp"][0:1, 5:6],
                scalar1=1.0 / nel, scalar2=1e-5, op0=Op.mult, op1=Op.max),
                drain=True)
            self.op("vector", lambda E: E.memset(
                t["pool"][0:1, 0:512], 0.0))
            self.op("vector", lambda E: E.tensor_scalar(
                out=t["redtmp"][0:1, 44:45], in0=t["redtmp"][0:1, 7:8],
                scalar1=-0.7, scalar2=None, op0=Op.mult), drain=True)
            self.op("vector", lambda E: E.tensor_scalar(
                out=t["redtmp"][0:1, 45:46], in0=t["redtmp"][0:1, 7:8],
                scalar1=0.7, scalar2=None, op0=Op.mult), drain=True)
            self.op("vector", lambda E: E.tensor_scalar(
                out=t["redtmp"][0:1, 46:47], in0=t["redtmp"][0:1, 7:8],
                scalar1=0.5, scalar2=None, op0=Op.mult), sem="v_prog")
            self.wait("tensor", "v_prog")
            self.op("tensor", lambda E: E.matmul(
                pp[:, 1024:1027], t["ones_row"][0:1, :],
                t["redtmp"][0:1, 44:47], start=True, stop=True), sem="t_prog")
            self.wait("vector", "t_prog")
            self.op("vector", lambda E: E.tensor_copy(
                t["bc4"][:, 0:3], pp[:, 1024:1027]), sem="v_prog")
            self.wait("scalar", "v_prog")
            self.wait("sync", "t_prog", self.c.val("t_prog"))
            for s_i, (src_ap, dst_ap, w) in enumerate(pieces):
                third = s_i % 3
                scv = sc[:, third * 2048:third * 2048 + w]
                if s_i >= 3:
                    self.wait("sync", "tz_fr", self.c.val("tz_fr") - 2)
                ldv = self.dma("sync", scv, src_ap, "tz_ld")
                self.wait("scalar", "tz_ld", ldv)
                self.wait("scalar", "v_prog", self.c.val("v_prog"))
                self.op("scalar", lambda E, scv=scv, w=w: E.sign(
                    out=t["wsl"][0][:, 0:w], in_=scv, bias=t["bc4"][:, 0:1]))
                self.op("scalar", lambda E, scv=scv, w=w: E.sign(
                    out=t["wsl"][1][:, 0:w], in_=scv, bias=t["bc4"][:, 1:2]),
                    sem="s_prog")
                self.wait("vector", "s_prog")
                self.op("vector", lambda E, w=w: E.tensor_add(
                    t["wsl"][2][:, 0:w], t["wsl"][0][:, 0:w], t["wsl"][1][:, 0:w]))
                self.op("vector", lambda E, w=w: E.tensor_scalar(
                    out=t["wsl"][2][:, 0:w], in0=t["wsl"][2][:, 0:w],
                    scalar1=t["bc4"][:, 2:3], scalar2=None, op0=Op.mult),
                    sem="v_prog")
                self.wait("gpsimd", "v_prog")
                stv = self.dma("gpsimd", dst_ap, t["wsl"][2][:, 0:w], "tz_st")
                self.wait("scalar", "tz_st", stv)
                self.op("scalar", lambda E: E.nop(), sem="tz_fr")
            self.wait("sync", "tz_st")

    def ph_ffn(self):
        t, a = self.t, self.a
        pp = t["pp"]
        xres3 = t["xres"].rearrange("p (c t) -> p c t", c=NCH)
        x3 = t["xln"].rearrange("p (c s) -> p c s", c=NCH)
        pool = t["pool"]
        wp = [pool[:, 20480 + i * 5120:20480 + (i + 1) * 5120] for i in range(2)]
        self.phase_barrier()
        self.ln_stats(xres3, (pool.rearrange("p (c t) -> p c t", c=NCH)[:, 0, :],
                              pool.rearrange("p (c t) -> p c t", c=NCH)[:, 1, :]))
        for nt in range(4):
            t0 = nt * 512
            self.ln_bcast(t0, pp[:, 1024:1536], pp[:, 1536:2048])
            for ci in range(NCH):
                self.op("vector", lambda E, ci=ci, t0=t0: E.tensor_tensor(
                    out=t["SC"][:, 0:512], in0=xres3[:, ci, t0:t0 + 512],
                    in1=pp[:, 1024:1536], op=Op.subtract))
                self.op("vector", lambda E, ci=ci: E.tensor_tensor(
                    out=x3[:, ci, 0:512], in0=t["SC"][:, 0:512],
                    in1=pp[:, 1536:2048], op=Op.mult),
                    sem="v_prog" if ci == NCH - 1 else None)
            self.wait("tensor", "v_prog")
            for m in range(40):
                g = self.c.bump("f_seq", 1) - 1
                buf = t["wsl"][g % 3]
                if self.on("sync") and g >= 3:
                    self.E.wait_ge(self.sems["f_fr"], g - 2)
                ldv = self.dma("sync", buf[:, :], a["wfc_t_sl"][m], "f_ld")
                self.wait("tensor", "f_ld", ldv)
                bk = m % 2
                gate = self.pp01_gate[bk]
                if gate is not None:
                    self.wait("tensor", gate[0], gate[1])
                ps = pp[:, bk * 512:bk * 512 + 512]
                for k in range(16):
                    self.op("tensor", lambda E, ps=ps, k=k, buf=buf: E.matmul(
                        ps, buf[:, k * 128:(k + 1) * 128],
                        x3[:, k, 0:512], start=(k == 0), stop=(k == 15)),
                        sem="t_prog" if k == 15 else None)
                self.op("tensor", lambda E: E.nop(), sem="f_fr")
                self.wait("scalar", "t_prog")
                self.wait("scalar", "v_prog", self.c.val("v_prog"))
                self.op("scalar", lambda E, ps=ps: E.activation(
                    out=t["SC"][:, 2048:2560], in_=ps, func=AF.Relu),
                    sem="s_prog")
                self.pp01_gate[bk] = ("s_prog", self.c.val("s_prog"))
                self.wait("vector", "s_prog")
                self.op("vector", lambda E, m=m: E.tensor_mul(
                    pool[:, m * 512:(m + 1) * 512], t["SC"][:, 2048:2560],
                    t["SC"][:, 2048:2560]), sem="v_prog")
            self.wait("tensor", "v_prog")
            for mo in range(16):
                g = self.c.bump("fp_seq", 1) - 1
                buf = wp[g % 2]
                if self.on("sync") and g >= 2:
                    self.E.wait_ge(self.sems["fp_fr"], g - 1)
                ldv = self.dma("sync", buf, a["wproj_t_sl"][mo], "fp_ld")
                self.wait("tensor", "fp_ld", ldv)
                bk = mo % 2
                gate = self.pp01_gate[bk]
                if gate is not None:
                    self.wait("tensor", gate[0], gate[1])
                ps = pp[:, bk * 512:bk * 512 + 512]
                for k in range(40):
                    self.op("tensor", lambda E, ps=ps, k=k, buf=buf: E.matmul(
                        ps, buf[:, k * 128:(k + 1) * 128],
                        pool[:, k * 512:(k + 1) * 512],
                        start=(k == 0), stop=(k == 39)),
                        sem="t_prog" if k == 39 else None)
                self.op("tensor", lambda E: E.nop(), sem="fp_fr")
                self.wait("vector", "t_prog")
                self.op("vector", lambda E, mo=mo, t0=t0, ps=ps: E.tensor_tensor(
                    out=xres3[:, mo, t0:t0 + 512],
                    in0=xres3[:, mo, t0:t0 + 512], in1=ps, op=Op.add),
                    sem="v_prog")
                self.pp01_gate[bk] = ("v_prog", self.c.val("v_prog"))

    def ph_attn_b(self):
        t, a = self.t, self.a
        pp = t["pp"]
        sc = t["SC"]
        xres3 = t["xres"].rearrange("p (c t) -> p c t", c=NCH)
        x3 = t["xln"].rearrange("p (c s) -> p c s", c=NCH)
        pool = t["pool"]
        SLAB = pool[:, 0:6144]
        WSM = pool[:, 6144:12288]
        KTM = pool[:, 12288:13312]
        VTM = pool[:, 13312:14336]
        KFM = pool[:, 14336:15360]
        VFM = pool[:, 15360:16384]
        SCR = [pool[:, 16384 + i * 1024:16384 + (i + 1) * 1024] for i in range(4)]
        DWF = pool[:, 20480:32768].bitcast(F32)
        self.phase_barrier()
        self.dma("gpsimd", WSM, a["wsmall"], "dma_g")
        self.dma("gpsimd", t["ident"][:, :], a["ident"], "dma_g")
        self.dma("gpsimd", t["pool"][:, 19456:20224], a["lrproj"], "dma_g")
        self.wait("tensor", "dma_g")
        self.wait("vector", "dma_g")
        self.op("vector", lambda E: E.memset(DWF, 0.0), sem="v_prog")
        self.ln_stats(xres3, (SCR[0], SCR[1]))
        for tile in range(2):
            t0 = tile * 1024
            self.build_xln_ab(xres3, x3, t0)
            self.lr_tile(x3)
            for h in range(H):
                self.ab_head_ph1(tile, h, x3)
        self.wait("gpsimd", "v_prog")
        self.dma("gpsimd", a["cc_ar_in"], DWF, "dma_g")
        self.wait("gpsimd", "dma_g")
        if CC_BYPASS:
            self.dma("gpsimd", DWF, a["cc_ar_in"], "dma_g")
        else:
            ccv = self.c.bump("cc", 1)
            if self.on("gpsimd"):
                self.E.collective_compute(
                    "AllReduce", Op.add,
                    replica_groups=[[0, 1], [2, 3], [4, 5], [6, 7]],
                    ins=[a["cc_ar_in"]], outs=[a["cc_ar_out"]],
                ).then_inc(self.sems["cc"], 1)
            self.wait("gpsimd", "cc", ccv)
            self.dma("gpsimd", DWF, a["cc_ar_out"], "dma_g")
        self.wait("vector", "dma_g")
        self.wait("scalar", "dma_g")
        self.wait("tensor", "dma_g")
        self.ns_renorm()
        for tile in range(2):
            t0 = tile * 1024
            for h in range(H):
                self.ab_head_ph2(tile, h)
            self.ab_oproj(tile, xres3)

    def build_xln_ab(self, xres3, x3, t0):
        t = self.t
        pp = t["pp"]
        for hf in range(2):
            tt = t0 + hf * 512
            self.ln_bcast(tt, pp[:, 1024:1536], pp[:, 1536:2048])
            for ci in range(NCH):
                self.op("vector", lambda E, ci=ci, tt=tt: E.tensor_tensor(
                    out=t["SC"][:, 0:512], in0=xres3[:, ci, tt:tt + 512],
                    in1=pp[:, 1024:1536], op=Op.subtract))
                self.op("vector", lambda E, ci=ci, hf=hf: E.tensor_tensor(
                    out=x3[:, ci, hf * 512:hf * 512 + 512], in0=t["SC"][:, 0:512],
                    in1=pp[:, 1536:2048], op=Op.mult),
                    sem="v_prog" if ci == NCH - 1 else None)
        self.wait("tensor", "v_prog")

    def lr_tile(self, x3):
        t = self.t
        pp = t["pp"]
        for tc in range(8):
            ps = pp[:, (tc % 4) * 512:(tc % 4) * 512 + 48]
            for k in range(NCH):
                self.op("tensor", lambda E, ps=ps, k=k, tc=tc: E.matmul(
                    ps, x3[:, k, tc * 128:(tc + 1) * 128],
                    t["pool"][:, 19456 + k * 48:19456 + (k + 1) * 48],
                    start=(k == 0), stop=(k == NCH - 1)),
                    sem="t_prog" if k == NCH - 1 else None)
            self.wait("scalar", "t_prog")
            self.wait("scalar", "v_prog", self.c.val("v_prog"))
            lt = t["SC"][:, 5248 + tc * 48:5248 + (tc + 1) * 48]
            self.op("scalar", lambda E, ps=ps: E.activation(
                out=t["SC"][:, 0:48], in_=ps, func=AF.Abs,
                bias=self.K("base_lr")), drain=True)
            self.op("scalar", lambda E: E.activation(
                out=t["SC"][:, 0:48], in_=t["SC"][:, 0:48], func=AF.Exp,
                scale=self.K("neg_one")), drain=True)
            self.op("scalar", lambda E: E.activation(
                out=t["SC"][:, 0:48], in_=t["SC"][:, 0:48], func=AF.Ln,
                bias=self.K("one")), sem="s_prog")
            self.wait("vector", "s_prog")
            self.op("vector", lambda E, ps=ps: E.tensor_scalar(
                out=t["SC"][:, 64:112], in0=ps, scalar1=self.K("base_lr"),
                scalar2=0.0, op0=Op.add, op1=Op.max), drain=True)
            self.op("vector", lambda E, lt=lt: E.tensor_add(
                lt, t["SC"][:, 0:48], t["SC"][:, 64:112]), sem="v_prog")
            self.wait("tensor", "v_prog", self.c.val("v_prog"))

    def ab_head_ph1(self, tile, h, x3):
        t, a = self.t, self.a
        pp = t["pp"]
        sc = t["SC"]
        pool = t["pool"]
        SLAB = pool[:, 0:6144]
        WSM3 = pool[:, 6144:12288].rearrange("p (b j) -> p b j", b=48)
        KTM = pool[:, 12288:13312]
        VTM = pool[:, 13312:14336]
        KFM = pool[:, 14336:15360]
        VFM = pool[:, 15360:16384]
        SCR = [pool[:, 16384 + i * 1024:16384 + (i + 1) * 1024] for i in range(4)]
        DW3 = pool[:, 20480:32768].bitcast(F32).rearrange(
            "p (b j) -> p b j", b=48)
        kvqs = sc[:, 0:3072].rearrange("p (c e) -> p c e", c=8)
        l2v = sc[:, 5120:5152]
        if self.ab_gate is not None:
            self.wait("gpsimd", "t_prog", self.ab_gate)
        ldv = self.dma("gpsimd", SLAB, a["wkvq"][h], "ab_ld")
        self.wait("tensor", "ab_ld", ldv)
        for grp in range(2):
            if grp == 1:
                self.wait("tensor", "s_prog")
            for tc4 in range(4):
                tc = grp * 4 + tc4
                ps = pp[:, tc4 * 512:tc4 * 512 + 384]
                for k in range(NCH):
                    self.op("tensor", lambda E, ps=ps, k=k, tc=tc: E.matmul(
                        ps, x3[:, k, tc * 128:(tc + 1) * 128],
                        SLAB[:, k * 384:(k + 1) * 384],
                        start=(k == 0), stop=(k == NCH - 1)),
                        sem="t_prog" if (k == NCH - 1 and tc4 == 3) else None)
            self.wait("scalar", "t_prog")
            self.wait("scalar", "v_prog", self.c.val("v_prog"))
            self.op("scalar", lambda E, grp=grp: E.activation(
                out=kvqs[:, grp * 4:(grp + 1) * 4, :],
                in_=pp[:, 0:2048].rearrange("p (c e) -> p c e", c=4)
                    [:, :, 0:384], func=AF.Silu), sem="s_prog")
        self.ab_gate = self.c.val("t_prog")
        self.wait("vector", "s_prog")
        self.wait("scalar", "v_prog", self.c.val("v_prog"))
        self.op("scalar", lambda E: E.activation(
            out=sc[:, 3072:4096], in_=kvqs[:, :, 0:128], func=AF.Square))
        self.op("scalar", lambda E: E.activation(
            out=sc[:, 4096:5120], in_=kvqs[:, :, 256:384], func=AF.Square),
            sem="s_prog")
        self.wait("vector", "s_prog")
        self.op("vector", lambda E: E.tensor_reduce(
            out=l2v[:, 0:8], in_=sc[:, 3072:4096].rearrange(
                "p (c e) -> p c e", c=8), axis=AX.X, op=Op.add))
        self.op("vector", lambda E: E.tensor_reduce(
            out=l2v[:, 8:16], in_=sc[:, 4096:5120].rearrange(
                "p (c e) -> p c e", c=8), axis=AX.X, op=Op.add), sem="v_prog")
        self.wait("scalar", "v_prog")
        self.op("scalar", lambda E: E.activation(
            out=l2v[:, 16:32], in_=l2v[:, 0:16], func=AF.Sqrt),
            sem="s_prog")
        self.wait("vector", "s_prog")
        self.op("vector", lambda E: E.tensor_scalar(
            out=l2v[:, 16:32], in0=l2v[:, 16:32],
            scalar1=self.K("eps_l2"), scalar2=None, op0=Op.add), drain=True)
        self.op("vector", lambda E: E.reciprocal(
            out=l2v[:, 16:32], in_=l2v[:, 16:32]), drain=True)
        self.op("vector", lambda E: E.tensor_tensor(
            out=KTM.rearrange("p (c e) -> p c e", c=8), in0=kvqs[:, :, 0:128],
            in1=l2v[:, 16:24].rearrange("p (c o) -> p c o", o=1)
                .broadcast_to((128, 8, 128)), op=Op.mult))
        self.op("vector", lambda E: E.tensor_copy(
            VTM.rearrange("p (c e) -> p c e", c=8), kvqs[:, :, 128:256]))
        self.op("vector", lambda E: E.tensor_tensor(
            out=sc[:, 3072:4096].rearrange("p (c e) -> p c e", c=8),
            in0=kvqs[:, :, 256:384],
            in1=l2v[:, 24:32].rearrange("p (c o) -> p c o", o=1)
                .broadcast_to((128, 8, 128)), op=Op.mult), sem="v_prog")
        self.wait("gpsimd", "v_prog")
        qsv = self.dma("gpsimd", a["qstore"][tile * H + h], sc[:, 3072:4096],
                       "q_st")
        self.wait("tensor", "v_prog")
        self.wait("tensor", "s_prog")
        for tc in range(8):
            self.op("tensor", lambda E, tc=tc: E.transpose(
                t["ppb"][:, tc * 128:(tc + 1) * 128],
                KTM[:, tc * 128:(tc + 1) * 128], t["ident"][:, :]))
            self.op("tensor", lambda E, tc=tc: E.transpose(
                t["ppb"][:, 1024 + tc * 128:1024 + (tc + 1) * 128],
                VTM[:, tc * 128:(tc + 1) * 128], t["ident"][:, :]),
                sem="t_prog" if tc == 7 else None)
        self.wait("scalar", "t_prog")
        self.op("scalar", lambda E: E.activation(
            out=pool[:, 14336:16384], in_=t["ppb"][:, 0:2048], func=AF.Copy),
            sem="s_prog")
        self.wait("tensor", "s_prog")
        for grp in range(2):
            for tc4 in range(4):
                tc = grp * 4 + tc4
                ps = pp[:, tc4 * 512:tc4 * 512 + 512]
                self.op("tensor", lambda E, ps=ps, tc=tc, h=h: E.matmul(
                    ps[:, 0:128], KFM[:, tc * 128:(tc + 1) * 128],
                    WSM3[:, h, :], start=True, stop=True))
                self.op("tensor", lambda E, ps=ps, tc=tc, h=h: E.matmul(
                    ps[:, 128:256], KFM[:, tc * 128:(tc + 1) * 128],
                    WSM3[:, 16 + h, :], start=True, stop=True))
                self.op("tensor", lambda E, ps=ps, tc=tc, h=h: E.matmul(
                    ps[:, 256:384], VFM[:, tc * 128:(tc + 1) * 128],
                    WSM3[:, 32 + h, :], start=True, stop=True),
                    sem="t_prog" if tc4 == 3 else None)
            self.wait("vector", "t_prog")
            pv = pp[:, 0:2048].rearrange("p (c e) -> p c e", c=4)
            for j, base in ((0, 0), (1, 1024), (2, 2048)):
                self.op("vector", lambda E, j=j, base=base, grp=grp: E.tensor_copy(
                    sc[:, base:base + 1024].rearrange("p (c e) -> p c e", c=8)
                        [:, grp * 4:(grp + 1) * 4, :],
                    pv[:, :, j * 128:(j + 1) * 128]),
                    sem="v_prog" if j == 2 else None)
            self.wait("tensor", "v_prog", self.c.val("v_prog"))
        A, Bq, C = sc[:, 0:1024], sc[:, 1024:2048], sc[:, 2048:3072]
        Dq, Eq = sc[:, 3072:4096], sc[:, 4096:5120]
        self.wait("scalar", "v_prog", self.c.val("v_prog"))
        self.wait("scalar", "q_st", qsv)
        self.op("scalar", lambda E: E.activation(out=Dq, in_=A, func=AF.Silu),
                sem="s_prog")
        self.wait("vector", "s_prog")
        self.op("vector", lambda E: E.tensor_mul(Eq, Dq, Bq))
        self.op("vector", lambda E: E.tensor_tensor(
            out=SCR[1].rearrange("p e -> p e"), in0=C, in1=Dq, op=Op.mult))
        self.op("vector", lambda E: E.tensor_mul(C, C, Bq), sem="v_prog")
        self.wait("scalar", "v_prog")
        self.op("scalar", lambda E: E.activation(out=Bq, in_=A, func=AF.Sigmoid),
                sem="s_prog")
        self.wait("vector", "s_prog")
        self.op("vector", lambda E: E.tensor_mul(A, Dq, Bq))
        self.op("vector", lambda E: E.tensor_sub(A, Dq, A))
        self.op("vector", lambda E: E.tensor_add(A, A, Bq))
        self.op("vector", lambda E: E.tensor_tensor(
            out=SCR[0].rearrange("p e -> p e"), in0=C, in1=A, op=Op.mult),
            sem="v_prog")
        lr3 = sc[:, 5248:5632].rearrange("p (c e) -> p c e", c=8)
        self.op("vector", lambda E, h=h: E.tensor_tensor(
            out=SCR[2].rearrange("p (c e) -> p c e", c=8),
            in0=Eq.rearrange("p (c e) -> p c e", c=8),
            in1=lr3[:, :, 3 * h + 1:3 * h + 2].broadcast_to((128, 8, 128)),
            op=Op.mult), sem="v_prog")
        self.wait("tensor", "v_prog")
        self.dw_mm(16 + h, VTM, SCR[2])
        self.op("vector", lambda E, h=h: E.tensor_tensor(
            out=SCR[2].rearrange("p (c e) -> p c e", c=8),
            in0=KTM.rearrange("p (c e) -> p c e", c=8),
            in1=lr3[:, :, 3 * h:3 * h + 1].broadcast_to((128, 8, 128)),
            op=Op.mult), sem="v_prog")
        self.wait("tensor", "v_prog")
        self.dw_mm(h, SCR[0], SCR[2])
        self.op("vector", lambda E, h=h: E.tensor_tensor(
            out=SCR[2].rearrange("p (c e) -> p c e", c=8),
            in0=KTM.rearrange("p (c e) -> p c e", c=8),
            in1=lr3[:, :, 3 * h + 2:3 * h + 3].broadcast_to((128, 8, 128)),
            op=Op.mult), sem="v_prog")
        self.wait("tensor", "v_prog")
        self.dw_mm(32 + h, SCR[1], SCR[2])

    def dw_mm(self, block, lhs_tm, rhs_tm):
        t = self.t
        pp = t["pp"]
        DW3 = t["pool"][:, 20480:32768].bitcast(F32).rearrange(
            "p (b j) -> p b j", b=48)
        bk = self.dw_bank
        self.dw_bank ^= 1
        gate = self.dw_gate[bk]
        if gate is not None:
            self.wait("tensor", "v_prog", gate)
        ps = pp[:, 2048 + bk * 512:2048 + bk * 512 + 128]
        for tc in range(8):
            self.op("tensor", lambda E, ps=ps, tc=tc: E.matmul(
                ps, lhs_tm[:, tc * 128:(tc + 1) * 128],
                rhs_tm[:, tc * 128:(tc + 1) * 128],
                start=(tc == 0), stop=(tc == 7)),
                sem="t_prog" if tc == 7 else None)
        self.wait("vector", "t_prog")
        self.op("vector", lambda E, block=block, ps=ps: E.tensor_tensor(
            out=DW3[:, block, :], in0=DW3[:, block, :], in1=ps, op=Op.add),
            sem="v_prog")
        self.dw_gate[bk] = self.c.val("v_prog")

    def transpose48(self, src3, dst3):
        t = self.t
        for grp in range(3):
            self.wait("tensor", "v_prog", self.c.val("v_prog"))
            for b16 in range(16):
                b = grp * 16 + b16
                self.op("tensor", lambda E, b=b, b16=b16: E.transpose(
                    t["ppb"][:, b16 * 128:(b16 + 1) * 128], src3[:, b, :],
                    t["ident"][:, :]), sem="t_prog" if b16 == 15 else None)
            self.wait("vector", "t_prog")
            self.op("vector", lambda E, grp=grp: E.tensor_copy(
                dst3[:, grp * 16:(grp + 1) * 16, :],
                t["ppb"][:, 0:2048].rearrange("p (c e) -> p c e", c=16)),
                sem="v_prog")

    def mm48(self, lhs3, rhs3, out3, post=None):
        t = self.t
        pp = t["pp"]
        for grp in range(12):
            self.wait("tensor", "v_prog", self.c.val("v_prog"))
            for b4 in range(4):
                b = grp * 4 + b4
                self.op("tensor", lambda E, b=b, b4=b4: E.matmul(
                    pp[:, b4 * 512:b4 * 512 + 128], lhs3[:, b, :],
                    rhs3[:, b, :], start=True, stop=True),
                    sem="t_prog" if b4 == 3 else None)
            self.wait("vector", "t_prog")
            pv = pp[:, 0:2048].rearrange("p (c e) -> p c e", c=4)[:, :, 0:128]
            if post is None:
                self.op("vector", lambda E, grp=grp: E.tensor_copy(
                    out3[:, grp * 4:(grp + 1) * 4, :], pv), sem="v_prog")
            else:
                post(grp, pv)

    def ns_renorm(self):
        t, a = self.t, self.a
        pp = t["pp"]
        sc = t["SC"]
        pool = t["pool"]
        XB3 = pool[:, 0:6144].rearrange("p (b j) -> p b j", b=48)
        XTB3 = pool[:, 6144:12288].rearrange("p (b j) -> p b j", b=48)
        AB3 = pool[:, 12288:18432].rearrange("p (b j) -> p b j", b=48)
        DWF = pool[:, 20480:32768].bitcast(F32)
        DW3 = DWF.rearrange("p (b j) -> p b j", b=48)
        ns48 = t["xln"][:, 0:320].bitcast(F32)
        self.phase_barrier()
        ldv = self.dma("gpsimd", sc[:, :], a["w012"], "dma_g")
        self.op("scalar", lambda E: E.activation(
            out=pool[:, 0:12288].bitcast(F32), in_=DWF, func=AF.Square),
            sem="s_prog")
        self.wait("vector", "s_prog")
        self.op("vector", lambda E: E.tensor_reduce(
            out=ns48[:, 0:48],
            in_=pool[:, 0:12288].bitcast(F32).rearrange("p (b j) -> p b j", b=48),
            axis=AX.X, op=Op.add), sem="v_prog")
        self.row48_recip_bcast(ns48[:, 0:48], "eps_ns", pp, 512)
        self.op("vector", lambda E: E.tensor_tensor(
            out=XB3, in0=DW3,
            in1=pp[:, 512:560].rearrange("p (b o) -> p b o", o=1)
                .broadcast_to((128, 48, 128)), op=Op.mult), sem="v_prog")
        self.transpose48(XB3, XTB3)
        for it, (ca, cb, cc_) in enumerate(NS_COEFS):
            self.mm48(XTB3, XTB3, AB3)
            for grp in range(12):
                self.wait("tensor", "v_prog", self.c.val("v_prog"))
                for b4 in range(4):
                    b = grp * 4 + b4
                    self.op("tensor", lambda E, b=b, b4=b4: E.matmul(
                        pp[:, b4 * 512:b4 * 512 + 128], AB3[:, b, :],
                        AB3[:, b, :], start=True, stop=True),
                        sem="t_prog" if b4 == 3 else None)
                self.wait("vector", "t_prog")
                pv = pp[:, 0:2048].rearrange("p (c e) -> p c e", c=4)[:, :, 0:128]
                tmp3 = pool[:, 18944:19456].rearrange("p (c e) -> p c e", c=4)
                self.op("vector", lambda E, pv=pv, cc_=cc_, tmp3=tmp3: E.tensor_scalar(
                    out=tmp3, in0=pv, scalar1=cc_, scalar2=None, op0=Op.mult))
                self.op("vector", lambda E, tmp3=tmp3, cb=cb, grp=grp: E.scalar_tensor_tensor(
                    out=AB3[:, grp * 4:(grp + 1) * 4, :],
                    in0=AB3[:, grp * 4:(grp + 1) * 4, :], scalar=cb, in1=tmp3,
                    op0=Op.mult, op1=Op.add), sem="v_prog")
            for grp in range(12):
                self.wait("tensor", "v_prog", self.c.val("v_prog"))
                for b4 in range(4):
                    b = grp * 4 + b4
                    self.op("tensor", lambda E, b=b, b4=b4: E.matmul(
                        pp[:, b4 * 512:b4 * 512 + 128], AB3[:, b, :],
                        XB3[:, b, :], start=True, stop=True))
                    self.op("tensor", lambda E, b=b, b4=b4: E.matmul(
                        pp[:, b4 * 512 + 128:b4 * 512 + 256], XB3[:, b, :],
                        AB3[:, b, :], start=True, stop=True),
                        sem="t_prog" if b4 == 3 else None)
                self.wait("vector", "t_prog")
                pv = pp[:, 0:2048].rearrange("p (c e) -> p c e", c=4)
                self.op("vector", lambda E, pv=pv, ca=ca, grp=grp: E.scalar_tensor_tensor(
                    out=XB3[:, grp * 4:(grp + 1) * 4, :],
                    in0=XB3[:, grp * 4:(grp + 1) * 4, :], scalar=ca,
                    in1=pv[:, :, 0:128], op0=Op.mult, op1=Op.add))
                self.op("vector", lambda E, pv=pv, ca=ca, grp=grp: E.scalar_tensor_tensor(
                    out=XTB3[:, grp * 4:(grp + 1) * 4, :],
                    in0=XTB3[:, grp * 4:(grp + 1) * 4, :], scalar=ca,
                    in1=pv[:, :, 128:256], op0=Op.mult, op1=Op.add),
                    sem="v_prog")
        self.wait("vector", "dma_g", ldv)
        self.op("vector", lambda E: E.tensor_tensor(
            out=DWF, in0=sc[:, :], in1=pool[:, 0:6144], op=Op.add),
            sem="v_prog")
        FS = pool[:, 12288:18432].bitcast(F32)
        FS3 = FS.rearrange("p (b j) -> p b j", b=24)
        self.wait("scalar", "v_prog")
        for hf in range(2):
            self.op("scalar", lambda E, hf=hf: E.activation(
                out=FS, in_=sc[:, hf * 3072:(hf + 1) * 3072], func=AF.Square),
                sem="s_prog")
            self.wait("vector", "s_prog")
            self.op("vector", lambda E, hf=hf: E.tensor_reduce(
                out=ns48[:, 48 + hf * 24:48 + (hf + 1) * 24], in_=FS3,
                axis=AX.X, op=Op.add), sem="v_prog")
            self.wait("scalar", "v_prog")
        for hf in range(2):
            self.op("scalar", lambda E, hf=hf: E.activation(
                out=FS, in_=DWF[:, hf * 3072:(hf + 1) * 3072], func=AF.Square),
                sem="s_prog")
            self.wait("vector", "s_prog")
            self.op("vector", lambda E, hf=hf: E.tensor_reduce(
                out=ns48[:, hf * 24:(hf + 1) * 24], in_=FS3,
                axis=AX.X, op=Op.add), sem="v_prog")
            self.wait("scalar", "v_prog")
        self.op("scalar", lambda E: E.activation(
            out=ns48[:, 96:144], in_=ns48[:, 48:96], func=AF.Sqrt), drain=True)
        self.op("scalar", lambda E: E.activation(
            out=ns48[:, 0:48], in_=ns48[:, 0:48], func=AF.Sqrt),
            sem="s_prog")
        self.wait("vector", "s_prog")
        self.op("vector", lambda E: E.tensor_scalar(
            out=ns48[:, 0:48], in0=ns48[:, 0:48],
            scalar1=self.K("eps_l2"), scalar2=None, op0=Op.add), drain=True)
        self.op("vector", lambda E: E.reciprocal(
            out=ns48[:, 0:48], in_=ns48[:, 0:48]), drain=True)
        self.op("vector", lambda E: E.tensor_mul(
            ns48[:, 96:144], ns48[:, 96:144], ns48[:, 0:48]), drain=True)
        self.op("vector", lambda E: E.tensor_tensor(
            out=XB3, in0=DW3,
            in1=ns48[:, 96:144].rearrange("p (b o) -> p b o", o=1)
                .broadcast_to((128, 48, 128)), op=Op.mult), sem="v_prog")
        self.transpose48(XB3, XTB3)

    def row48_recip_bcast(self, src48, epsname, pp, col0):
        t = self.t
        pool = self.t["pool"]
        stg = pool[:, 18432:18480]
        self.wait("vector", "v_prog", self.c.val("v_prog"))
        self.op("vector", lambda E, stg=stg, src48=src48: E.tensor_copy(
            stg, src48), sem="v_prog")
        self.wait("tensor", "v_prog")
        self.op("tensor", lambda E, stg=stg: E.matmul(
            pp[0:1, 0:48], t["ones_bf"][:, 0:1], stg,
            start=True, stop=True), sem="t_prog")
        self.wait("scalar", "t_prog")
        row = t["redtmp"][0:1, 96:144]
        self.op("scalar", lambda E, row=row: E.activation(
            out=row, in_=pp[0:1, 0:48], func=AF.Sqrt), drain=True)
        self.op("scalar", lambda E, row=row, epsname=epsname: E.activation(
            out=row, in_=row, func=AF.Ln, bias=self.K1(epsname)), drain=True)
        self.op("scalar", lambda E, row=row: E.activation(
            out=row, in_=row, func=AF.Exp, scale=self.K1("neg_one")),
            sem="s_prog")
        self.wait("vector", "s_prog")
        self.op("vector", lambda E, row=row: E.tensor_copy(
            t["stats"][0:1, 0:48], row), sem="v_prog")
        self.wait("tensor", "v_prog")
        self.op("tensor", lambda E, col0=col0: E.matmul(
            pp[:, col0:col0 + 48], t["ones_row_bf"][0:1, :],
            t["stats"][0:1, 0:48], start=True, stop=True), sem="t_prog")
        self.wait("vector", "t_prog")

    def ab_head_ph2(self, tile, h):
        t, a = self.t, self.a
        pp = t["pp"]
        sc = t["SC"]
        pool = t["pool"]
        WFIN3 = pool[:, 0:6144].rearrange("p (b j) -> p b j", b=48)
        WFINT3 = pool[:, 6144:12288].rearrange("p (b j) -> p b j", b=48)
        Q = pool[:, 12288:13312]
        QFM = pool[:, 13312:14336]
        OG = pool[:, 14336:15360]
        OGFM = pool[:, 15360:16384]
        OALL = pool[:, 16384:32768]
        self.wait("gpsimd", "t_prog")
        self.wait("gpsimd", "v_prog")
        ldv = self.dma("gpsimd", Q, a["qstore"][tile * H + h], "ab_ld")
        self.wait("tensor", "ab_ld", ldv)
        self.wait("tensor", "v_prog")
        for tc in range(8):
            self.op("tensor", lambda E, tc=tc: E.transpose(
                t["ppb"][:, tc * 128:(tc + 1) * 128],
                Q[:, tc * 128:(tc + 1) * 128], t["ident"][:, :]),
                sem="t_prog" if tc == 7 else None)
        self.wait("vector", "t_prog")
        self.op("vector", lambda E: E.tensor_copy(QFM, t["ppb"][:, 0:1024]),
                sem="v_prog")
        self.wait("tensor", "v_prog", self.c.val("v_prog"))
        for grp in range(2):
            for tc4 in range(4):
                tc = grp * 4 + tc4
                ps = pp[:, tc4 * 512:tc4 * 512 + 512]
                self.op("tensor", lambda E, ps=ps, tc=tc, h=h: E.matmul(
                    ps[:, 0:128], QFM[:, tc * 128:(tc + 1) * 128],
                    WFINT3[:, h, :], start=True, stop=True))
                self.op("tensor", lambda E, ps=ps, tc=tc, h=h: E.matmul(
                    ps[:, 128:256], QFM[:, tc * 128:(tc + 1) * 128],
                    WFINT3[:, 32 + h, :], start=True, stop=True),
                    sem="t_prog" if tc4 == 3 else None)
            self.wait("scalar", "t_prog")
            self.wait("scalar", "v_prog", self.c.val("v_prog"))
            pv = pp[:, 0:2048].rearrange("p (c e) -> p c e", c=4)
            self.op("scalar", lambda E, pv=pv, grp=grp: E.activation(
                out=sc[:, 0:1024].rearrange("p (c e) -> p c e", c=8)
                    [:, grp * 4:(grp + 1) * 4, :],
                in_=pv[:, :, 0:128], func=AF.Silu), sem="s_prog")
            self.wait("vector", "s_prog")
            self.op("vector", lambda E, pv=pv, grp=grp: E.tensor_copy(
                sc[:, 1024:2048].rearrange("p (c e) -> p c e", c=8)
                    [:, grp * 4:(grp + 1) * 4, :], pv[:, :, 128:256]),
                sem="v_prog")
            self.wait("tensor", "v_prog", self.c.val("v_prog"))
        self.op("vector", lambda E: E.tensor_tensor(
            out=OG, in0=sc[:, 0:1024], in1=sc[:, 1024:2048], op=Op.mult),
            sem="v_prog")
        self.wait("tensor", "v_prog")
        for tc in range(8):
            self.op("tensor", lambda E, tc=tc: E.transpose(
                t["ppb"][:, tc * 128:(tc + 1) * 128],
                OG[:, tc * 128:(tc + 1) * 128], t["ident"][:, :]),
                sem="t_prog" if tc == 7 else None)
        self.wait("vector", "t_prog")
        self.op("vector", lambda E: E.tensor_copy(OGFM, t["ppb"][:, 0:1024]),
                sem="v_prog")
        self.wait("tensor", "v_prog", self.c.val("v_prog"))
        for n2 in range(2):
            self.op("tensor", lambda E, n2=n2, h=h: E.matmul(
                pp[:, n2 * 512:(n2 + 1) * 512], WFINT3[:, 16 + h, :],
                OGFM[:, n2 * 512:(n2 + 1) * 512], start=True, stop=True),
                sem="t_prog" if n2 == 1 else None)
        self.wait("scalar", "t_prog")
        self.wait("scalar", "v_prog", self.c.val("v_prog"))
        self.op("scalar", lambda E: E.activation(
            out=pool[:, 12288:13312], in_=pp[:, 0:1024], func=AF.Square),
            sem="s_prog")
        self.wait("tensor", "s_prog")
        self.op("tensor", lambda E: E.matmul(
            pp[0:1, 1024:1536], t["ones_bf"][:, 0:1], pool[:, 12288:12800],
            start=True, stop=True))
        self.op("tensor", lambda E: E.matmul(
            pp[0:1, 1536:2048], t["ones_bf"][:, 0:1], pool[:, 12800:13312],
            start=True, stop=True), sem="t_prog")
        self.wait("scalar", "t_prog")
        row = sc[0:1, 4096:5120]
        self.op("scalar", lambda E, row=row: E.activation(
            out=row[0:1, 0:512], in_=pp[0:1, 1024:1536], func=AF.Ln,
            scale=self.K1("inv_hd"), bias=self.K1("eps_ln")), drain=True)
        self.op("scalar", lambda E, row=row: E.activation(
            out=row[0:1, 512:1024], in_=pp[0:1, 1536:2048], func=AF.Ln,
            scale=self.K1("inv_hd"), bias=self.K1("eps_ln")), drain=True)
        self.op("scalar", lambda E, row=row: E.activation(
            out=row, in_=row, func=AF.Exp, scale=self.K1("neg_half")),
            sem="s_prog")
        self.wait("vector", "s_prog")
        self.op("vector", lambda E, row=row: E.tensor_copy(
            t["stats"][0:1, 0:1024], row), sem="v_prog")
        self.wait("tensor", "v_prog")
        self.op("tensor", lambda E: E.matmul(
            pp[:, 1024:1536], t["gamma_row_bf"][0:1, :],
            t["stats"][0:1, 0:512], start=True, stop=True))
        self.op("tensor", lambda E: E.matmul(
            pp[:, 1536:2048], t["gamma_row_bf"][0:1, :],
            t["stats"][0:1, 512:1024], start=True, stop=True),
            sem="t_prog")
        self.wait("vector", "t_prog")
        self.op("vector", lambda E: E.tensor_copy(sc[:, 0:1024], pp[:, 0:1024]),
                sem="v_prog")
        self.op("vector", lambda E, h=h: E.tensor_tensor(
            out=OALL[:, h * 1024:(h + 1) * 1024], in0=sc[:, 0:1024],
            in1=pp[:, 1024:2048], op=Op.mult), sem="v_prog")
        self.wait("tensor", "v_prog", self.c.val("v_prog"))

    def ab_oproj(self, tile, xres3):
        t, a = self.t, self.a
        pp = t["pp"]
        pool = t["pool"]
        OALL = pool[:, 16384:32768]
        wps = [pool[:, 12288 + i * 2048:12288 + (i + 1) * 2048] for i in range(2)]
        t0 = tile * 1024
        self.wait("gpsimd", "t_prog")
        self.wait("gpsimd", "v_prog")
        self.pp01_gate[0] = ("v_prog", self.c.val("v_prog"))
        self.pp01_gate[1] = ("v_prog", self.c.val("v_prog"))
        for mo in range(16):
            g = self.c.bump("op_seq", 1) - 1
            buf = wps[g % 2]
            if g - 2 in self.op_gate:
                self.wait("gpsimd", "t_prog", self.op_gate[g - 2])
            ldv = self.dma("gpsimd", buf, a["woproj"][mo], "op_ld")
            self.wait("tensor", "op_ld", ldv)
            for n2 in range(2):
                bk = n2
                gate = self.pp01_gate[bk]
                if gate is not None:
                    self.wait("tensor", gate[0], gate[1])
                ps = pp[:, bk * 512:bk * 512 + 512]
                for k in range(NCH):
                    self.op("tensor", lambda E, ps=ps, k=k, n2=n2, buf=buf: E.matmul(
                        ps, buf[:, k * 128:(k + 1) * 128],
                        OALL[:, k * 1024 + n2 * 512:k * 1024 + n2 * 512 + 512],
                        start=(k == 0), stop=(k == NCH - 1)),
                        sem="t_prog" if k == NCH - 1 else None)
                self.wait("vector", "t_prog")
                self.op("vector", lambda E, mo=mo, t0=t0, n2=n2, ps=ps: E.tensor_tensor(
                    out=xres3[:, mo, t0 + n2 * 512:t0 + n2 * 512 + 512],
                    in0=xres3[:, mo, t0 + n2 * 512:t0 + n2 * 512 + 512],
                    in1=ps, op=Op.add), sem="v_prog")
                self.pp01_gate[bk] = ("v_prog", self.c.val("v_prog"))
            self.op_gate[g] = self.c.val("t_prog")

    def ph_store(self):
        t, a = self.t, self.a
        self.wait("gpsimd", "v_prog")
        self.dma("gpsimd", a["out"], t["xres"][:, :], "dma_st")
        self.wait("sync", "dma_st")


def build_nc():
    nc = bass.Bass("TRN2", target_bir_lowering=False, debug=False,
                   detect_race_conditions=False)
    a = {}
    a["x"] = nc.dram_tensor("x", [128, NCH, T], F32, kind="ExternalInput").ap()
    a["prev"] = nc.dram_tensor("prev", [128, 16], F32, kind="ExternalInput").ap()
    a["consts"] = nc.dram_tensor("consts", [128, len(CONSTS)], F32,
                                 kind="ExternalInput").ap()
    a["flags"] = nc.dram_tensor("flags", [128, 2], F32, kind="ExternalInput").ap()
    a["offs8"] = nc.dram_tensor("offs8", [128, 8], F32, kind="ExternalInput").ap()
    a["gamma"] = nc.dram_tensor("gamma", [128, 1], F32, kind="ExternalInput").ap()
    a["gamma_row"] = nc.dram_tensor("gamma_row", [1, 128], F32,
                                    kind="ExternalInput").ap()
    a["ones"] = nc.dram_tensor("ones", [128, 2], F32, kind="ExternalInput").ap()
    a["ones_row"] = nc.dram_tensor("ones_row", [1, 128], F32,
                                   kind="ExternalInput").ap()
    wact = nc.dram_tensor("wact", [16, 128, 32 * 128], F32, kind="ExternalInput")
    wrec = nc.dram_tensor("wrec", [16, 128, 32 * 128], F32, kind="ExternalInput")
    wval = nc.dram_tensor("wval", [16, 128, 16 * 128], F32, kind="ExternalInput")
    a["wact"] = [wact.ap()[m] for m in range(16)]
    a["wrec"] = [wrec.ap()[m] for m in range(16)]
    a["wval"] = [wval.ap()[m] for m in range(16)]
    wfc = nc.dram_tensor("wfc", [40, 128, 2048], F32, kind="ExternalInput")
    wproj = nc.dram_tensor("wproj", [16, 128, 5120], F32, kind="ExternalInput")
    wfc_t = nc.dram_tensor("wfc_t", [40, 128, 2048], BF16)
    wproj_t = nc.dram_tensor("wproj_t", [16, 128, 5120], BF16)
    a["wfc_t_sl"] = [wfc_t.ap()[m] for m in range(40)]
    a["wproj_t_sl"] = [wproj_t.ap()[m] for m in range(16)]
    tz0 = []
    for m in range(40):
        for o in range(0, 2048, 1024):
            tz0.append((wfc.ap()[m][:, o:o + 1024], wfc_t.ap()[m][:, o:o + 1024], 1024))
    tz1 = []
    for m in range(16):
        for o in range(0, 5120, 1024):
            tz1.append((wproj.ap()[m][:, o:o + 1024], wproj_t.ap()[m][:, o:o + 1024], 1024))
    a["tz_src"] = [tz0, tz1]
    a["tz_nel"] = [40 * 128 * 2048, 16 * 128 * 5120]
    wkvq = nc.dram_tensor("wkvq", [16, 128, 6144], F32, kind="ExternalInput")
    a["wkvq"] = [wkvq.ap()[hh] for hh in range(16)]
    a["wsmall"] = nc.dram_tensor("wsmall", [128, 6144], F32,
                                 kind="ExternalInput").ap()
    a["w012"] = nc.dram_tensor("w012", [128, 6144], F32,
                               kind="ExternalInput").ap()
    woproj = nc.dram_tensor("woproj", [16, 128, 2048], F32,
                            kind="ExternalInput")
    a["woproj"] = [woproj.ap()[mm] for mm in range(16)]
    a["lrproj"] = nc.dram_tensor("lrproj", [128, 768], F32,
                                 kind="ExternalInput").ap()
    a["ident"] = nc.dram_tensor("ident", [128, 128], F32,
                                kind="ExternalInput").ap()
    qstore = nc.dram_tensor("qstore", [32, 128, 1024], BF16)
    a["qstore"] = [qstore.ap()[i] for i in range(32)]
    a["cc_ar_in"] = nc.dram_tensor("cc_ar_in", [128, 6144], F32).ap()
    a["cc_ar_out"] = nc.dram_tensor("cc_ar_out", [128, 6144], F32).ap()
    a["out"] = nc.dram_tensor("out", [128, NCH * T], F32,
                              kind="ExternalOutput").ap()
    a["cc_ag_in"] = nc.dram_tensor("cc_ag_in", [128, 24], F32).ap()
    ago = nc.dram_tensor("cc_ag_out", [256, 24], F32)
    a["cc_ag_out"] = ago.ap()
    a["cc_ag_out_r0"] = ago.ap()[0:128, :]

    stack = contextlib.ExitStack()
    t = {}

    def sb(name, shape, dt=F32):
        t[name] = stack.enter_context(nc.sbuf_tensor("sb_" + name, shape, dt))
        return t[name]

    sb("consts", [128, len(CONSTS)])
    sb("flags", [128, 2])
    sb("offs8", [128, 8])
    sb("gamma", [128, 1])
    sb("prev", [128, 16])
    sb("prev_ln", [128, 32], BF16)
    sb("prev_bf", [128, 16], BF16)
    sb("prevsq_bf", [128, 16], BF16)
    sb("prevstat_bf", [1, 2], BF16)
    sb("redtmp", [1, 160])
    sb("bc4", [128, 4])
    sb("ones_bf", [128, 2], BF16)
    sb("ones_f32", [128, 2])
    sb("ones_row", [1, 128])
    sb("ones_row_bf", [1, 128], BF16)
    sb("gamma_row_bf", [1, 128], BF16)
    sb("xres", [128, NCH * T], BF16)
    sb("pool", [128, NCH * T], BF16)
    sb("xln", [128, NCH * XSTR], BF16)
    sb("xln_last", [128, 32], BF16)
    sb("SC", [128, 6144])
    sb("stats", [1, 2 * T], BF16)
    sb("summ", [128, 24])
    sb("peers", [128, 24])
    sb("carry", [128, 24])
    t["wsl"] = [sb("wsl0", [128, 2048], BF16), sb("wsl1", [128, 2048], BF16),
                sb("wsl2", [128, 2048], BF16)]
    sb("ident", [128, 128], BF16)
    sb("tzsl", [128, 192])
    sb("bc6", [128, 4])

    pstack = contextlib.ExitStack()
    t["pp"] = pstack.enter_context(nc.psum_tensor("pp", [128, 3072], F32))
    t["ppb"] = pstack.enter_context(nc.psum_tensor("ppb", [128, 2048], BF16))

    sems = {}
    semstack = contextlib.ExitStack()
    for s in ["boot", "dma_s", "dma_g", "dma_st", "v_prog", "s_prog", "t_prog",
              "t_bc", "g_prog", "cc", "lnx2", "w_ld", "w_fr", "wr_ld", "wr_fr",
              "wv_ld", "wv_fr", "tz_ld", "tz_st", "tz_fr", "f_ld", "f_fr",
              "fp_ld", "fp_fr", "ab_ld", "q_st", "op_ld", "tz_s", "tz_v"]:
        sems[s] = semstack.enter_context(nc.semaphore(s))

    kb = KB(nc, sems, t, a)
    with nc.Block() as block:
        @block.sync
        def _(E):
            kb.emit("sync", E)

        @block.vector
        def _(E):
            kb.emit("vector", E)

        @block.scalar
        def _(E):
            kb.emit("scalar", E)

        @block.tensor
        def _(E):
            kb.emit("tensor", E)

        @block.gpsimd
        def _(E):
            kb.emit("gpsimd", E)

    stack.close()
    pstack.close()
    semstack.close()
    return nc


def make_in_maps(inputs):
    x = np.asarray(inputs["L0_attn_a_x"], dtype=np.float32)
    B_, S, C = x.shape
    Th = S // 2

    consts = np.zeros((128, len(CONSTS)), dtype=np.float32)
    for k, v in CONSTS.items():
        consts[:, CIDX[k]] = v
    ones = np.zeros((128, 2), dtype=np.float32)
    ones[:, 0] = 1.0
    ones_row = np.ones((1, 128), dtype=np.float32)

    W_act = np.asarray(inputs["L0_attn_a_W_act"], dtype=np.float32)
    perm = np.concatenate([np.arange(0, C, 2), np.arange(1, C, 2)])
    wact_sl = np.ascontiguousarray(
        np.ascontiguousarray(W_act[perm].T).reshape(32, 128, 16, 128)
        .transpose(2, 1, 0, 3)).reshape(16, 128, 32 * 128)
    W_rec = np.asarray(inputs["L0_attn_a_W_rec"], dtype=np.float32)
    wrec_sl = np.ascontiguousarray(
        np.ascontiguousarray(W_rec.T).reshape(32, 128, 16, 128)
        .transpose(2, 1, 0, 3)).reshape(16, 128, 32 * 128)
    W_val = np.asarray(inputs["L0_attn_a_W_val"], dtype=np.float32)
    wval_sl = np.ascontiguousarray(
        np.ascontiguousarray(W_val[:, perm].T).reshape(16, 128, 16, 128)
        .transpose(2, 1, 0, 3)).reshape(16, 128, 16 * 128)

    W_fc = np.asarray(inputs["L0_ffn_w_fc"], dtype=np.float32)
    wfc_sl = np.ascontiguousarray(
        np.ascontiguousarray(W_fc.T).reshape(16, 128, 40, 128)
        .transpose(2, 1, 0, 3)).reshape(40, 128, 2048)
    W_proj = np.asarray(inputs["L0_ffn_w_proj"], dtype=np.float32)
    wproj_sl = np.ascontiguousarray(
        np.ascontiguousarray(W_proj.T).reshape(40, 128, 16, 128)
        .transpose(2, 1, 0, 3)).reshape(16, 128, 5120)
    gamma = np.zeros((128, 1), dtype=np.float32)
    gamma[:, 0] = np.asarray(inputs["L0_attn_b_o_norm_w"], dtype=np.float32)
    gamma_row = np.ascontiguousarray(
        np.asarray(inputs["L0_attn_b_o_norm_w"], dtype=np.float32)
        .reshape(1, 128))

    tq = np.asarray(inputs["L0_attn_b_to_qkv_w"], dtype=np.float32)
    kvq_sl = np.empty((16, 128, 6144), np.float32)
    for hh in range(16):
        cat = np.concatenate([tq[2048 + hh * 128:2048 + (hh + 1) * 128],
                              tq[4096 + hh * 128:4096 + (hh + 1) * 128],
                              tq[hh * 128:(hh + 1) * 128]], axis=0)
        kvq_sl[hh] = cat.reshape(384, 16, 128).transpose(2, 1, 0).reshape(128, 6144)
    w0 = np.asarray(inputs["L0_attn_b_w0"], np.float32)
    w1 = np.asarray(inputs["L0_attn_b_w1"], np.float32)
    w2 = np.asarray(inputs["L0_attn_b_w2"], np.float32)
    w012 = np.ascontiguousarray(
        np.concatenate([w0, w1, w2], axis=0).transpose(1, 0, 2)).reshape(128, 6144)
    wsmall = np.ascontiguousarray(
        np.concatenate([w0.transpose(0, 2, 1), w2.transpose(0, 2, 1), w1],
                       axis=0).transpose(1, 0, 2)).reshape(128, 6144)
    op_ = np.asarray(inputs["L0_attn_b_o_proj_w"], np.float32)
    wop = np.ascontiguousarray(
        op_.reshape(16, 128, 16, 128).transpose(0, 3, 2, 1)).reshape(16, 128, 2048)
    lrp = np.asarray(inputs["L0_attn_b_lr_proj_w"], np.float32)
    lrh = np.ascontiguousarray(
        lrp.T.reshape(16, 128, 48).transpose(1, 0, 2)).reshape(128, 768)
    identm = np.eye(128, dtype=np.float32)

    in_maps = []
    for core in range(8):
        b_, half = core // 2, core % 2
        xs = np.ascontiguousarray(x[b_, half * Th:(half + 1) * Th].T)
        x_l = np.ascontiguousarray(xs.reshape(16, 128, Th).transpose(1, 0, 2))
        prev = (np.zeros(C, dtype=np.float32) if half == 0
                else np.ascontiguousarray(x[b_, Th - 1]))
        prev_l = np.ascontiguousarray(prev.reshape(16, 128).T)
        flags = np.zeros((128, 2), dtype=np.float32)
        flags[:, 0] = float(half)
        offs8 = np.full((128, 8), (float(half) - 1.0) * 1e30, dtype=np.float32)
        in_maps.append({
            "x": x_l, "prev": prev_l, "consts": consts, "flags": flags,
            "offs8": offs8, "gamma": gamma, "gamma_row": gamma_row,
            "ones": ones, "ones_row": ones_row,
            "wact": wact_sl, "wrec": wrec_sl, "wval": wval_sl,
            "wfc": wfc_sl, "wproj": wproj_sl,
            "wkvq": kvq_sl, "wsmall": wsmall, "w012": w012,
            "woproj": wop, "lrproj": lrh, "ident": identm,
        })
    return in_maps


def kernel(**inputs):
    x = np.asarray(inputs["L0_attn_a_x"], dtype=np.float32)
    B_, S, C = x.shape
    Th = S // 2
    in_maps = make_in_maps(inputs)
    nc = build_nc()
    res = run_bass_kernel_spmd(nc, in_maps, core_ids=list(range(8)))
    out = np.empty((B_, S, C), dtype=np.float32)
    for core in range(8):
        b_, half = core // 2, core % 2
        o = res.results[core]["out"].reshape(128, 16, Th)
        out[b_, half * Th:(half + 1) * Th] = (
            o.transpose(1, 0, 2).reshape(C, Th).T)
    return out
